# version 03
# baseline (speedup 1.0000x reference)
import numpy as np
import ml_dtypes
from contextlib import ExitStack

import concourse.bass as bass
import concourse.mybir as mybir
from concourse.bass_utils import run_bass_kernel_spmd

F32 = mybir.dt.float32
BF16 = mybir.dt.bfloat16
I32 = mybir.dt.int32
AF = mybir.ActivationFunctionType
OP = mybir.AluOpType
AX = mybir.AxisListType.X
ds = bass.ds

H = 2048
NHEAD = 16
DH = 128
T = 2048
E = 8
II = 1024
R = 16
SCALING = 2.0
NEG = -1e9
EPS = 1e-5
ISQD = float(1.0 / np.sqrt(DH))
GROUPS = [[0, 1, 2, 3], [4, 5, 6, 7]]

PHASES = 7
TAPS = ()


def _diag_status(s, qh):
    if s * 128 + 127 <= qh * 512:
        return "o"
    if s * 128 > qh * 512 + 511:
        return "c"
    return "p"


class Sem:
    def __init__(self, nc, es, name):
        self.h = es.enter_context(nc.semaphore(name))
        self.n = 0


class Rec:

    def __init__(self):
        self.ops = {k: [] for k in ("sy", "pe", "ve", "sc", "gp")}

    def raw(self, eng, fn):
        self.ops[eng].append(fn)

    def wait(self, eng, sem, val):
        if val > 0:
            self.ops[eng].append(lambda e, s=sem, v=val: e.wait_ge(s.h, v))

    def op(self, eng, sem, fn):
        def g(e, fn=fn, sem=sem):
            fn(e).then_inc(sem.h, 1)
        self.ops[eng].append(g)
        sem.n += 1
        return sem.n

    def opn(self, eng, fn):
        self.ops[eng].append(fn)

    def barrier(self, bar, drains):
        for eng, sems in drains.items():
            for sm in sems:
                self.wait(eng, sm, sm.n)
        for eng in ("sy", "pe", "ve", "sc", "gp"):
            def g(e, bar=bar):
                e.nop().then_inc(bar.h, 1)
            self.ops[eng].append(g)
            bar.n += 1
        tgt = bar.n
        for eng in ("sy", "pe", "ve", "sc", "gp"):
            self.wait(eng, bar, tgt)

    def vsync(self, eng, sem):
        def g(e, sem=sem):
            e.nop().then_inc(sem.h, 1)
        self.ops[eng].append(g)
        sem.n += 1
        self.wait(eng, sem, sem.n)

    def dma(self, eng, sem, out, in_, **kw):
        def g(e, out=out, in_=in_, kw=kw, sem=sem):
            e.dma_start(out=out, in_=in_, **kw).then_inc(sem.h, 16)
        self.ops[eng].append(g)
        sem.n += 16
        return sem.n

    def emit(self, block):
        ops = self.ops

        @block.sync
        def _(e):
            for fn in ops["sy"]:
                fn(e)

        @block.tensor
        def _(e):
            for fn in ops["pe"]:
                fn(e)

        @block.vector
        def _(e):
            for fn in ops["ve"]:
                fn(e)

        @block.scalar
        def _(e):
            for fn in ops["sc"]:
                fn(e)

        @block.gpsimd
        def _(e):
            for fn in ops["gp"]:
                fn(e)


def build(taps=(), phases=7):
    nc = bass.Bass("TRN2", target_bir_lowering=False, debug=False, detect_race_conditions=False)
    es = ExitStack()

    dr_x = nc.dram_tensor("x", [T, H], F32, kind="ExternalInput").ap()
    dr_wqkv = nc.dram_tensor("wqkv", [H, 3 * H], F32, kind="ExternalInput").ap()
    dr_wout = nc.dram_tensor("wout", [H, H], F32, kind="ExternalInput").ap()
    dr_gw = nc.dram_tensor("gatew", [E, H], F32, kind="ExternalInput").ap()
    dr_gA = nc.dram_tensor("gateA", [R, H], F32, kind="ExternalInput").ap()
    dr_gB = nc.dram_tensor("gateB", [E, R], F32, kind="ExternalInput").ap()
    dr_gup = nc.dram_tensor("gup", [E, H, 2 * II], F32, kind="ExternalInput").ap()
    dr_down = nc.dram_tensor("down", [E, II, H], F32, kind="ExternalInput").ap()
    dr_gupA = nc.dram_tensor("gupA", [E, R, H], F32, kind="ExternalInput").ap()
    dr_gupB = nc.dram_tensor("gupB", [E, 2 * II, R], F32, kind="ExternalInput").ap()
    dr_downA = nc.dram_tensor("downA", [E, R, II], F32, kind="ExternalInput").ap()
    dr_downB = nc.dram_tensor("downB", [E, H, R], F32, kind="ExternalInput").ap()
    dr_idbf = nc.dram_tensor("c_idbf", [128, 128], BF16, kind="ExternalInput").ap()
    dr_idf = nc.dram_tensor("c_idf", [128, 128], F32, kind="ExternalInput").ap()
    dr_mask = nc.dram_tensor("c_mask", [128, 4 * 512], F32, kind="ExternalInput").ap()
    dr_cfg = nc.dram_tensor("c_cfg", [1, 64], I32, kind="ExternalInput").ap()
    dr_out = nc.dram_tensor("out", [T, H], F32, kind="ExternalOutput").ap()

    dr_ck = nc.dram_tensor("contribK", [H, T], BF16, kind="Internal").ap()
    dr_cv = nc.dram_tensor("contribV", [T, H], BF16, kind="Internal").ap()
    dr_gk = nc.dram_tensor("gatheredK", [4 * H, T], BF16, kind="Internal").ap()
    dr_gv = nc.dram_tensor("gatheredV", [4 * T, H], BF16, kind="Internal").ap()
    dr_h = nc.dram_tensor("hspill", [E, II, T], BF16, kind="Internal").ap()
    dr_wr = nc.dram_tensor("wrout", [E, T], F32, kind="Internal").ap()
    dr_kni = nc.dram_tensor("kn_in", [1, NHEAD], F32, kind="Internal").ap()
    dr_kno = nc.dram_tensor("kn_out", [1, NHEAD], F32, kind="Internal").ap()

    tap_aps = {}
    if taps:
        tshapes = {"xn": ([128, 16 * T], BF16), "qo": ([128, NHEAD * T], BF16),
                   "gk": ([4 * H, T], BF16), "gv": ([4 * T, H], BF16),
                   "h": ([E, II, T], BF16), "w": ([E, T], F32),
                   "negc": ([128, 32], F32), "lnst": ([128, 16], F32)}
        for t in taps:
            sh, dt = tshapes[t]
            tap_aps[t] = nc.dram_tensor("tap_" + t, sh, dt, kind="ExternalOutput").ap()

    m_xn_done = 0
    m_qkv_pe = 0
    m_qkv_ve = 0
    m_wsp = 0
    m_gup_done = (0, 0)
    m_attn_ve = 0

    with es:
        sbt = lambda name, shape, dtype: es.enter_context(nc.sbuf_tensor(name, shape, dtype))
        xn_es = ExitStack()
        xn = xn_es.enter_context(nc.sbuf_tensor("xn", [128, 16 * T], BF16, side="right"))
        QO = sbt("QO", [128, NHEAD * T], BF16)
        idbf = sbt("idbf", [128, 128], BF16)
        idf = sbt("idf", [128, 128], F32)
        cfgs = sbt("cfgs", [1, 64], I32)
        ones_col = sbt("ones_col", [128, 1], F32)
        ones_col_bf = sbt("ones_col_bf", [128, 1], BF16)
        ones_row = sbt("ones_row", [1, 128], F32)
        zero_bf = sbt("zero_bf", [128, 512], BF16)
        eps_t = sbt("eps_t", [128, 1], F32)
        qn2 = sbt("qn2", [1, 32], F32)
        kn2 = sbt("kn2", [1, NHEAD], F32)
        negc_rep = sbt("negc_rep", [128, 32], F32)
        w45_es = ExitStack()
        wT8 = w45_es.enter_context(nc.sbuf_tensor("wT8", [E, T], F32, side="right"))

        S = lambda name: Sem(nc, es, name)
        d_in = S("d_in")
        d_x = [S("d_x0"), S("d_x1")]
        d_w2 = [S("d_w20"), S("d_w21")]
        d_kst = [S("d_kst0"), S("d_kst1")]
        d_vtt = [S("d_vtt0"), S("d_vtt1")]
        d_g4 = S("d_g4")
        d_a5 = S("d_a5")
        d_b5 = S("d_b5")
        d_w1 = S("d_w1")
        d_w5 = [S("d_w50"), S("d_w51")]
        d_h = [S("d_h0"), S("d_h1")]
        d_kv = [S("d_kv0"), S("d_kv1")]
        d_kn6 = S("d_kn6")
        d_b7 = S("d_b7")
        d_a7 = S("d_a7")
        d_hbs = [S("d_hbs0"), S("d_hbs1")]
        d_wo = [S("d_wo0"), S("d_wo1")]
        d_dn = [S("d_dn0"), S("d_dn1")]
        d_hb = [S("d_hb0"), S("d_hb1")]
        d_xr = [S("d_xr0"), S("d_xr1")]
        d_ot = [S("d_ot0"), S("d_ot1")]
        d_sp = S("d_sp")
        c_k = S("c_k")
        c_v = S("c_v")
        c_nn = S("c_nn")
        pe_s = S("pe_s")
        ve_s = S("ve_s")
        sc_s = S("sc_s")
        bar = S("bar")

        def phase_barrier():
            r.barrier(bar, {
                "sy": [d_w2[0], d_w2[1], d_kv[0], d_kv[1], d_g4, d_a5, d_b5,
                       d_w1, d_w5[0], d_w5[1], d_hbs[0], d_hbs[1], d_wo[0],
                       d_wo[1], d_dn[0], d_dn[1], d_kn6, d_b7, d_a7],
                "gp": [d_in, d_x[0], d_x[1], d_kst[0], d_kst[1], d_vtt[0],
                       d_vtt[1], d_sp, d_h[0], d_h[1], d_hb[0], d_hb[1],
                       d_xr[0], d_xr[1], d_ot[0], d_ot[1]],
            })

        rg_sy = [es.enter_context(nc.sync.register(f"rs{i}")) for i in range(3)]
        rg_pe = es.enter_context(nc.tensor.register("rt0"))
        rg_ve = es.enter_context(nc.vector.register("rv0"))
        rg_sc = es.enter_context(nc.scalar.register("rc0"))

        block = es.enter_context(nc.Block())
        r = Rec()

        r.dma("gp", d_in, idbf[:, :], dr_idbf)
        r.dma("gp", d_in, idf[:, :], dr_idf)
        m_consts = r.dma("gp", d_in, cfgs[:, :], dr_cfg)
        r.opn("ve", lambda e: e.memset(ones_col[:, :], 1.0))
        r.opn("ve", lambda e: e.memset(ones_col_bf[:, :], 1.0))
        r.opn("ve", lambda e: e.memset(ones_row[:, :], 1.0))
        r.opn("ve", lambda e: e.memset(zero_bf[:, :], 0.0))
        r.opn("ve", lambda e: e.memset(eps_t[:, :], EPS))
        r.opn("ve", lambda e: e.memset(qn2[:, :], 0.0))
        m_vec_consts = r.op("ve", ve_s, lambda e: e.memset(kn2[:, :], 0.0))

        with ExitStack() as _ph:
            xbuf = _ph.enter_context(nc.sbuf_tensor("xbuf", [128, 2 * H], F32))
            junk = _ph.enter_context(nc.sbuf_tensor("junk", [128, H], BF16))
            lnst = _ph.enter_context(nc.sbuf_tensor("lnst", [128, 16], F32))
            xntm = _ph.enter_context(nc.sbuf_tensor("xntm", [128, 2 * H], BF16))
            lntmp = _ph.enter_context(nc.sbuf_tensor("lntmp", [128, H], F32))
            lnac = _ph.enter_context(nc.sbuf_tensor("lnac", [128, 1], F32))
            psLN = _ph.enter_context(nc.psum_tensor("psLN", [128, 1024], F32))
            r.opn("ve", lambda e: e.memset(lnst[:, :], 0.0))
            free_xb = [0, 0]
            free_half = [0, 0]
            prev_var = 0
            xn3 = xn.ap().rearrange("p (k t) -> p k t", t=T)
            ps3 = psLN.ap().rearrange("p (g c) -> p g c", c=128)
            for t in range(16):
                par = t % 2
                xb = xbuf[:, par * H:(par + 1) * H]
                xt = xntm[:, par * H:(par + 1) * H]
                r.wait("gp", ve_s, free_xb[par])
                m_x = r.dma("gp", d_x[par], xb, dr_x[t * 128:(t + 1) * 128, :])
                r.wait("ve", d_x[par], m_x)
                m_s1 = r.op("ve", ve_s, lambda e, xb=xb: e.reduce_sum(lnst[:, 0:1], xb, axis=AX))
                r.wait("sc", d_x[par], m_x)
                r.wait("sc", ve_s, m_s1)
                r.opn("sc", lambda e: e.activation(lnst[:, 1:2], lnst[:, 0:1], AF.Copy, scale=1.0 / H))
                m_sq = r.op("sc", sc_s, lambda e, xb=xb: e.activation(
                    junk[:, :], xb, AF.Square, accum_out=lnac[:, 0:1]))
                r.wait("ve", sc_s, m_sq)
                r.vsync("ve", ve_s)
                r.opn("ve", lambda e: e.tensor_scalar(lnst[:, 9:10], lnst[:, 1:2], lnst[:, 0:1], None, OP.mult))
                r.vsync("ve", ve_s)
                m_var = r.op("ve", ve_s, lambda e: e.tensor_scalar(
                    lnst[:, 5:6], lnac[:, 0:1], lnst[:, 9:10], None, OP.subtract))
                prev_var = m_var
                r.wait("sc", ve_s, m_var)
                m_sd = r.op("sc", sc_s, lambda e: e.activation(
                    lnst[:, 6:7], lnst[:, 5:6], AF.Sqrt, scale=1.0 / H, bias=eps_t[:, 0:1]))
                r.wait("ve", sc_s, m_sd)
                r.opn("ve", lambda e: e.reciprocal(lnst[:, 7:8], lnst[:, 6:7]))
                r.opn("ve", lambda e: e.tensor_scalar(lnst[:, 8:9], lnst[:, 1:2], -1.0, None, OP.mult))
                r.vsync("ve", ve_s)
                r.opn("ve", lambda e, xb=xb: e.tensor_scalar(
                    lntmp[:, :], xb, lnst[:, 8:9], None, OP.add))
                m_xnt = r.op("ve", ve_s, lambda e, xt=xt: e.tensor_scalar(
                    xt, lntmp[:, :], lnst[:, 7:8], None, OP.mult))
                free_xb[par] = m_xnt
                for g in range(4):
                    half = g % 2
                    r.wait("pe", ve_s, max(m_xnt, free_half[half], m_vec_consts))
                    r.wait("pe", d_in, m_consts)
                    for kk in range(4):
                        k = g * 4 + kk
                        r.op("pe", pe_s, lambda e, xt=xt, k=k, half=half, kk=kk: e.matmul(
                            psLN[:, half * 512 + kk * 128: half * 512 + (kk + 1) * 128],
                            xt[:, k * 128:(k + 1) * 128], idbf[:, :],
                            start=True, stop=True))
                    m_tr = pe_s.n
                    r.wait("ve", pe_s, m_tr)
                    m_ev = r.op("ve", ve_s, lambda e, g=g, t=t, half=half: e.tensor_copy(
                        xn3[:, g * 4:(g + 1) * 4, t * 128:(t + 1) * 128],
                        ps3[:, half * 4:(half + 1) * 4, :]))
                    free_half[half] = m_ev
            m_xn_done = ve_s.n
            if "lnst" in tap_aps:
                r.wait("gp", ve_s, m_xn_done)
                r.dma("gp", d_sp, tap_aps["lnst"], lnst[:, :])

        if "xn" in tap_aps:
            r.wait("gp", ve_s, m_xn_done)
            r.dma("gp", d_sp, tap_aps["xn"], xn[:, :])

        phase_barrier()

        if phases >= 2:
            with ExitStack() as _ph:
                wst = _ph.enter_context(nc.sbuf_tensor("wst", [128, 2 * H], F32))
                wbf = _ph.enter_context(nc.sbuf_tensor("wbf", [128, 2 * H], BF16))
                kst = _ph.enter_context(nc.sbuf_tensor("kst", [128, 2 * 512], BF16))
                vst = _ph.enter_context(nc.sbuf_tensor("vst", [128, 2 * 512], BF16))
                vtt = _ph.enter_context(nc.sbuf_tensor("vtt", [128, 2 * 512], BF16))
                sqb = _ph.enter_context(nc.sbuf_tensor("sqb", [128, 512], BF16))
                nrm = _ph.enter_context(nc.sbuf_tensor("nrm", [1, 8], F32))
                nsb = _ph.enter_context(nc.sbuf_tensor("nsb", [1, 2048], F32))
                psQ = _ph.enter_context(nc.psum_tensor("psQ", [128, 4 * 512], F32))
                psN = _ph.enter_context(nc.psum_tensor("psN", [128, 1024], F32))
                psVT = _ph.enter_context(nc.psum_tensor("psVT", [128, 512], F32))
                wq3 = dr_wqkv.rearrange("(kc p) m -> p kc m", p=128)
                free_w = [0, 0]
                free_kst = [0, 0]
                free_vst = [0, 0]
                free_vtt = [0, 0]
                free_psq = [0, 0, 0, 0]
                free_psn = [0, 0]
                free_sqb = 0
                free_psvt = 0
                for m in range(48):
                    par = m % 2
                    r.wait("sy", pe_s, free_w[par])
                    m_w = r.dma("sy", d_w2[par],
                                wst[:, par * H:(par + 1) * H].rearrange("p (kc m) -> p kc m", m=128),
                                wq3[:, :, m * 128:(m + 1) * 128])
                    r.wait("ve", d_w2[par], m_w)
                    m_cv = r.op("ve", ve_s, lambda e, par=par: e.tensor_copy(
                        wbf[:, par * H:(par + 1) * H], wst[:, par * H:(par + 1) * H]))
                    last_mm = 0
                    for t in range(4):
                        r.wait("pe", ve_s, max(m_cv, m_xn_done))
                        r.wait("pe", sc_s, free_psq[t])
                        for k in range(16):
                            r.op("pe", pe_s, lambda e, par=par, k=k, t=t: e.matmul(
                                psQ[:, t * 512:(t + 1) * 512],
                                wbf[:, par * H + k * 128: par * H + (k + 1) * 128],
                                xn[:, k * T + t * 512: k * T + (t + 1) * 512],
                                start=(k == 0), stop=(k == 15)))
                        m_mm = pe_s.n
                        last_mm = m_mm
                        if m < 16:
                            hq = m
                            r.wait("sc", pe_s, m_mm)
                            m_ev = r.op("sc", sc_s, lambda e, t=t, hq=hq: e.activation(
                                QO[:, hq * T + t * 512: hq * T + (t + 1) * 512],
                                psQ[:, t * 512:(t + 1) * 512], AF.Copy, scale=ISQD))
                            free_psq[t] = m_ev
                            r.wait("sc", pe_s, free_sqb)
                            m_sq = r.op("sc", sc_s, lambda e, t=t, hq=hq: e.activation(
                                sqb[:, :], QO[:, hq * T + t * 512: hq * T + (t + 1) * 512], AF.Square))
                            nh = t % 2
                            r.wait("pe", sc_s, m_sq)
                            r.wait("pe", ve_s, free_psn[nh])
                            m_on = r.op("pe", pe_s, lambda e, nh=nh: e.matmul(
                                psN[0:1, nh * 512:(nh + 1) * 512], ones_col_bf[:, :], sqb[:, :],
                                start=True, stop=True))
                            free_sqb = m_on
                            r.wait("ve", pe_s, m_on)
                            m_nm = r.op("ve", ve_s, lambda e, nh=nh, t=t: e.tensor_copy(
                                nsb[0:1, t * 512:(t + 1) * 512], psN[0:1, nh * 512:(nh + 1) * 512]))
                            free_psn[nh] = m_nm
                            if t == 3:
                                r.opn("ve", lambda e: e.reduce_max(nrm[0:1, 0:1], nsb[0:1, 0:1024], axis=AX))
                                m_nm2 = r.op("ve", ve_s, lambda e: e.reduce_max(nrm[0:1, 1:2], nsb[0:1, 1024:2048], axis=AX))
                                r.wait("sc", ve_s, m_nm2)
                                r.opn("sc", lambda e, hq=hq: e.activation(qn2[0:1, hq * 2: hq * 2 + 1], nrm[0:1, 0:1], AF.Copy))
                                m_qn = r.op("sc", sc_s, lambda e, hq=hq: e.activation(qn2[0:1, hq * 2 + 1: hq * 2 + 2], nrm[0:1, 1:2], AF.Copy))
                                r.wait("ve", sc_s, m_qn)
                                r.op("ve", ve_s, lambda e: e.nop())
                        elif m < 32:
                            hk = m - 16
                            kp = t % 2
                            r.wait("sc", pe_s, m_mm)
                            r.wait("sc", d_kst[kp], free_kst[kp])
                            m_ev = r.op("sc", sc_s, lambda e, t=t, kp=kp: e.activation(
                                kst[:, kp * 512:(kp + 1) * 512], psQ[:, t * 512:(t + 1) * 512], AF.Copy))
                            free_psq[t] = m_ev
                            r.wait("gp", sc_s, m_ev)
                            m_kd = r.dma("gp", d_kst[kp],
                                         dr_ck[hk * 128:(hk + 1) * 128, t * 512:(t + 1) * 512],
                                         kst[:, kp * 512:(kp + 1) * 512])
                            free_kst[kp] = m_kd
                            r.wait("sc", pe_s, free_sqb)
                            m_sq = r.op("sc", sc_s, lambda e, kp=kp: e.activation(
                                sqb[:, :], kst[:, kp * 512:(kp + 1) * 512], AF.Square))
                            nh = t % 2
                            r.wait("pe", sc_s, m_sq)
                            r.wait("pe", ve_s, free_psn[nh])
                            m_on = r.op("pe", pe_s, lambda e, nh=nh: e.matmul(
                                psN[0:1, nh * 512:(nh + 1) * 512], ones_col_bf[:, :], sqb[:, :],
                                start=True, stop=True))
                            free_sqb = m_on
                            r.wait("ve", pe_s, m_on)
                            m_nm = r.op("ve", ve_s, lambda e, nh=nh, t=t: e.tensor_copy(
                                nsb[0:1, t * 512:(t + 1) * 512], psN[0:1, nh * 512:(nh + 1) * 512]))
                            free_psn[nh] = m_nm
                            if t == 3:
                                m_nm2 = r.op("ve", ve_s, lambda e: e.reduce_max(nrm[0:1, 2:3], nsb[0:1, 0:2048], axis=AX))
                                r.wait("sc", ve_s, m_nm2)
                                m_kn = r.op("sc", sc_s, lambda e, hk=hk: e.activation(kn2[0:1, hk:hk + 1], nrm[0:1, 2:3], AF.Copy))
                                r.wait("ve", sc_s, m_kn)
                                r.op("ve", ve_s, lambda e: e.nop())
                        else:
                            hv = m - 32
                            vp = t % 2
                            r.wait("sc", pe_s, m_mm)
                            r.wait("sc", pe_s, free_vst[vp])
                            m_ev = r.op("sc", sc_s, lambda e, t=t, vp=vp: e.activation(
                                vst[:, vp * 512:(vp + 1) * 512], psQ[:, t * 512:(t + 1) * 512], AF.Copy))
                            free_psq[t] = m_ev
                            r.wait("pe", sc_s, m_ev)
                            r.wait("pe", ve_s, free_psvt)
                            for i in range(4):
                                r.op("pe", pe_s, lambda e, vp=vp, i=i: e.matmul(
                                    psVT[:, i * 128:(i + 1) * 128],
                                    vst[:, vp * 512 + i * 128: vp * 512 + (i + 1) * 128],
                                    idbf[:, :], start=True, stop=True))
                            m_tr = pe_s.n
                            free_vst[vp] = m_tr
                            r.wait("ve", pe_s, m_tr)
                            r.wait("ve", d_vtt[vp], free_vtt[vp])
                            m_vv = r.op("ve", ve_s, lambda e, vp=vp: e.tensor_copy(
                                vtt[:, vp * 512:(vp + 1) * 512], psVT[:, :]))
                            free_psvt = m_vv
                            r.wait("gp", ve_s, m_vv)
                            m_vd = r.dma("gp", d_vtt[vp],
                                         dr_cv[t * 512:(t + 1) * 512, hv * 128:(hv + 1) * 128]
                                         .rearrange("(i p) d -> p i d", p=128),
                                         vtt[:, vp * 512:(vp + 1) * 512].rearrange("p (i d) -> p i d", d=128))
                            free_vtt[vp] = m_vd
                    free_w[par] = last_mm
                m_qkv_pe = pe_s.n
                m_qkv_ve = ve_s.n
            r.wait("gp", ve_s, m_qkv_ve)
            r.dma("gp", d_sp, dr_kni, kn2[:, :])

        if phases >= 3:
            r.wait("gp", d_sp, d_sp.n)
            r.wait("gp", d_kst[0], d_kst[0].n)
            r.wait("gp", d_kst[1], d_kst[1].n)
            r.wait("gp", d_vtt[0], d_vtt[0].n)
            r.wait("gp", d_vtt[1], d_vtt[1].n)
            r.op("gp", c_k, lambda e: e.collective_compute(
                "AllGather", OP.bypass, replica_groups=GROUPS,
                ins=[dr_ck], outs=[dr_gk]))
            r.op("gp", c_v, lambda e: e.collective_compute(
                "AllGather", OP.bypass, replica_groups=GROUPS,
                ins=[dr_cv], outs=[dr_gv]))
            r.op("gp", c_nn, lambda e: e.collective_compute(
                "AllReduce", OP.max, replica_groups=GROUPS,
                ins=[dr_kni], outs=[dr_kno]))

        if phases >= 4:
            phase_barrier()

        if phases >= 4:
            with ExitStack() as _ph:
                gst = _ph.enter_context(nc.sbuf_tensor("gst", [16, 2 * H], F32))
                gbst = _ph.enter_context(nc.sbuf_tensor("gbst", [8, 16], F32))
                gbf = _ph.enter_context(nc.sbuf_tensor("gbf", [16, 2 * H], BF16))
                gbbf = _ph.enter_context(nc.sbuf_tensor("gbbf", [8, 16], BF16))
                wgT = _ph.enter_context(nc.sbuf_tensor("wgT", [128, 16 * 8], BF16))
                agT = _ph.enter_context(nc.sbuf_tensor("agT", [128, 16 * 16], BF16))
                bgT = _ph.enter_context(nc.sbuf_tensor("bgT", [16, 8], BF16))
                rterg = _ph.enter_context(nc.sbuf_tensor("rterg", [16, 512], BF16))
                lgsb = _ph.enter_context(nc.sbuf_tensor("lgsb", [8, 512], F32))
                rst = _ph.enter_context(nc.sbuf_tensor("rst", [128, 96], F32))
                wtm = _ph.enter_context(nc.sbuf_tensor("wtm", [128, 8], F32))
                psR = _ph.enter_context(nc.psum_tensor("psR", [128, 1024], F32))
                psR2 = _ph.enter_context(nc.psum_tensor("psR2", [128, 512], F32))
                r.dma("sy", d_g4, gst[0:8, 0:H], dr_gw)
                r.dma("sy", d_g4, gst[0:16, H:2 * H], dr_gA)
                m_gl = r.dma("sy", d_g4, gbst[:, :], dr_gB)
                r.wait("ve", d_g4, m_gl)
                r.opn("ve", lambda e: e.tensor_copy(gbf[0:8, 0:H], gst[0:8, 0:H]))
                r.opn("ve", lambda e: e.tensor_copy(gbf[0:16, H:2 * H], gst[0:16, H:2 * H]))
                m_gc = r.op("ve", ve_s, lambda e: e.tensor_copy(gbbf[:, :], gbst[:, :]))
                last_ev = [0, 0]
                r.wait("pe", ve_s, m_gc)
                for k in range(16):
                    pp = k % 2
                    r.wait("pe", ve_s, last_ev[pp])
                    r.op("pe", pe_s, lambda e, k=k, pp=pp: e.matmul(
                        psR[:, pp * 512: pp * 512 + 8],
                        gbf[0:8, k * 128:(k + 1) * 128], idbf[0:8, 0:8],
                        start=True, stop=True))
                    m_t = pe_s.n
                    r.wait("ve", pe_s, m_t)
                    last_ev[pp] = r.op("ve", ve_s, lambda e, k=k, pp=pp: e.tensor_copy(
                        wgT[:, k * 8:(k + 1) * 8], psR[:, pp * 512: pp * 512 + 8]))
                for k in range(16):
                    pp = k % 2
                    r.wait("pe", ve_s, last_ev[pp])
                    r.op("pe", pe_s, lambda e, k=k, pp=pp: e.matmul(
                        psR[:, pp * 512: pp * 512 + 16],
                        gbf[0:16, H + k * 128: H + (k + 1) * 128], idbf[0:16, 0:16],
                        start=True, stop=True))
                    m_t = pe_s.n
                    r.wait("ve", pe_s, m_t)
                    last_ev[pp] = r.op("ve", ve_s, lambda e, k=k, pp=pp: e.tensor_copy(
                        agT[:, k * 16:(k + 1) * 16], psR[:, pp * 512: pp * 512 + 16]))
                r.wait("pe", ve_s, last_ev[0])
                r.op("pe", pe_s, lambda e: e.matmul(psR[0:16, 0:8], gbbf[:, :], idbf[0:8, 0:8],
                                               start=True, stop=True))
                m_t3 = pe_s.n
                r.wait("ve", pe_s, m_t3)
                m_bgT = r.op("ve", ve_s, lambda e: e.tensor_copy(bgT[:, :], psR[0:16, 0:8]))

                prev_psR_sc = 0
                prev_psR_ve = m_bgT
                for t in range(4):
                    r.wait("pe", ve_s, prev_psR_ve)
                    r.wait("pe", sc_s, prev_psR_sc)
                    for k in range(16):
                        r.op("pe", pe_s, lambda e, k=k, t=t: e.matmul(
                            psR[0:16, 0:512], agT[:, k * 16:(k + 1) * 16],
                            xn[:, k * T + t * 512: k * T + (t + 1) * 512],
                            start=(k == 0), stop=(k == 15)))
                    m_ter = pe_s.n
                    r.wait("sc", pe_s, m_ter)
                    m_terg = r.op("sc", sc_s, lambda e: e.activation(
                        rterg[:, :], psR[0:16, 0:512], AF.Copy, scale=SCALING))
                    r.wait("pe", sc_s, m_terg)
                    for k in range(16):
                        r.opn("pe", lambda e, k=k, t=t: e.matmul(
                            psR2[0:8, 0:512], wgT[:, k * 8:(k + 1) * 8],
                            xn[:, k * T + t * 512: k * T + (t + 1) * 512],
                            start=(k == 0), stop=False))
                    r.op("pe", pe_s, lambda e: e.matmul(psR2[0:8, 0:512], bgT[:, :], rterg[:, :],
                                                   start=False, stop=True))
                    m_lg = pe_s.n
                    r.wait("ve", pe_s, m_lg)
                    m_lgs = r.op("ve", ve_s, lambda e: e.tensor_copy(lgsb[:, :], psR2[0:8, 0:512]))
                    r.wait("pe", ve_s, m_lgs)
                    for i in range(4):
                        r.op("pe", pe_s, lambda e, i=i: e.matmul(
                            psR[:, 512 + i * 128: 512 + i * 128 + 8],
                            lgsb[0:8, i * 128:(i + 1) * 128],
                            idf[0:8, 0:8], start=True, stop=True))
                    m_lt = pe_s.n
                    for i in range(4):
                        sl = psR[:, 512 + i * 128: 512 + i * 128 + 8]
                        r.wait("ve", pe_s, m_lt)
                        m_nm = r.op("ve", ve_s, lambda e, sl=sl: e.reduce_max(
                            rst[:, 0:1], sl, axis=AX, negate=True))
                        r.wait("sc", ve_s, m_nm)
                        m_px = r.op("sc", sc_s, lambda e, sl=sl: e.activation(
                            rst[:, 8:16], sl, AF.Exp, bias=rst[:, 0:1], accum_out=rst[:, 1:2]))
                        prev_psR_sc = m_px
                        r.wait("ve", sc_s, m_px)
                        r.opn("ve", lambda e: e.reciprocal(rst[:, 2:3], rst[:, 1:2]))
                        r.vsync("ve", ve_s)
                        m_p = r.op("ve", ve_s, lambda e: e.tensor_scalar(rst[:, 16:24], rst[:, 8:16], rst[:, 2:3], None, OP.mult))
                        r.wait("sc", ve_s, m_p)
                        m_pc = r.op("sc", sc_s, lambda e: e.activation(rst[:, 48:56], rst[:, 16:24], AF.Copy))
                        r.wait("ve", sc_s, m_pc)
                        r.opn("ve", lambda e: e.reduce_max(rst[:, 24:25], rst[:, 48:56], axis=AX))
                        r.vsync("ve", ve_s)
                        m_pm = r.op("ve", ve_s, lambda e: e.scalar_tensor_tensor(
                            rst[:, 32:40], rst[:, 48:56], rst[:, 24:25], rst[:, 48:56], OP.is_lt, OP.mult))
                        r.wait("sc", ve_s, m_pm)
                        r.opn("sc", lambda e: e.activation(rst[:, 56:64], rst[:, 32:40], AF.Copy))
                        m_m1c = r.op("sc", sc_s, lambda e: e.activation(rst[:, 28:29], rst[:, 24:25], AF.Copy))
                        r.wait("ve", sc_s, m_m1c)
                        r.opn("ve", lambda e: e.reduce_max(rst[:, 25:26], rst[:, 56:64], axis=AX))
                        r.vsync("ve", ve_s)
                        m_den = r.op("ve", ve_s, lambda e: e.tensor_scalar(rst[:, 26:27], rst[:, 28:29], rst[:, 25:26], None, OP.add))
                        r.wait("sc", ve_s, m_den)
                        m_dc = r.op("sc", sc_s, lambda e: e.activation(rst[:, 29:30], rst[:, 26:27], AF.Copy))
                        r.wait("ve", sc_s, m_dc)
                        r.opn("ve", lambda e: e.reciprocal(rst[:, 27:28], rst[:, 29:30]))
                        r.vsync("ve", ve_s)
                        m_sel = r.op("ve", ve_s, lambda e: e.scalar_tensor_tensor(
                            rst[:, 40:48], rst[:, 48:56], rst[:, 25:26], rst[:, 48:56], OP.is_ge, OP.mult))
                        r.wait("sc", ve_s, m_sel)
                        m_slc = r.op("sc", sc_s, lambda e: e.activation(rst[:, 64:72], rst[:, 40:48], AF.Copy))
                        r.wait("ve", sc_s, m_slc)
                        m_wtm = r.op("ve", ve_s, lambda e: e.tensor_scalar(
                            wtm[:, 0:8], rst[:, 64:72], rst[:, 27:28], None, OP.mult))
                        prev_psR_ve = m_wtm
                        r.wait("pe", ve_s, m_wtm)
                        r.op("pe", pe_s, lambda e: e.matmul(psR2[0:8, 0:128], wtm[:, 0:8], idf[:, :],
                                                       start=True, stop=True))
                        m_wt = pe_s.n
                        r.wait("ve", pe_s, m_wt)
                        m_w8 = r.op("ve", ve_s, lambda e, t=t, i=i: e.tensor_copy(
                            wT8[:, t * 512 + i * 128: t * 512 + (i + 1) * 128], psR2[0:8, 0:128]))
                        r.wait("pe", ve_s, m_w8)
                m_rout_ve = ve_s.n
            r.wait("gp", ve_s, m_rout_ve)
            m_wsp = r.dma("gp", d_sp, dr_wr, wT8[:, :])
            if "w" in tap_aps:
                r.wait("gp", d_sp, m_wsp)
                r.dma("gp", d_sp, tap_aps["w"], dr_wr)

        if phases >= 5:
            phase_barrier()

        if phases >= 5:
            with ExitStack() as _ph:
                agupT = _ph.enter_context(nc.sbuf_tensor("agupT", [128, 256], BF16))
                gast = _ph.enter_context(nc.sbuf_tensor("gast", [16, H // 2], F32))
                gabf = _ph.enter_context(nc.sbuf_tensor("gabf", [16, H], BF16))
                gbst2 = _ph.enter_context(nc.sbuf_tensor("gbst2", [128, 256], F32))
                gbbf2 = _ph.enter_context(nc.sbuf_tensor("gbbf2", [128, 256], BF16))
                bgupT = _ph.enter_context(nc.sbuf_tensor("bgupT", [16, H], BF16))
                gter = _ph.enter_context(nc.sbuf_tensor("gter", [16, T], BF16))
                w1r = _ph.enter_context(nc.sbuf_tensor("w1r", [1, 512], F32))
                wrep = _ph.enter_context(nc.sbuf_tensor("wrep", [128, 4 * 512], BF16))
                gwst = _ph.enter_context(nc.sbuf_tensor("gwst", [128, 4096], F32))
                gwbf = _ph.enter_context(nc.sbuf_tensor("gwbf", [128, 2 * 4096], BF16))
                sgx = _ph.enter_context(nc.sbuf_tensor("sgx", [128, 2 * 512], BF16))
                upx = _ph.enter_context(nc.sbuf_tensor("upx", [128, 2 * 512], BF16))
                hst = _ph.enter_context(nc.sbuf_tensor("hst", [128, 2 * 512], BF16))
                psT5 = _ph.enter_context(nc.psum_tensor("psT5", [128, 512], F32))
                psW5 = _ph.enter_context(nc.psum_tensor("psW5", [128, 512], F32))
                psG = _ph.enter_context(nc.psum_tensor("psG", [128, 2 * 512], F32))
                psU5 = _ph.enter_context(nc.psum_tensor("psU5", [128, 2 * 512], F32))
                prev_t5_sc = 0
                m_prev_gabf_rd = 0
                m_prev_gbbf_rd = 0
                gw3 = dr_gup.rearrange("e (kc p) m -> e p kc m", p=128)
                free_gw = [0, 0]
                m_gw_conv = 0
                free_psg = [0, 0]
                free_psu = [0, 0]
                free_sgx = [0, 0]
                free_upx = [0, 0]
                free_hst = [0, 0]
                m_bc_prev = 0
                for e_i in range(E):
                    m_ac = 0
                    for hf in range(2):
                        r.wait("sy", ve_s, m_ac)
                        r.wait("sy", pe_s, m_prev_gabf_rd)
                        m_a = r.dma("sy", d_a5, gast[:, :], dr_gupA[e_i, :, hf * (H // 2):(hf + 1) * (H // 2)])
                        r.wait("ve", d_a5, m_a)
                        m_ac = r.op("ve", ve_s, lambda e, hf=hf: e.tensor_copy(
                            gabf[:, hf * (H // 2):(hf + 1) * (H // 2)], gast[:, :]))
                    m_agupT = 0
                    for k in range(16):
                        r.wait("pe", ve_s, max(m_ac, m_agupT))
                        r.wait("pe", sc_s, prev_t5_sc)
                        r.op("pe", pe_s, lambda e, k=k: e.matmul(
                            psT5[:, 0:16], gabf[0:16, k * 128:(k + 1) * 128],
                            idbf[0:16, 0:16], start=True, stop=True))
                        m_t = pe_s.n
                        r.wait("ve", pe_s, m_t)
                        m_agupT = r.op("ve", ve_s, lambda e, k=k: e.tensor_copy(
                            agupT[:, k * 16:(k + 1) * 16], psT5[:, 0:16]))
                    m_prev_gabf_rd = pe_s.n
                    r.wait("sy", ve_s, m_bc_prev)
                    r.wait("sy", pe_s, m_prev_gbbf_rd)
                    m_b = r.dma("sy", d_b5, gbst2.ap().rearrange("p (c r) -> p c r", r=16),
                                dr_gupB[e_i, :, :].rearrange("(c p) r -> p c r", p=128))
                    r.wait("ve", d_b5, m_b)
                    r.wait("ve", pe_s, m_prev_gbbf_rd)
                    m_bc = r.op("ve", ve_s, lambda e: e.tensor_copy(gbbf2[:, :], gbst2[:, :]))
                    m_bc_prev = m_bc
                    last_ev = 0
                    for k in range(16):
                        r.wait("pe", ve_s, max(m_bc, last_ev))
                        r.op("pe", pe_s, lambda e, k=k: e.matmul(
                            psT5[0:16, 0:128], gbbf2[:, k * 16:(k + 1) * 16],
                            idbf[:, :], start=True, stop=True))
                        m_t = pe_s.n
                        r.wait("ve", pe_s, m_t)
                        last_ev = r.op("ve", ve_s, lambda e, k=k: e.tensor_copy(
                            bgupT[0:16, k * 128:(k + 1) * 128], psT5[0:16, 0:128]))
                    m_prev_gbbf_rd = pe_s.n
                    r.wait("sy", d_sp, m_wsp)
                    last_wr = 0
                    last_w1_rd = 0
                    for t in range(4):
                        r.wait("sy", pe_s, last_w1_rd)
                        m_w1 = r.dma("sy", d_w1, w1r[:, :], dr_wr[e_i:e_i + 1, t * 512:(t + 1) * 512])
                        r.wait("pe", d_w1, m_w1)
                        r.wait("pe", ve_s, max(last_wr, last_ev))
                        r.op("pe", pe_s, lambda e: e.matmul(
                            psW5[:, 0:512], ones_row[:, :], w1r[0:1, 0:512],
                            start=True, stop=True))
                        m_t = pe_s.n
                        last_w1_rd = m_t
                        r.wait("ve", pe_s, m_t)
                        last_wr = r.op("ve", ve_s, lambda e, t=t: e.tensor_copy(
                            wrep[:, t * 512:(t + 1) * 512], psW5[:, 0:512]))
                    m_wrep = last_wr
                    last_terg = prev_t5_sc
                    for t in range(4):
                        r.wait("pe", sc_s, last_terg)
                        for k in range(16):
                            r.op("pe", pe_s, lambda e, k=k, t=t: e.matmul(
                                psT5[0:16, 0:512], agupT[:, k * 16:(k + 1) * 16],
                                xn[:, k * T + t * 512: k * T + (t + 1) * 512],
                                start=(k == 0), stop=(k == 15)))
                        m_tm = pe_s.n
                        r.wait("sc", pe_s, m_tm)
                        last_terg = r.op("sc", sc_s, lambda e, t=t: e.activation(
                            gter[:, t * 512:(t + 1) * 512], psT5[0:16, 0:512], AF.Copy, scale=SCALING))
                    prev_t5_sc = last_terg
                    m_gter = last_terg
                    for mp in range(8):
                        par = mp % 2
                        gpar = (e_i * 8 + mp) % 2
                        r.wait("sy", pe_s, free_gw[par])
                        r.wait("sy", ve_s, m_gw_conv)
                        r.dma("sy", d_w5[gpar],
                              gwst[:, 0:2048].rearrange("p (kc m) -> p kc m", m=128),
                              gw3[e_i, :, :, mp * 128:(mp + 1) * 128])
                        m_w2 = r.dma("sy", d_w5[gpar],
                                     gwst[:, 2048:4096].rearrange("p (kc m) -> p kc m", m=128),
                                     gw3[e_i, :, :, (mp + 8) * 128:(mp + 9) * 128])
                        r.wait("ve", d_w5[gpar], m_w2)
                        m_wc = r.op("ve", ve_s, lambda e, par=par: e.tensor_copy(
                            gwbf[:, par * 4096:(par + 1) * 4096], gwst[:, :]))
                        m_gw_conv = m_wc
                        for t in range(4):
                            tp = t % 2
                            r.wait("pe", ve_s, max(m_wc, m_wrep))
                            r.wait("pe", sc_s, max(m_gter, free_psg[tp]))
                            for k in range(16):
                                r.opn("pe", lambda e, par=par, k=k, t=t, tp=tp: e.matmul(
                                    psG[:, tp * 512:(tp + 1) * 512],
                                    gwbf[:, par * 4096 + k * 128: par * 4096 + (k + 1) * 128],
                                    xn[:, k * T + t * 512: k * T + (t + 1) * 512],
                                    start=(k == 0), stop=False))
                            r.op("pe", pe_s, lambda e, mp=mp, t=t, tp=tp: e.matmul(
                                psG[:, tp * 512:(tp + 1) * 512],
                                bgupT[0:16, mp * 128:(mp + 1) * 128],
                                gter[:, t * 512:(t + 1) * 512], start=False, stop=True))
                            m_gmm = pe_s.n
                            r.wait("pe", ve_s, free_psu[tp])
                            for k in range(16):
                                r.opn("pe", lambda e, par=par, k=k, t=t, tp=tp: e.matmul(
                                    psU5[:, tp * 512:(tp + 1) * 512],
                                    gwbf[:, par * 4096 + 2048 + k * 128: par * 4096 + 2048 + (k + 1) * 128],
                                    xn[:, k * T + t * 512: k * T + (t + 1) * 512],
                                    start=(k == 0), stop=False))
                            r.op("pe", pe_s, lambda e, mp=mp, t=t, tp=tp: e.matmul(
                                psU5[:, tp * 512:(tp + 1) * 512],
                                bgupT[0:16, (mp + 8) * 128:(mp + 9) * 128],
                                gter[:, t * 512:(t + 1) * 512], start=False, stop=True))
                            m_umm = pe_s.n
                            r.wait("sc", pe_s, m_gmm)
                            r.wait("sc", ve_s, free_sgx[tp])
                            m_sg = r.op("sc", sc_s, lambda e, tp=tp: e.activation(
                                sgx[:, tp * 512:(tp + 1) * 512], psG[:, tp * 512:(tp + 1) * 512], AF.Silu))
                            free_psg[tp] = m_sg
                            r.wait("ve", pe_s, m_umm)
                            r.wait("ve", ve_s, free_upx[tp])
                            m_ux = r.op("ve", ve_s, lambda e, t=t, tp=tp: e.tensor_tensor(
                                upx[:, tp * 512:(tp + 1) * 512], psU5[:, tp * 512:(tp + 1) * 512],
                                wrep[:, t * 512:(t + 1) * 512], OP.mult))
                            free_psu[tp] = m_ux
                            r.wait("ve", sc_s, m_sg)
                            r.wait("ve", d_h[tp], free_hst[tp])
                            m_h = r.op("ve", ve_s, lambda e, tp=tp: e.tensor_tensor(
                                hst[:, tp * 512:(tp + 1) * 512], sgx[:, tp * 512:(tp + 1) * 512],
                                upx[:, tp * 512:(tp + 1) * 512], OP.mult))
                            free_sgx[tp] = m_h
                            free_upx[tp] = m_h
                            r.wait("gp", ve_s, m_h)
                            m_hd = r.dma("gp", d_h[tp],
                                         dr_h[e_i, mp * 128:(mp + 1) * 128, t * 512:(t + 1) * 512],
                                         hst[:, tp * 512:(tp + 1) * 512])
                            free_hst[tp] = m_hd
                        free_gw[par] = pe_s.n
            m_gup_done = (d_h[0].n, d_h[1].n)
            if "h" in tap_aps:
                r.wait("gp", d_h[0], m_gup_done[0])
                r.wait("gp", d_h[1], m_gup_done[1])
                r.dma("gp", d_sp, tap_aps["h"], dr_h)

        if phases >= 6:
            phase_barrier()
        w45_es.close()
        xn_es.close()

        if phases >= 6:
            with ExitStack() as _ph:
                kbuf = _ph.enter_context(nc.sbuf_tensor("kbuf", [128, 2 * 1024], BF16))
                vbuf = _ph.enter_context(nc.sbuf_tensor("vbuf", [128, 2 * 1024], BF16))
                ptb = _ph.enter_context(nc.sbuf_tensor("ptb", [128, 2 * 1024], BF16))
                smask = _ph.enter_context(nc.sbuf_tensor("smask", [128, 2 * 512], F32))
                masks = _ph.enter_context(nc.sbuf_tensor("masks", [128, 4 * 512], F32))
                ssum = _ph.enter_context(nc.sbuf_tensor("ssum", [128, 2 * 2048], F32))
                knrow = _ph.enter_context(nc.sbuf_tensor("knrow", [1, NHEAD], F32))
                kdup = _ph.enter_context(nc.sbuf_tensor("kdup", [1, 32], F32))
                crow = _ph.enter_context(nc.sbuf_tensor("crow", [1, 32], F32))
                csq = _ph.enter_context(nc.sbuf_tensor("csq", [1, 32], F32))
                rrow = _ph.enter_context(nc.sbuf_tensor("rrow", [1, 1024], F32))
                rrep = _ph.enter_context(nc.sbuf_tensor("rrep", [128, 1024], F32))
                psS0 = _ph.enter_context(nc.psum_tensor("psS0", [128, 1024], F32))
                psS1 = _ph.enter_context(nc.psum_tensor("psS1", [128, 1024], F32))
                psUa = _ph.enter_context(nc.psum_tensor("psUa", [128, 2048], F32))
                psS = [psS0, psS1]
                m_masks = r.dma("gp", d_in, masks[:, :], dr_mask)
                r.wait("sy", c_nn, 1)
                m_kn = r.dma("sy", d_kn6, knrow[:, :], dr_kno)
                kd3 = kdup.ap().rearrange("o (hh two) -> o hh two", two=2)
                kn3 = knrow.ap().rearrange("o (h u) -> o h u", u=1)
                r.wait("sc", d_kn6, m_kn)
                r.opn("sc", lambda e: e.activation(kd3[:, :, 0:1], kn3, AF.Copy))
                m_kd = r.op("sc", sc_s, lambda e: e.activation(kd3[:, :, 1:2], kn3, AF.Copy))
                r.wait("ve", sc_s, m_kd)
                r.wait("ve", ve_s, m_qkv_ve)
                m_cp = r.op("ve", ve_s, lambda e: e.tensor_tensor(csq[:, :], qn2[:, :], kdup[:, :], OP.mult))
                r.wait("sc", ve_s, m_cp)
                m_cr = r.op("sc", sc_s, lambda e: e.activation(crow[:, :], csq[:, :], AF.Sqrt))
                r.wait("ve", sc_s, m_cr)
                m_ng = r.op("ve", ve_s, lambda e: e.tensor_scalar(csq[:, :], crow[:, :], -1.0, None, OP.mult))
                r.wait("pe", ve_s, m_ng)
                r.wait("pe", pe_s, m_qkv_pe)
                r.op("pe", pe_s, lambda e: e.matmul(psS0[:, 0:32], ones_row[:, :], csq[0:1, :],
                                               start=True, stop=True))
                m_nr = pe_s.n
                r.wait("ve", pe_s, m_nr)
                m_negc = r.op("ve", ve_s, lambda e: e.tensor_copy(negc_rep[:, :], psS0[:, 0:32]))
                if "negc" in tap_aps:
                    r.wait("gp", ve_s, m_negc)
                    r.dma("gp", d_sp, tap_aps["negc"], negc_rep[:, :])
                r.wait("pe", ve_s, m_negc)
                r.wait("sy", c_k, 1)
                r.wait("sy", c_v, 1)
                r.wait("ve", d_in, m_masks)
                r.wait("sc", ve_s, m_negc)

                gv3 = dr_gv.rearrange("(s p) d -> p s d", p=128)

                def emit_attn_variant(rv, cl):
                    units = [(0, cl), (1, 7 - cl)]
                    for jj in range(cl):
                        units.append((0, jj))
                    for jj in range(7 - cl):
                        units.append((1, jj))
                    kv_pe_done = {}
                    g_seq = 0
                    for h_i in range(NHEAD):
                        rv.wait("pe", ve_s, ve_s.n)
                        for j4 in range(4):
                            rv.op("pe", pe_s, lambda e, j4=j4: e.matmul(
                                psUa[:, j4 * 512:(j4 + 1) * 512], zero_bf[:, 0:128], zero_bf[:, :],
                                start=True, stop=False))
                        pe_s.n
                        m_ssz = rv.op("ve", ve_s, lambda e: e.memset(ssum[:, 0:2048], 0.0))
                        m_exp_prev = [0, 0]
                        m_psS_ve = [0, 0]
                        m_pt_cons = [(0, 0), (0, 0)]
                        m_smask_rd = [0, 0]
                        for u in range(9):
                            qc, jch = units[u]
                            up = g_seq % 2
                            rk = jch if jch < 4 else 7 - jch
                            lci = 0 if jch < 4 else 1
                            rv.wait("sy", pe_s, kv_pe_done.get(g_seq - 2, 0))
                            rv.dma("sy", d_kv[up],
                                   kbuf[:, up * 1024:(up + 1) * 1024],
                                   dr_gk[rk * H + h_i * 128: rk * H + (h_i + 1) * 128,
                                         lci * 1024:(lci + 1) * 1024])
                            vs = (rk * T + lci * 1024) // 128
                            m_kvd = rv.dma("sy", d_kv[up],
                                           vbuf[:, up * 1024:(up + 1) * 1024].rearrange("p (s d) -> p s d", d=128),
                                           gv3[:, vs:vs + 8, h_i * 128:(h_i + 1) * 128])
                            nbias = negc_rep[:, h_i * 2 + qc: h_i * 2 + qc + 1]
                            diag = u < 2
                            pending = []

                            def emit_av(rv=rv, up=up, qc=qc):
                                sss, pp, segs, m_exp = pending.pop(0)
                                rv.wait("pe", sc_s, m_exp if m_exp else sc_s.n)
                                for qh in range(2):
                                    if segs[qh] == "c":
                                        continue
                                    rv.op("pe", pe_s, lambda e, sss=sss, qh=qh, pp=pp: e.matmul(
                                        psUa[:, qc * 1024 + qh * 512: qc * 1024 + (qh + 1) * 512],
                                        vbuf[:, up * 1024 + sss * 128: up * 1024 + (sss + 1) * 128],
                                        ptb[:, pp * 1024 + qh * 512: pp * 1024 + (qh + 1) * 512],
                                        start=False, stop=False))
                                m_pt_cons[pp] = (pe_s.n, m_pt_cons[pp][1])

                            for sss in range(8):
                                pp = sss % 2
                                segs = [_diag_status(sss, qh) for qh in range(2)] if diag else ["f", "f"]
                                rv.wait("pe", d_kv[up], m_kvd)
                                rv.wait("pe", sc_s, m_exp_prev[pp])
                                rv.wait("pe", ve_s, m_psS_ve[pp])
                                for qh in range(2):
                                    if segs[qh] == "c":
                                        continue
                                    rv.op("pe", pe_s, lambda e, sss=sss, qh=qh, up=up, pp=pp, h_i=h_i, qc=qc: e.matmul(
                                        psS[pp][:, qh * 512:(qh + 1) * 512],
                                        kbuf[:, up * 1024 + sss * 128: up * 1024 + (sss + 1) * 128],
                                        QO[:, h_i * T + qc * 1024 + qh * 512: h_i * T + qc * 1024 + (qh + 1) * 512],
                                        start=True, stop=True))
                                m_smm = pe_s.n
                                if pending:
                                    emit_av()
                                m_msk = 0
                                if diag:
                                    for qh in range(2):
                                        if segs[qh] != "p":
                                            continue
                                        mi = (sss * 128 - qh * 512) // 128
                                        rv.wait("ve", pe_s, m_smm)
                                        rv.wait("ve", sc_s, m_smask_rd[qh])
                                        m_msk = rv.op("ve", ve_s, lambda e, pp=pp, qh=qh, mi=mi: e.tensor_tensor(
                                            smask[:, qh * 512:(qh + 1) * 512],
                                            psS[pp][:, qh * 512:(qh + 1) * 512],
                                            masks[:, mi * 512:(mi + 1) * 512], OP.add))
                                        m_psS_ve[pp] = m_msk
                                    for qh in range(2):
                                        if segs[qh] == "c":
                                            pev, vev = m_pt_cons[pp]
                                            rv.wait("ve", pe_s, pev)
                                            m_msk = rv.op("ve", ve_s, lambda e, pp=pp, qh=qh: e.memset(
                                                ptb[:, pp * 1024 + qh * 512: pp * 1024 + (qh + 1) * 512], 0.0))
                                pev, vev = m_pt_cons[pp]
                                rv.wait("sc", pe_s, max(m_smm, pev))
                                rv.wait("sc", ve_s, max(m_msk, vev))
                                if not diag:
                                    m_exp = rv.op("sc", sc_s, lambda e, pp=pp, nbias=nbias: e.activation(
                                        ptb[:, pp * 1024:(pp + 1) * 1024], psS[pp][:, :],
                                        AF.Exp, bias=nbias))
                                    m_exp_prev[pp] = m_exp
                                else:
                                    m_exp = 0
                                    for qh in range(2):
                                        if segs[qh] == "c":
                                            continue
                                        if segs[qh] == "p":
                                            m_exp = rv.op("sc", sc_s, lambda e, pp=pp, qh=qh, nbias=nbias: e.activation(
                                                ptb[:, pp * 1024 + qh * 512: pp * 1024 + (qh + 1) * 512],
                                                smask[:, qh * 512:(qh + 1) * 512], AF.Exp, bias=nbias))
                                            m_smask_rd[qh] = m_exp
                                        else:
                                            m_exp = rv.op("sc", sc_s, lambda e, pp=pp, qh=qh, nbias=nbias: e.activation(
                                                ptb[:, pp * 1024 + qh * 512: pp * 1024 + (qh + 1) * 512],
                                                psS[pp][:, qh * 512:(qh + 1) * 512], AF.Exp, bias=nbias))
                                            m_exp_prev[pp] = m_exp
                                spp, dpp = sss % 2, (sss + 1) % 2
                                rv.wait("ve", sc_s, m_exp if m_exp else sc_s.n)
                                if u == 0 and sss == 0:
                                    rv.wait("ve", ve_s, m_ssz)
                                rv.opn("ve", lambda e, spp=spp, dpp=dpp, pp=pp, qc=qc: e.tensor_tensor(
                                    ssum[:, dpp * 2048 + qc * 1024: dpp * 2048 + (qc + 1) * 1024],
                                    ssum[:, spp * 2048 + qc * 1024: spp * 2048 + (qc + 1) * 1024],
                                    ptb[:, pp * 1024:(pp + 1) * 1024], OP.add))
                                m_ssa = rv.op("ve", ve_s, lambda e: e.nop())
                                m_pt_cons[pp] = (m_pt_cons[pp][0], m_ssa)
                                pending.append((sss, pp, segs, m_exp))
                            while pending:
                                emit_av()
                            kv_pe_done[g_seq] = pe_s.n
                            g_seq += 1
                        for j4 in range(4):
                            if j4 == 3:
                                rv.op("pe", pe_s, lambda e, j4=j4: e.matmul(
                                    psUa[:, j4 * 512:(j4 + 1) * 512], zero_bf[:, 0:128], zero_bf[:, :],
                                    start=False, stop=True))
                            else:
                                rv.opn("pe", lambda e, j4=j4: e.matmul(
                                    psUa[:, j4 * 512:(j4 + 1) * 512], zero_bf[:, 0:128], zero_bf[:, :],
                                    start=False, stop=True))
                        m_last_exp = sc_s.n
                        m_last_ssa = ve_s.n
                        m_last_av = pe_s.n
                        for qc in range(2):
                            ppf = qc
                            rv.wait("pe", sc_s, m_last_exp)
                            rv.wait("pe", ve_s, max(m_last_ssa, m_psS_ve[ppf]))
                            for qh in range(2):
                                rv.op("pe", pe_s, lambda e, qc=qc, qh=qh, ppf=ppf: e.matmul(
                                    psS[ppf][0:1, qh * 512:(qh + 1) * 512], ones_col[:, :],
                                    ssum[:, qc * 1024 + qh * 512: qc * 1024 + (qh + 1) * 512],
                                    start=True, stop=True))
                            m_sr = pe_s.n
                            rv.wait("ve", pe_s, m_sr)
                            m_rr = rv.op("ve", ve_s, lambda e, ppf=ppf: e.reciprocal(
                                rrow[0:1, :], psS[ppf][0:1, :]))
                            rv.wait("pe", ve_s, m_rr)
                            for qh in range(2):
                                rv.op("pe", pe_s, lambda e, qh=qh, ppf=ppf: e.matmul(
                                    psS[ppf][:, qh * 512:(qh + 1) * 512], ones_row[:, :],
                                    rrow[0:1, qh * 512:(qh + 1) * 512], start=True, stop=True))
                            m_rp = pe_s.n
                            rv.wait("ve", pe_s, m_rp)
                            m_rrep = rv.op("ve", ve_s, lambda e, ppf=ppf: e.tensor_copy(
                                rrep[:, :], psS[ppf][:, :]))
                            rv.wait("ve", pe_s, m_last_av)
                            rv.op("ve", ve_s, lambda e, h_i=h_i, qc=qc: e.tensor_tensor(
                                QO[:, h_i * T + qc * 1024: h_i * T + (qc + 1) * 1024],
                                psUa[:, qc * 1024:(qc + 1) * 1024], rrep[:, :], OP.mult))

                all_sems = [d_in, d_kv[0], d_kv[1], d_sp, c_k, c_v, c_nn, pe_s, ve_s, sc_s]
                snap0 = {id(sm): sm.n for sm in all_sems}
                variants = []
                final0 = None
                for v in range(4):
                    for sm in all_sems:
                        sm.n = snap0[id(sm)]
                    rvv = Rec()
                    emit_attn_variant(rvv, v)
                    fin = {id(sm): sm.n for sm in all_sems}
                    if final0 is None:
                        final0 = fin
                    else:
                        assert fin == final0, "attention variants diverge in sem counts"
                    variants.append(rvv)
                for sm in all_sems:
                    sm.n = final0[id(sm)]
                eng_regs = {"sy": rg_sy[0], "pe": rg_pe, "ve": rg_ve, "sc": rg_sc}
                for eng in ("sy", "pe", "ve", "sc"):
                    bodies = [variants[v].ops[eng] for v in range(4)]
                    r.wait(eng, d_in, m_consts)
                    def sw(e, reg=eng_regs[eng], bodies=bodies):
                        e.reg_load(reg, cfgs[0:1, 40:41])
                        idx = e.snap(reg, min_val=0, max_val=3)
                        for jv in nc.Switch(engines=[e], index=[idx], n=4):
                            for fn in bodies[jv]:
                                fn(e)
                    r.raw(eng, sw)
            m_attn_ve = ve_s.n
            if "qo" in tap_aps:
                r.wait("gp", ve_s, m_attn_ve)
                r.dma("gp", d_sp, tap_aps["qo"], QO[:, :])
        if "qo" in tap_aps and phases < 6:
            r.wait("gp", sc_s, sc_s.n)
            r.dma("gp", d_sp, tap_aps["qo"], QO[:, :])
        if "gk" in tap_aps and phases >= 3:
            r.wait("gp", c_k, 1)
            r.dma("gp", d_sp, tap_aps["gk"], dr_gk)
        if "gv" in tap_aps and phases >= 3:
            r.wait("gp", c_v, 1)
            r.dma("gp", d_sp, tap_aps["gv"], dr_gv)

        if phases >= 7:
            phase_barrier()

        if phases >= 7:
            with ExitStack() as _ph:
                dATst = _ph.enter_context(nc.sbuf_tensor("dATst", [16, II], F32))
                ter2P = [_ph.enter_context(nc.sbuf_tensor(f"ter2_{j}", [128, T], BF16)) for j in range(3)]
                dBTP = [_ph.enter_context(nc.sbuf_tensor(f"dBT_{j}", [128, H], BF16)) for j in range(3)]
                dATbf = _ph.enter_context(nc.sbuf_tensor("dATbf", [16, II], BF16))
                dAT = _ph.enter_context(nc.sbuf_tensor("dAT", [128, 128], BF16))
                dBst = _ph.enter_context(nc.sbuf_tensor("dBst", [128, 256], F32))
                dBbf = _ph.enter_context(nc.sbuf_tensor("dBbf", [128, 256], BF16))
                wost = _ph.enter_context(nc.sbuf_tensor("wost", [128, 512], F32))
                wobf = _ph.enter_context(nc.sbuf_tensor("wobf", [128, 2 * 16 * 512], BF16))
                dnst = _ph.enter_context(nc.sbuf_tensor("dnst", [128, 4096], F32))
                dnbf = _ph.enter_context(nc.sbuf_tensor("dnbf", [128, 2 * 4096], BF16))
                hbb = _ph.enter_context(nc.sbuf_tensor("hbb", [128, 2 * 8192], BF16))
                xres = _ph.enter_context(nc.sbuf_tensor("xres", [128, 2 * 512], F32))
                osb = _ph.enter_context(nc.sbuf_tensor("osb", [128, 2 * 512], F32))
                ps7_es = ExitStack()
                psT7 = ps7_es.enter_context(nc.psum_tensor("psT7", [128, 512], F32))
                m_dbt = 0
                m_bc7 = 0
                m_dbbf_rd = 0
                for e_i in range(E):
                    r.wait("sy", ve_s, m_bc7)
                    r.wait("sy", pe_s, m_dbbf_rd)
                    m_b = r.dma("sy", d_b7, dBst.ap().rearrange("p (c r) -> p c r", r=16),
                                dr_downB[e_i, :, :].rearrange("(c p) r -> p c r", p=128))
                    r.wait("ve", d_b7, m_b)
                    r.wait("ve", pe_s, m_dbbf_rd)
                    m_bc = r.op("ve", ve_s, lambda e: e.tensor_copy(dBbf[:, :], dBst[:, :]))
                    m_bc7 = m_bc
                    last_ev = 0
                    for k in range(16):
                        r.wait("pe", ve_s, max(m_bc, last_ev))
                        r.op("pe", pe_s, lambda e, k=k: e.matmul(
                            psT7[0:16, 0:128], dBbf[:, k * 16:(k + 1) * 16],
                            idbf[:, :], start=True, stop=True))
                        m_t = pe_s.n
                        r.wait("ve", pe_s, m_t)
                        last_ev = r.op("ve", ve_s, lambda e, e_i=e_i, k=k: e.tensor_copy(
                            dBTP[e_i // 3][(e_i % 3) * 32:(e_i % 3) * 32 + 16, k * 128:(k + 1) * 128],
                            psT7[0:16, 0:128]))
                    m_dbt = last_ev
                    m_dbbf_rd = pe_s.n
                free_hb = [0, 0]
                prev_t7_sc = 0
                m_ac7 = 0
                m_datbf_rd = 0
                for e_i in range(E):
                    r.wait("sy", ve_s, m_ac7)
                    r.wait("sy", pe_s, m_datbf_rd)
                    m_a = r.dma("sy", d_a7, dATst[:, :], dr_downA[e_i, :, :])
                    r.wait("ve", d_a7, m_a)
                    r.wait("ve", pe_s, m_datbf_rd)
                    m_ac = r.op("ve", ve_s, lambda e: e.tensor_copy(dATbf[:, :], dATst[:, :]))
                    m_ac7 = m_ac
                    last_ev = 0
                    for k in range(8):
                        r.wait("pe", ve_s, max(m_ac, last_ev, m_dbt))
                        r.wait("pe", sc_s, prev_t7_sc)
                        r.op("pe", pe_s, lambda e, k=k: e.matmul(
                            psT7[:, 0:16], dATbf[0:16, k * 128:(k + 1) * 128],
                            idbf[0:16, 0:16], start=True, stop=True))
                        m_t = pe_s.n
                        r.wait("ve", pe_s, m_t)
                        last_ev = r.op("ve", ve_s, lambda e, k=k: e.tensor_copy(
                            dAT[:, k * 16:(k + 1) * 16], psT7[:, 0:16]))
                    m_dat = last_ev
                    m_datbf_rd = pe_s.n
                    for t in range(4):
                        hp = t % 2
                        r.wait("sy", pe_s, free_hb[hp])
                        r.wait("sy", d_h[0], m_gup_done[0])
                        r.wait("sy", d_h[1], m_gup_done[1])
                        m_hb = r.dma("sy", d_hbs[hp],
                                     hbb[:, hp * 8192: hp * 8192 + 4096].rearrange("p (ic t) -> p ic t", t=512),
                                     dr_h[e_i, :, t * 512:(t + 1) * 512].rearrange("(ic p) t -> p ic t", p=128))
                        r.wait("pe", d_hbs[hp], m_hb)
                        r.wait("pe", ve_s, m_dat)
                        r.wait("pe", sc_s, prev_t7_sc)
                        for k in range(8):
                            r.op("pe", pe_s, lambda e, k=k, hp=hp: e.matmul(
                                psT7[0:16, 0:512], dAT[:, k * 16:(k + 1) * 16],
                                hbb[:, hp * 8192 + k * 512: hp * 8192 + (k + 1) * 512],
                                start=(k == 0), stop=(k == 7)))
                        m_tm = pe_s.n
                        free_hb[hp] = m_tm
                        r.wait("sc", pe_s, m_tm)
                        prev_t7_sc = r.op("sc", sc_s, lambda e, e_i=e_i, t=t: e.activation(
                            ter2P[e_i // 3][(e_i % 3) * 32:(e_i % 3) * 32 + 16, t * 512:(t + 1) * 512],
                            psT7[0:16, 0:512], AF.Copy, scale=SCALING))
                m_ter2_sc = prev_t7_sc
                ps7_es.close()
                psF = _ph.enter_context(nc.psum_tensor("psF", [128, 8 * 512], F32))
                free_hbb_init = pe_s.n
                wo3 = dr_wout.rearrange("(kc p) f -> p kc f", p=128)
                dn3 = dr_down.rearrange("e (ic p) f -> e p ic f", p=128)
                free_bank = [0] * 8
                free_wo = 0
                free_dn = [0, 0]
                free_hbb = [free_hbb_init, free_hbb_init]
                free_xres = [0, 0]
                free_osb = [0, 0]
                free_wo_par = [0, 0]
                m_dnc_last = 0
                m_woc = 0
                for f in range(4):
                    fpar = f % 2
                    r.wait("sy", pe_s, free_wo_par[fpar])
                    for k in range(16):
                        wp = k % 2
                        r.wait("sy", ve_s, m_woc)
                        m_wd = r.dma("sy", d_wo[wp], wost[:, :],
                                     wo3[:, k, f * 512:(f + 1) * 512])
                        r.wait("ve", d_wo[wp], m_wd)
                        m_woc = r.op("ve", ve_s, lambda e, k=k, fpar=fpar: e.tensor_copy(
                            wobf[:, fpar * 8192 + k * 512: fpar * 8192 + (k + 1) * 512],
                            wost[:, :]))
                    m_wo_ready = m_woc
                    for th in range(2):
                        for e_i in range(E):
                            ep = e_i % 2
                            r.wait("sy", pe_s, free_dn[ep])
                            r.wait("sy", ve_s, m_dnc_last)
                            m_dd = r.dma("sy", d_dn[ep],
                                         dnst.ap().rearrange("p (k c) -> p k c", c=512),
                                         dn3[e_i, :, :, f * 512:(f + 1) * 512])
                            r.wait("ve", d_dn[ep], m_dd)
                            m_dnc = r.op("ve", ve_s, lambda e, ep=ep: e.tensor_copy(
                                dnbf[:, ep * 4096:(ep + 1) * 4096], dnst[:, :]))
                            m_dnc_last = m_dnc
                            r.wait("gp", pe_s, free_hbb[ep])
                            r.wait("gp", d_h[0], m_gup_done[0])
                            r.wait("gp", d_h[1], m_gup_done[1])
                            m_hh = r.dma("gp", d_hb[ep],
                                         hbb[:, ep * 8192:(ep + 1) * 8192].rearrange("p (ic t) -> p ic t", t=1024),
                                         dr_h[e_i, :, th * 1024:(th + 1) * 1024].rearrange("(ic p) t -> p ic t", p=128))
                            for t in range(8):
                                tt = th * 8 + t
                                if e_i == 0:
                                    r.wait("pe", ve_s, max(m_wo_ready, free_bank[t], m_attn_ve))
                                    r.wait("pe", sc_s, m_ter2_sc)
                                    for k in range(16):
                                        r.opn("pe", lambda e, k=k, tt=tt, t=t, fpar=fpar: e.matmul(
                                            psF[:, t * 512:(t + 1) * 512],
                                            QO[:, k * T + tt * 128: k * T + (tt + 1) * 128],
                                            wobf[:, fpar * 8192 + k * 512: fpar * 8192 + (k + 1) * 512],
                                            start=(k == 0), stop=False))
                                r.wait("pe", ve_s, m_dnc)
                                r.wait("pe", d_hb[ep], m_hh)
                                r.wait("pe", sc_s, m_ter2_sc)
                                for k in range(8):
                                    r.opn("pe", lambda e, k=k, t=t, ep=ep: e.matmul(
                                        psF[:, t * 512:(t + 1) * 512],
                                        hbb[:, ep * 8192 + k * 1024 + t * 128: ep * 8192 + k * 1024 + (t + 1) * 128],
                                        dnbf[:, ep * 4096 + k * 512: ep * 4096 + (k + 1) * 512],
                                        start=False, stop=False))
                                r.op("pe", pe_s, lambda e, e_i=e_i, tt=tt, t=t, f=f: e.matmul(
                                    psF[:, t * 512:(t + 1) * 512],
                                    ter2P[e_i // 3][(e_i % 3) * 32:(e_i % 3) * 32 + 16, tt * 128:(tt + 1) * 128],
                                    dBTP[e_i // 3][(e_i % 3) * 32:(e_i % 3) * 32 + 16, f * 512:(f + 1) * 512],
                                    start=False, stop=(e_i == E - 1)))
                                m_pk = pe_s.n
                                if e_i == E - 1:
                                    xp = t % 2
                                    r.wait("gp", ve_s, free_xres[xp])
                                    m_xr = r.dma("gp", d_xr[xp], xres[:, xp * 512:(xp + 1) * 512],
                                                 dr_x[tt * 128:(tt + 1) * 128, f * 512:(f + 1) * 512])
                                    r.wait("ve", pe_s, m_pk)
                                    r.wait("ve", d_xr[xp], m_xr)
                                    r.wait("ve", d_ot[xp], free_osb[xp])
                                    m_ov = r.op("ve", ve_s, lambda e, t=t, xp=xp: e.tensor_tensor(
                                        osb[:, xp * 512:(xp + 1) * 512], psF[:, t * 512:(t + 1) * 512],
                                        xres[:, xp * 512:(xp + 1) * 512], OP.add))
                                    free_bank[t] = m_ov
                                    free_xres[xp] = m_ov
                                    r.wait("gp", ve_s, m_ov)
                                    m_os = r.dma("gp", d_ot[xp],
                                                 dr_out[tt * 128:(tt + 1) * 128, f * 512:(f + 1) * 512],
                                                 osb[:, xp * 512:(xp + 1) * 512])
                                    free_osb[xp] = m_os
                            free_dn[ep] = pe_s.n
                            free_hbb[ep] = pe_s.n
                        free_wo_par[fpar] = pe_s.n
            r.wait("gp", d_ot[0], d_ot[0].n)
            r.wait("gp", d_ot[1], d_ot[1].n)
        else:
            r.wait("gp", d_sp, d_sp.n)

        r.emit(block)
    return nc



def _host_consts(core):
    cl = core % 4
    idbf = np.eye(128, dtype=ml_dtypes.bfloat16)
    idf = np.eye(128, dtype=np.float32)
    mask = np.zeros((128, 4 * 512), np.float32)
    for mi in range(4):
        o = mi * 128
        p = np.arange(128)[:, None]
        q = np.arange(512)[None, :]
        mask[:, mi * 512:(mi + 1) * 512] = np.where(q >= p + o, 0.0, NEG)
    cfg = np.zeros((1, 64), np.int32)
    chA, chB = cl, 7 - cl
    units = [(0, chA), (1, chB)]
    for j in range(chA):
        units.append((0, j))
    for j in range(chB):
        units.append((1, j))
    assert len(units) == 9
    for u, (qc, j) in enumerate(units):
        rk = j if j < 4 else 7 - j
        lci = 0 if j < 4 else 1
        cfg[0, u * 4 + 0] = qc
        cfg[0, u * 4 + 1] = j
    cfg[0, 40] = cl
    return {"c_idbf": idbf, "c_idf": idf, "c_mask": mask, "c_cfg": cfg}


def _shard_x(x, core):
    cl = core % 4
    b = core // 4
    chA, chB = cl, 7 - cl
    return np.ascontiguousarray(np.concatenate(
        [x[b, chA * 1024:(chA + 1) * 1024], x[b, chB * 1024:(chB + 1) * 1024]], axis=0))


def kernel(L0_attn_x, L0_attn_w_qkv, L0_attn_w_out, L0_ffn_gate_weight,
           L0_ffn_gate_lora_A, L0_ffn_gate_lora_B, L0_ffn_gate_up_proj,
           L0_ffn_down_proj, L0_ffn_gup_lora_A, L0_ffn_gup_lora_B,
           L0_ffn_down_lora_A, L0_ffn_down_lora_B):
    x = np.asarray(L0_attn_x, np.float32)
    weights = {
        "wqkv": np.ascontiguousarray(np.asarray(L0_attn_w_qkv, np.float32)),
        "wout": np.ascontiguousarray(np.asarray(L0_attn_w_out, np.float32)),
        "gatew": np.ascontiguousarray(np.asarray(L0_ffn_gate_weight, np.float32)),
        "gateA": np.ascontiguousarray(np.asarray(L0_ffn_gate_lora_A, np.float32)),
        "gateB": np.ascontiguousarray(np.asarray(L0_ffn_gate_lora_B, np.float32)),
        "gup": np.ascontiguousarray(np.asarray(L0_ffn_gate_up_proj, np.float32)),
        "down": np.ascontiguousarray(np.asarray(L0_ffn_down_proj, np.float32)),
        "gupA": np.ascontiguousarray(np.asarray(L0_ffn_gup_lora_A, np.float32)),
        "gupB": np.ascontiguousarray(np.asarray(L0_ffn_gup_lora_B, np.float32)),
        "downA": np.ascontiguousarray(np.asarray(L0_ffn_down_lora_A, np.float32)),
        "downB": np.ascontiguousarray(np.asarray(L0_ffn_down_lora_B, np.float32)),
    }
    nc = build(taps=TAPS, phases=PHASES)
    in_maps = []
    for c in range(8):
        m = dict(weights)
        m["x"] = _shard_x(x, c)
        m.update(_host_consts(c))
        in_maps.append(m)
    res = run_bass_kernel_spmd(nc, in_maps, core_ids=list(range(8)))
    out = np.zeros((2, 8192, H), np.float32)
    for c in range(8):
        cl, b = c % 4, c // 4
        chA, chB = cl, 7 - cl
        o = res.results[c]["out"]
        out[b, chA * 1024:(chA + 1) * 1024] = o[0:1024]
        out[b, chB * 1024:(chB + 1) * 1024] = o[1024:2048]
    return out


# version 04
# speedup vs baseline: 1.1263x; 1.1263x over previous
import numpy as np
import ml_dtypes
from contextlib import ExitStack

import concourse.bass as bass
import concourse.mybir as mybir
from concourse.bass_utils import run_bass_kernel_spmd

F32 = mybir.dt.float32
BF16 = mybir.dt.bfloat16
I32 = mybir.dt.int32
AF = mybir.ActivationFunctionType
OP = mybir.AluOpType
AX = mybir.AxisListType.X
ds = bass.ds

H = 2048
NHEAD = 16
DH = 128
T = 2048
E = 8
II = 1024
R = 16
SCALING = 2.0
NEG = -1e9
EPS = 1e-5
ISQD = float(1.0 / np.sqrt(DH))
GROUPS = [[0, 1, 2, 3], [4, 5, 6, 7]]

PHASES = 7
TAPS = ()


def _diag_status(s, qh):
    if s * 128 + 127 <= qh * 512:
        return "o"
    if s * 128 > qh * 512 + 511:
        return "c"
    return "p"


class Sem:
    def __init__(self, nc, es, name):
        self.h = es.enter_context(nc.semaphore(name))
        self.n = 0


class Rec:

    def __init__(self):
        self.ops = {k: [] for k in ("sy", "pe", "ve", "sc", "gp")}

    def raw(self, eng, fn):
        self.ops[eng].append(fn)

    def wait(self, eng, sem, val):
        if val > 0:
            self.ops[eng].append(lambda e, s=sem, v=val: e.wait_ge(s.h, v))

    def op(self, eng, sem, fn):
        def g(e, fn=fn, sem=sem):
            fn(e).then_inc(sem.h, 1)
        self.ops[eng].append(g)
        sem.n += 1
        return sem.n

    def opn(self, eng, fn):
        self.ops[eng].append(fn)

    def barrier(self, bar, drains):
        for eng, sems in drains.items():
            for sm in sems:
                self.wait(eng, sm, sm.n)
        for eng in ("sy", "pe", "ve", "sc", "gp"):
            def g(e, bar=bar):
                e.nop().then_inc(bar.h, 1)
            self.ops[eng].append(g)
            bar.n += 1
        tgt = bar.n
        for eng in ("sy", "pe", "ve", "sc", "gp"):
            self.wait(eng, bar, tgt)

    def vsync(self, eng, sem):
        def g(e, sem=sem):
            e.nop().then_inc(sem.h, 1)
        self.ops[eng].append(g)
        sem.n += 1
        self.wait(eng, sem, sem.n)

    def dma(self, eng, sem, out, in_, **kw):
        def g(e, out=out, in_=in_, kw=kw, sem=sem):
            e.dma_start(out=out, in_=in_, **kw).then_inc(sem.h, 16)
        self.ops[eng].append(g)
        sem.n += 16
        return sem.n

    def emit(self, block):
        ops = self.ops

        @block.sync
        def _(e):
            for fn in ops["sy"]:
                fn(e)

        @block.tensor
        def _(e):
            for fn in ops["pe"]:
                fn(e)

        @block.vector
        def _(e):
            for fn in ops["ve"]:
                fn(e)

        @block.scalar
        def _(e):
            for fn in ops["sc"]:
                fn(e)

        @block.gpsimd
        def _(e):
            for fn in ops["gp"]:
                fn(e)


def build(taps=(), phases=7):
    nc = bass.Bass("TRN2", target_bir_lowering=False, debug=False, detect_race_conditions=False)
    es = ExitStack()

    dr_x = nc.dram_tensor("x", [T, H], F32, kind="ExternalInput").ap()
    dr_wqkv = nc.dram_tensor("wqkv", [H, 3 * H], F32, kind="ExternalInput").ap()
    dr_wout = nc.dram_tensor("wout", [H, H], F32, kind="ExternalInput").ap()
    dr_gw = nc.dram_tensor("gatew", [E, H], F32, kind="ExternalInput").ap()
    dr_gA = nc.dram_tensor("gateA", [R, H], F32, kind="ExternalInput").ap()
    dr_gB = nc.dram_tensor("gateB", [E, R], F32, kind="ExternalInput").ap()
    dr_gup = nc.dram_tensor("gup", [E, H, 2 * II], F32, kind="ExternalInput").ap()
    dr_down = nc.dram_tensor("down", [E, II, H], F32, kind="ExternalInput").ap()
    dr_gupA = nc.dram_tensor("gupA", [E, R, H], F32, kind="ExternalInput").ap()
    dr_gupB = nc.dram_tensor("gupB", [E, 2 * II, R], F32, kind="ExternalInput").ap()
    dr_downA = nc.dram_tensor("downA", [E, R, II], F32, kind="ExternalInput").ap()
    dr_downB = nc.dram_tensor("downB", [E, H, R], F32, kind="ExternalInput").ap()
    dr_idbf = nc.dram_tensor("c_idbf", [128, 128], BF16, kind="ExternalInput").ap()
    dr_idf = nc.dram_tensor("c_idf", [128, 128], F32, kind="ExternalInput").ap()
    dr_mask = nc.dram_tensor("c_mask", [128, 4 * 512], F32, kind="ExternalInput").ap()
    dr_cfg = nc.dram_tensor("c_cfg", [1, 64], I32, kind="ExternalInput").ap()
    dr_out = nc.dram_tensor("out", [T, H], F32, kind="ExternalOutput").ap()

    dr_ck = nc.dram_tensor("contribK", [H, T], BF16, kind="Internal").ap()
    dr_cv = nc.dram_tensor("contribV", [T, H], BF16, kind="Internal").ap()
    dr_gk = nc.dram_tensor("gatheredK", [4 * H, T], BF16, kind="Internal").ap()
    dr_gv = nc.dram_tensor("gatheredV", [4 * T, H], BF16, kind="Internal").ap()
    dr_h = nc.dram_tensor("hspill", [E, II, T], BF16, kind="Internal").ap()
    dr_wr = nc.dram_tensor("wrout", [E, T], F32, kind="Internal").ap()
    dr_kni = nc.dram_tensor("kn_in", [1, NHEAD], F32, kind="Internal").ap()
    dr_kno = nc.dram_tensor("kn_out", [1, NHEAD], F32, kind="Internal").ap()

    tap_aps = {}
    if taps:
        tshapes = {"xn": ([128, 16 * T], BF16), "qo": ([128, NHEAD * T], BF16),
                   "gk": ([4 * H, T], BF16), "gv": ([4 * T, H], BF16),
                   "h": ([E, II, T], BF16), "w": ([E, T], F32),
                   "negc": ([128, 32], F32), "lnst": ([128, 16], F32)}
        for t in taps:
            sh, dt = tshapes[t]
            tap_aps[t] = nc.dram_tensor("tap_" + t, sh, dt, kind="ExternalOutput").ap()

    m_xn_done = 0
    m_qkv_pe = 0
    m_qkv_ve = 0
    m_wsp = 0
    m_gup_done = (0, 0)
    m_attn_ve = 0

    with es:
        sbt = lambda name, shape, dtype: es.enter_context(nc.sbuf_tensor(name, shape, dtype))
        xn_es = ExitStack()
        xn = xn_es.enter_context(nc.sbuf_tensor("xn", [128, 16 * T], BF16, side="right"))
        QO = sbt("QO", [128, NHEAD * T], BF16)
        idbf = sbt("idbf", [128, 128], BF16)
        idf = sbt("idf", [128, 128], F32)
        cfgs = sbt("cfgs", [1, 64], I32)
        ones_col = sbt("ones_col", [128, 1], F32)
        ones_col_bf = sbt("ones_col_bf", [128, 1], BF16)
        ones_row = sbt("ones_row", [1, 128], F32)
        zero_bf = sbt("zero_bf", [128, 512], BF16)
        eps_t = sbt("eps_t", [128, 1], F32)
        qn2 = sbt("qn2", [1, 32], F32)
        kn2 = sbt("kn2", [1, NHEAD], F32)
        negc_rep = sbt("negc_rep", [128, 32], F32)
        w45_es = ExitStack()
        wT8 = w45_es.enter_context(nc.sbuf_tensor("wT8", [E, T], F32, side="right"))

        S = lambda name: Sem(nc, es, name)
        d_in = S("d_in")
        d_x = [S("d_x0"), S("d_x1")]
        d_w2 = [S("d_w20"), S("d_w21")]
        d_kst = [S("d_kst0"), S("d_kst1")]
        d_vtt = [S("d_vtt0"), S("d_vtt1")]
        d_g4 = S("d_g4")
        d_a5 = S("d_a5")
        d_b5 = S("d_b5")
        d_w1 = S("d_w1")
        d_w5 = [S("d_w50"), S("d_w51")]
        d_h = [S("d_h0"), S("d_h1")]
        d_kv = [S("d_kv0"), S("d_kv1")]
        d_kn6 = S("d_kn6")
        d_b7 = S("d_b7")
        d_a7 = S("d_a7")
        d_hbs = [S("d_hbs0"), S("d_hbs1")]
        d_wo = [S("d_wo0"), S("d_wo1")]
        d_dn = [S("d_dn0"), S("d_dn1")]
        d_hb = [S("d_hb0"), S("d_hb1")]
        d_xr = [S("d_xr0"), S("d_xr1")]
        d_ot = [S("d_ot0"), S("d_ot1")]
        d_sp = S("d_sp")
        c_k = S("c_k")
        c_v = S("c_v")
        c_nn = S("c_nn")
        pe_s = S("pe_s")
        ve_s = S("ve_s")
        sc_s = S("sc_s")
        gp_s = S("gp_s")
        bar = S("bar")

        def phase_barrier():
            r.barrier(bar, {
                "sy": [d_w2[0], d_w2[1], d_kv[0], d_kv[1], d_g4, d_a5, d_b5,
                       d_w1, d_w5[0], d_w5[1], d_hbs[0], d_hbs[1], d_wo[0],
                       d_wo[1], d_dn[0], d_dn[1], d_kn6, d_b7, d_a7],
                "gp": [d_in, d_x[0], d_x[1], d_kst[0], d_kst[1], d_vtt[0],
                       d_vtt[1], d_sp, d_h[0], d_h[1], d_hb[0], d_hb[1],
                       d_xr[0], d_xr[1], d_ot[0], d_ot[1]],
            })

        rg_sy = [es.enter_context(nc.sync.register(f"rs{i}")) for i in range(3)]
        rg_pe = es.enter_context(nc.tensor.register("rt0"))
        rg_ve = es.enter_context(nc.vector.register("rv0"))
        rg_sc = es.enter_context(nc.scalar.register("rc0"))
        rg_gp = es.enter_context(nc.gpsimd.register("rg0"))

        block = es.enter_context(nc.Block())
        r = Rec()

        r.dma("gp", d_in, idbf[:, :], dr_idbf)
        r.dma("gp", d_in, idf[:, :], dr_idf)
        m_consts = r.dma("gp", d_in, cfgs[:, :], dr_cfg)
        r.opn("ve", lambda e: e.memset(ones_col[:, :], 1.0))
        r.opn("ve", lambda e: e.memset(ones_col_bf[:, :], 1.0))
        r.opn("ve", lambda e: e.memset(ones_row[:, :], 1.0))
        r.opn("ve", lambda e: e.memset(zero_bf[:, :], 0.0))
        r.opn("ve", lambda e: e.memset(eps_t[:, :], EPS))
        r.opn("ve", lambda e: e.memset(qn2[:, :], 0.0))
        m_vec_consts = r.op("ve", ve_s, lambda e: e.memset(kn2[:, :], 0.0))

        with ExitStack() as _ph:
            xbuf = _ph.enter_context(nc.sbuf_tensor("xbuf", [128, 2 * H], F32))
            junk = _ph.enter_context(nc.sbuf_tensor("junk", [128, H], BF16))
            lnst = _ph.enter_context(nc.sbuf_tensor("lnst", [128, 16], F32))
            xntm = _ph.enter_context(nc.sbuf_tensor("xntm", [128, 2 * H], BF16))
            lntmp = _ph.enter_context(nc.sbuf_tensor("lntmp", [128, H], F32))
            lnac = _ph.enter_context(nc.sbuf_tensor("lnac", [128, 1], F32))
            psLN = _ph.enter_context(nc.psum_tensor("psLN", [128, 1024], F32))
            r.opn("ve", lambda e: e.memset(lnst[:, :], 0.0))
            free_xb = [0, 0]
            free_half = [0, 0]
            prev_var = 0
            xn3 = xn.ap().rearrange("p (k t) -> p k t", t=T)
            ps3 = psLN.ap().rearrange("p (g c) -> p g c", c=128)
            for t in range(16):
                par = t % 2
                xb = xbuf[:, par * H:(par + 1) * H]
                xt = xntm[:, par * H:(par + 1) * H]
                r.wait("gp", ve_s, free_xb[par])
                m_x = r.dma("gp", d_x[par], xb, dr_x[t * 128:(t + 1) * 128, :])
                r.wait("ve", d_x[par], m_x)
                m_s1 = r.op("ve", ve_s, lambda e, xb=xb: e.reduce_sum(lnst[:, 0:1], xb, axis=AX))
                r.wait("sc", d_x[par], m_x)
                r.wait("sc", ve_s, m_s1)
                r.opn("sc", lambda e: e.activation(lnst[:, 1:2], lnst[:, 0:1], AF.Copy, scale=1.0 / H))
                m_sq = r.op("sc", sc_s, lambda e, xb=xb: e.activation(
                    junk[:, :], xb, AF.Square, accum_out=lnac[:, 0:1]))
                r.wait("ve", sc_s, m_sq)
                r.vsync("ve", ve_s)
                r.opn("ve", lambda e: e.tensor_scalar(lnst[:, 9:10], lnst[:, 1:2], lnst[:, 0:1], None, OP.mult))
                r.vsync("ve", ve_s)
                m_var = r.op("ve", ve_s, lambda e: e.tensor_scalar(
                    lnst[:, 5:6], lnac[:, 0:1], lnst[:, 9:10], None, OP.subtract))
                prev_var = m_var
                r.wait("sc", ve_s, m_var)
                m_sd = r.op("sc", sc_s, lambda e: e.activation(
                    lnst[:, 6:7], lnst[:, 5:6], AF.Sqrt, scale=1.0 / H, bias=eps_t[:, 0:1]))
                r.wait("ve", sc_s, m_sd)
                r.opn("ve", lambda e: e.reciprocal(lnst[:, 7:8], lnst[:, 6:7]))
                r.opn("ve", lambda e: e.tensor_scalar(lnst[:, 8:9], lnst[:, 1:2], -1.0, None, OP.mult))
                r.vsync("ve", ve_s)
                r.opn("ve", lambda e, xb=xb: e.tensor_scalar(
                    lntmp[:, :], xb, lnst[:, 8:9], None, OP.add))
                m_xnt = r.op("ve", ve_s, lambda e, xt=xt: e.tensor_scalar(
                    xt, lntmp[:, :], lnst[:, 7:8], None, OP.mult))
                free_xb[par] = m_xnt
                for g in range(4):
                    half = g % 2
                    r.wait("pe", ve_s, max(m_xnt, free_half[half], m_vec_consts))
                    r.wait("pe", d_in, m_consts)
                    for kk in range(4):
                        k = g * 4 + kk
                        r.op("pe", pe_s, lambda e, xt=xt, k=k, half=half, kk=kk: e.matmul(
                            psLN[:, half * 512 + kk * 128: half * 512 + (kk + 1) * 128],
                            xt[:, k * 128:(k + 1) * 128], idbf[:, :],
                            start=True, stop=True))
                    m_tr = pe_s.n
                    r.wait("ve", pe_s, m_tr)
                    m_ev = r.op("ve", ve_s, lambda e, g=g, t=t, half=half: e.tensor_copy(
                        xn3[:, g * 4:(g + 1) * 4, t * 128:(t + 1) * 128],
                        ps3[:, half * 4:(half + 1) * 4, :]))
                    free_half[half] = m_ev
            m_xn_done = ve_s.n
            if "lnst" in tap_aps:
                r.wait("gp", ve_s, m_xn_done)
                r.dma("gp", d_sp, tap_aps["lnst"], lnst[:, :])

        if "xn" in tap_aps:
            r.wait("gp", ve_s, m_xn_done)
            r.dma("gp", d_sp, tap_aps["xn"], xn[:, :])

        phase_barrier()

        if phases >= 2:
            with ExitStack() as _ph:
                wst = _ph.enter_context(nc.sbuf_tensor("wst", [128, 2 * H], F32))
                wbf = _ph.enter_context(nc.sbuf_tensor("wbf", [128, 2 * H], BF16))
                kst = _ph.enter_context(nc.sbuf_tensor("kst", [128, 2 * 512], BF16))
                vst = _ph.enter_context(nc.sbuf_tensor("vst", [128, 2 * 512], BF16))
                vtt = _ph.enter_context(nc.sbuf_tensor("vtt", [128, 2 * 512], BF16))
                sqb = _ph.enter_context(nc.sbuf_tensor("sqb", [128, 512], BF16))
                nrm = _ph.enter_context(nc.sbuf_tensor("nrm", [1, 8], F32))
                nsb = _ph.enter_context(nc.sbuf_tensor("nsb", [1, 2048], F32))
                psQ = _ph.enter_context(nc.psum_tensor("psQ", [128, 4 * 512], F32))
                psN = _ph.enter_context(nc.psum_tensor("psN", [128, 1024], F32))
                psVT = _ph.enter_context(nc.psum_tensor("psVT", [128, 512], F32))
                wq3 = dr_wqkv.rearrange("(kc p) m -> p kc m", p=128)
                free_w = [0, 0]
                free_kst = [0, 0]
                free_vst = [0, 0]
                free_vtt = [0, 0]
                free_psq = [0, 0, 0, 0]
                free_psn = [0, 0]
                free_sqb = 0
                free_psvt = 0
                for m in range(48):
                    par = m % 2
                    r.wait("sy", pe_s, free_w[par])
                    m_w = r.dma("sy", d_w2[par],
                                wst[:, par * H:(par + 1) * H].rearrange("p (kc m) -> p kc m", m=128),
                                wq3[:, :, m * 128:(m + 1) * 128])
                    r.wait("ve", d_w2[par], m_w)
                    m_cv = r.op("ve", ve_s, lambda e, par=par: e.tensor_copy(
                        wbf[:, par * H:(par + 1) * H], wst[:, par * H:(par + 1) * H]))
                    last_mm = 0
                    for t in range(4):
                        r.wait("pe", ve_s, max(m_cv, m_xn_done))
                        r.wait("pe", sc_s, free_psq[t])
                        for k in range(16):
                            r.op("pe", pe_s, lambda e, par=par, k=k, t=t: e.matmul(
                                psQ[:, t * 512:(t + 1) * 512],
                                wbf[:, par * H + k * 128: par * H + (k + 1) * 128],
                                xn[:, k * T + t * 512: k * T + (t + 1) * 512],
                                start=(k == 0), stop=(k == 15)))
                        m_mm = pe_s.n
                        last_mm = m_mm
                        if m < 16:
                            hq = m
                            r.wait("sc", pe_s, m_mm)
                            m_ev = r.op("sc", sc_s, lambda e, t=t, hq=hq: e.activation(
                                QO[:, hq * T + t * 512: hq * T + (t + 1) * 512],
                                psQ[:, t * 512:(t + 1) * 512], AF.Copy, scale=ISQD))
                            free_psq[t] = m_ev
                            r.wait("sc", pe_s, free_sqb)
                            m_sq = r.op("sc", sc_s, lambda e, t=t, hq=hq: e.activation(
                                sqb[:, :], QO[:, hq * T + t * 512: hq * T + (t + 1) * 512], AF.Square))
                            nh = t % 2
                            r.wait("pe", sc_s, m_sq)
                            r.wait("pe", ve_s, free_psn[nh])
                            m_on = r.op("pe", pe_s, lambda e, nh=nh: e.matmul(
                                psN[0:1, nh * 512:(nh + 1) * 512], ones_col_bf[:, :], sqb[:, :],
                                start=True, stop=True))
                            free_sqb = m_on
                            r.wait("ve", pe_s, m_on)
                            m_nm = r.op("ve", ve_s, lambda e, nh=nh, t=t: e.tensor_copy(
                                nsb[0:1, t * 512:(t + 1) * 512], psN[0:1, nh * 512:(nh + 1) * 512]))
                            free_psn[nh] = m_nm
                            if t == 3:
                                r.opn("ve", lambda e: e.reduce_max(nrm[0:1, 0:1], nsb[0:1, 0:1024], axis=AX))
                                m_nm2 = r.op("ve", ve_s, lambda e: e.reduce_max(nrm[0:1, 1:2], nsb[0:1, 1024:2048], axis=AX))
                                r.wait("sc", ve_s, m_nm2)
                                r.opn("sc", lambda e, hq=hq: e.activation(qn2[0:1, hq * 2: hq * 2 + 1], nrm[0:1, 0:1], AF.Copy))
                                m_qn = r.op("sc", sc_s, lambda e, hq=hq: e.activation(qn2[0:1, hq * 2 + 1: hq * 2 + 2], nrm[0:1, 1:2], AF.Copy))
                                r.wait("ve", sc_s, m_qn)
                                r.op("ve", ve_s, lambda e: e.nop())
                        elif m < 32:
                            hk = m - 16
                            kp = t % 2
                            r.wait("sc", pe_s, m_mm)
                            r.wait("sc", d_kst[kp], free_kst[kp])
                            m_ev = r.op("sc", sc_s, lambda e, t=t, kp=kp: e.activation(
                                kst[:, kp * 512:(kp + 1) * 512], psQ[:, t * 512:(t + 1) * 512], AF.Copy))
                            free_psq[t] = m_ev
                            r.wait("gp", sc_s, m_ev)
                            m_kd = r.dma("gp", d_kst[kp],
                                         dr_ck[hk * 128:(hk + 1) * 128, t * 512:(t + 1) * 512],
                                         kst[:, kp * 512:(kp + 1) * 512])
                            free_kst[kp] = m_kd
                            r.wait("sc", pe_s, free_sqb)
                            m_sq = r.op("sc", sc_s, lambda e, kp=kp: e.activation(
                                sqb[:, :], kst[:, kp * 512:(kp + 1) * 512], AF.Square))
                            nh = t % 2
                            r.wait("pe", sc_s, m_sq)
                            r.wait("pe", ve_s, free_psn[nh])
                            m_on = r.op("pe", pe_s, lambda e, nh=nh: e.matmul(
                                psN[0:1, nh * 512:(nh + 1) * 512], ones_col_bf[:, :], sqb[:, :],
                                start=True, stop=True))
                            free_sqb = m_on
                            r.wait("ve", pe_s, m_on)
                            m_nm = r.op("ve", ve_s, lambda e, nh=nh, t=t: e.tensor_copy(
                                nsb[0:1, t * 512:(t + 1) * 512], psN[0:1, nh * 512:(nh + 1) * 512]))
                            free_psn[nh] = m_nm
                            if t == 3:
                                m_nm2 = r.op("ve", ve_s, lambda e: e.reduce_max(nrm[0:1, 2:3], nsb[0:1, 0:2048], axis=AX))
                                r.wait("sc", ve_s, m_nm2)
                                m_kn = r.op("sc", sc_s, lambda e, hk=hk: e.activation(kn2[0:1, hk:hk + 1], nrm[0:1, 2:3], AF.Copy))
                                r.wait("ve", sc_s, m_kn)
                                r.op("ve", ve_s, lambda e: e.nop())
                        else:
                            hv = m - 32
                            vp = t % 2
                            r.wait("sc", pe_s, m_mm)
                            r.wait("sc", pe_s, free_vst[vp])
                            m_ev = r.op("sc", sc_s, lambda e, t=t, vp=vp: e.activation(
                                vst[:, vp * 512:(vp + 1) * 512], psQ[:, t * 512:(t + 1) * 512], AF.Copy))
                            free_psq[t] = m_ev
                            r.wait("pe", sc_s, m_ev)
                            r.wait("pe", ve_s, free_psvt)
                            for i in range(4):
                                r.op("pe", pe_s, lambda e, vp=vp, i=i: e.matmul(
                                    psVT[:, i * 128:(i + 1) * 128],
                                    vst[:, vp * 512 + i * 128: vp * 512 + (i + 1) * 128],
                                    idbf[:, :], start=True, stop=True))
                            m_tr = pe_s.n
                            free_vst[vp] = m_tr
                            r.wait("ve", pe_s, m_tr)
                            r.wait("ve", d_vtt[vp], free_vtt[vp])
                            m_vv = r.op("ve", ve_s, lambda e, vp=vp: e.tensor_copy(
                                vtt[:, vp * 512:(vp + 1) * 512], psVT[:, :]))
                            free_psvt = m_vv
                            r.wait("gp", ve_s, m_vv)
                            m_vd = r.dma("gp", d_vtt[vp],
                                         dr_cv[t * 512:(t + 1) * 512, hv * 128:(hv + 1) * 128]
                                         .rearrange("(i p) d -> p i d", p=128),
                                         vtt[:, vp * 512:(vp + 1) * 512].rearrange("p (i d) -> p i d", d=128))
                            free_vtt[vp] = m_vd
                    free_w[par] = last_mm
                m_qkv_pe = pe_s.n
                m_qkv_ve = ve_s.n
            r.wait("gp", ve_s, m_qkv_ve)
            r.dma("gp", d_sp, dr_kni, kn2[:, :])

        if phases >= 3:
            r.wait("gp", d_sp, d_sp.n)
            r.wait("gp", d_kst[0], d_kst[0].n)
            r.wait("gp", d_kst[1], d_kst[1].n)
            r.wait("gp", d_vtt[0], d_vtt[0].n)
            r.wait("gp", d_vtt[1], d_vtt[1].n)
            r.op("gp", c_k, lambda e: e.collective_compute(
                "AllGather", OP.bypass, replica_groups=GROUPS,
                ins=[dr_ck], outs=[dr_gk]))
            r.op("gp", c_v, lambda e: e.collective_compute(
                "AllGather", OP.bypass, replica_groups=GROUPS,
                ins=[dr_cv], outs=[dr_gv]))
            r.op("gp", c_nn, lambda e: e.collective_compute(
                "AllReduce", OP.max, replica_groups=GROUPS,
                ins=[dr_kni], outs=[dr_kno]))

        if phases >= 4:
            phase_barrier()

        if phases >= 4:
            with ExitStack() as _ph:
                gst = _ph.enter_context(nc.sbuf_tensor("gst", [16, 2 * H], F32))
                gbst = _ph.enter_context(nc.sbuf_tensor("gbst", [8, 16], F32))
                gbf = _ph.enter_context(nc.sbuf_tensor("gbf", [16, 2 * H], BF16))
                gbbf = _ph.enter_context(nc.sbuf_tensor("gbbf", [8, 16], BF16))
                wgT = _ph.enter_context(nc.sbuf_tensor("wgT", [128, 16 * 8], BF16))
                agT = _ph.enter_context(nc.sbuf_tensor("agT", [128, 16 * 16], BF16))
                bgT = _ph.enter_context(nc.sbuf_tensor("bgT", [16, 8], BF16))
                rterg = _ph.enter_context(nc.sbuf_tensor("rterg", [16, 512], BF16))
                lgsb = _ph.enter_context(nc.sbuf_tensor("lgsb", [8, 512], F32))
                rst = _ph.enter_context(nc.sbuf_tensor("rst", [128, 96], F32))
                wtm = _ph.enter_context(nc.sbuf_tensor("wtm", [128, 8], F32))
                psR = _ph.enter_context(nc.psum_tensor("psR", [128, 1024], F32))
                psR2 = _ph.enter_context(nc.psum_tensor("psR2", [128, 512], F32))
                r.dma("sy", d_g4, gst[0:8, 0:H], dr_gw)
                r.dma("sy", d_g4, gst[0:16, H:2 * H], dr_gA)
                m_gl = r.dma("sy", d_g4, gbst[:, :], dr_gB)
                r.wait("ve", d_g4, m_gl)
                r.opn("ve", lambda e: e.tensor_copy(gbf[0:8, 0:H], gst[0:8, 0:H]))
                r.opn("ve", lambda e: e.tensor_copy(gbf[0:16, H:2 * H], gst[0:16, H:2 * H]))
                m_gc = r.op("ve", ve_s, lambda e: e.tensor_copy(gbbf[:, :], gbst[:, :]))
                last_ev = [0, 0]
                r.wait("pe", ve_s, m_gc)
                for k in range(16):
                    pp = k % 2
                    r.wait("pe", ve_s, last_ev[pp])
                    r.op("pe", pe_s, lambda e, k=k, pp=pp: e.matmul(
                        psR[:, pp * 512: pp * 512 + 8],
                        gbf[0:8, k * 128:(k + 1) * 128], idbf[0:8, 0:8],
                        start=True, stop=True))
                    m_t = pe_s.n
                    r.wait("ve", pe_s, m_t)
                    last_ev[pp] = r.op("ve", ve_s, lambda e, k=k, pp=pp: e.tensor_copy(
                        wgT[:, k * 8:(k + 1) * 8], psR[:, pp * 512: pp * 512 + 8]))
                for k in range(16):
                    pp = k % 2
                    r.wait("pe", ve_s, last_ev[pp])
                    r.op("pe", pe_s, lambda e, k=k, pp=pp: e.matmul(
                        psR[:, pp * 512: pp * 512 + 16],
                        gbf[0:16, H + k * 128: H + (k + 1) * 128], idbf[0:16, 0:16],
                        start=True, stop=True))
                    m_t = pe_s.n
                    r.wait("ve", pe_s, m_t)
                    last_ev[pp] = r.op("ve", ve_s, lambda e, k=k, pp=pp: e.tensor_copy(
                        agT[:, k * 16:(k + 1) * 16], psR[:, pp * 512: pp * 512 + 16]))
                r.wait("pe", ve_s, last_ev[0])
                r.op("pe", pe_s, lambda e: e.matmul(psR[0:16, 0:8], gbbf[:, :], idbf[0:8, 0:8],
                                               start=True, stop=True))
                m_t3 = pe_s.n
                r.wait("ve", pe_s, m_t3)
                m_bgT = r.op("ve", ve_s, lambda e: e.tensor_copy(bgT[:, :], psR[0:16, 0:8]))

                prev_psR_sc = 0
                prev_psR_ve = m_bgT
                for t in range(4):
                    r.wait("pe", ve_s, prev_psR_ve)
                    r.wait("pe", sc_s, prev_psR_sc)
                    for k in range(16):
                        r.op("pe", pe_s, lambda e, k=k, t=t: e.matmul(
                            psR[0:16, 0:512], agT[:, k * 16:(k + 1) * 16],
                            xn[:, k * T + t * 512: k * T + (t + 1) * 512],
                            start=(k == 0), stop=(k == 15)))
                    m_ter = pe_s.n
                    r.wait("sc", pe_s, m_ter)
                    m_terg = r.op("sc", sc_s, lambda e: e.activation(
                        rterg[:, :], psR[0:16, 0:512], AF.Copy, scale=SCALING))
                    r.wait("pe", sc_s, m_terg)
                    for k in range(16):
                        r.opn("pe", lambda e, k=k, t=t: e.matmul(
                            psR2[0:8, 0:512], wgT[:, k * 8:(k + 1) * 8],
                            xn[:, k * T + t * 512: k * T + (t + 1) * 512],
                            start=(k == 0), stop=False))
                    r.op("pe", pe_s, lambda e: e.matmul(psR2[0:8, 0:512], bgT[:, :], rterg[:, :],
                                                   start=False, stop=True))
                    m_lg = pe_s.n
                    r.wait("ve", pe_s, m_lg)
                    m_lgs = r.op("ve", ve_s, lambda e: e.tensor_copy(lgsb[:, :], psR2[0:8, 0:512]))
                    r.wait("pe", ve_s, m_lgs)
                    for i in range(4):
                        r.op("pe", pe_s, lambda e, i=i: e.matmul(
                            psR[:, 512 + i * 128: 512 + i * 128 + 8],
                            lgsb[0:8, i * 128:(i + 1) * 128],
                            idf[0:8, 0:8], start=True, stop=True))
                    m_lt = pe_s.n
                    for i in range(4):
                        sl = psR[:, 512 + i * 128: 512 + i * 128 + 8]
                        r.wait("ve", pe_s, m_lt)
                        m_nm = r.op("ve", ve_s, lambda e, sl=sl: e.reduce_max(
                            rst[:, 0:1], sl, axis=AX, negate=True))
                        r.wait("sc", ve_s, m_nm)
                        m_px = r.op("sc", sc_s, lambda e, sl=sl: e.activation(
                            rst[:, 8:16], sl, AF.Exp, bias=rst[:, 0:1], accum_out=rst[:, 1:2]))
                        prev_psR_sc = m_px
                        r.wait("ve", sc_s, m_px)
                        r.opn("ve", lambda e: e.reciprocal(rst[:, 2:3], rst[:, 1:2]))
                        r.vsync("ve", ve_s)
                        m_p = r.op("ve", ve_s, lambda e: e.tensor_scalar(rst[:, 16:24], rst[:, 8:16], rst[:, 2:3], None, OP.mult))
                        r.wait("sc", ve_s, m_p)
                        m_pc = r.op("sc", sc_s, lambda e: e.activation(rst[:, 48:56], rst[:, 16:24], AF.Copy))
                        r.wait("ve", sc_s, m_pc)
                        r.opn("ve", lambda e: e.reduce_max(rst[:, 24:25], rst[:, 48:56], axis=AX))
                        r.vsync("ve", ve_s)
                        m_pm = r.op("ve", ve_s, lambda e: e.scalar_tensor_tensor(
                            rst[:, 32:40], rst[:, 48:56], rst[:, 24:25], rst[:, 48:56], OP.is_lt, OP.mult))
                        r.wait("sc", ve_s, m_pm)
                        r.opn("sc", lambda e: e.activation(rst[:, 56:64], rst[:, 32:40], AF.Copy))
                        m_m1c = r.op("sc", sc_s, lambda e: e.activation(rst[:, 28:29], rst[:, 24:25], AF.Copy))
                        r.wait("ve", sc_s, m_m1c)
                        r.opn("ve", lambda e: e.reduce_max(rst[:, 25:26], rst[:, 56:64], axis=AX))
                        r.vsync("ve", ve_s)
                        m_den = r.op("ve", ve_s, lambda e: e.tensor_scalar(rst[:, 26:27], rst[:, 28:29], rst[:, 25:26], None, OP.add))
                        r.wait("sc", ve_s, m_den)
                        m_dc = r.op("sc", sc_s, lambda e: e.activation(rst[:, 29:30], rst[:, 26:27], AF.Copy))
                        r.wait("ve", sc_s, m_dc)
                        r.opn("ve", lambda e: e.reciprocal(rst[:, 27:28], rst[:, 29:30]))
                        r.vsync("ve", ve_s)
                        m_sel = r.op("ve", ve_s, lambda e: e.scalar_tensor_tensor(
                            rst[:, 40:48], rst[:, 48:56], rst[:, 25:26], rst[:, 48:56], OP.is_ge, OP.mult))
                        r.wait("sc", ve_s, m_sel)
                        m_slc = r.op("sc", sc_s, lambda e: e.activation(rst[:, 64:72], rst[:, 40:48], AF.Copy))
                        r.wait("ve", sc_s, m_slc)
                        m_wtm = r.op("ve", ve_s, lambda e: e.tensor_scalar(
                            wtm[:, 0:8], rst[:, 64:72], rst[:, 27:28], None, OP.mult))
                        prev_psR_ve = m_wtm
                        r.wait("pe", ve_s, m_wtm)
                        r.op("pe", pe_s, lambda e: e.matmul(psR2[0:8, 0:128], wtm[:, 0:8], idf[:, :],
                                                       start=True, stop=True))
                        m_wt = pe_s.n
                        r.wait("ve", pe_s, m_wt)
                        m_w8 = r.op("ve", ve_s, lambda e, t=t, i=i: e.tensor_copy(
                            wT8[:, t * 512 + i * 128: t * 512 + (i + 1) * 128], psR2[0:8, 0:128]))
                        r.wait("pe", ve_s, m_w8)
                m_rout_ve = ve_s.n
            r.wait("gp", ve_s, m_rout_ve)
            m_wsp = r.dma("gp", d_sp, dr_wr, wT8[:, :])
            if "w" in tap_aps:
                r.wait("gp", d_sp, m_wsp)
                r.dma("gp", d_sp, tap_aps["w"], dr_wr)

        if phases >= 5:
            phase_barrier()

        if phases >= 5:
            with ExitStack() as _ph:
                agupT = _ph.enter_context(nc.sbuf_tensor("agupT", [128, 256], BF16))
                gast = _ph.enter_context(nc.sbuf_tensor("gast", [16, H // 2], F32))
                gabf = _ph.enter_context(nc.sbuf_tensor("gabf", [16, H], BF16))
                gbst2 = _ph.enter_context(nc.sbuf_tensor("gbst2", [128, 256], F32))
                gbbf2 = _ph.enter_context(nc.sbuf_tensor("gbbf2", [128, 256], BF16))
                bgupT = _ph.enter_context(nc.sbuf_tensor("bgupT", [16, H], BF16))
                gter = _ph.enter_context(nc.sbuf_tensor("gter", [16, T], BF16))
                w1r = _ph.enter_context(nc.sbuf_tensor("w1r", [1, 512], F32))
                wrep = _ph.enter_context(nc.sbuf_tensor("wrep", [128, 4 * 512], BF16))
                gwst = _ph.enter_context(nc.sbuf_tensor("gwst", [128, 4096], F32))
                gwbf = _ph.enter_context(nc.sbuf_tensor("gwbf", [128, 2 * 4096], BF16))
                sgx = _ph.enter_context(nc.sbuf_tensor("sgx", [128, 2 * 512], BF16))
                upx = _ph.enter_context(nc.sbuf_tensor("upx", [128, 2 * 512], BF16))
                hst = _ph.enter_context(nc.sbuf_tensor("hst", [128, 2 * 512], BF16))
                psT5 = _ph.enter_context(nc.psum_tensor("psT5", [128, 512], F32))
                psW5 = _ph.enter_context(nc.psum_tensor("psW5", [128, 512], F32))
                psG = _ph.enter_context(nc.psum_tensor("psG", [128, 2 * 512], F32))
                psU5 = _ph.enter_context(nc.psum_tensor("psU5", [128, 2 * 512], F32))
                prev_t5_sc = 0
                m_prev_gabf_rd = 0
                m_prev_gbbf_rd = 0
                gw3 = dr_gup.rearrange("e (kc p) m -> e p kc m", p=128)
                free_gw = [0, 0]
                m_gw_conv = 0
                free_psg = [0, 0]
                free_psu = [0, 0]
                free_sgx = [0, 0]
                free_upx = [0, 0]
                free_hst = [0, 0]
                m_bc_prev = 0
                for e_i in range(E):
                    m_ac = 0
                    for hf in range(2):
                        r.wait("sy", ve_s, m_ac)
                        r.wait("sy", pe_s, m_prev_gabf_rd)
                        m_a = r.dma("sy", d_a5, gast[:, :], dr_gupA[e_i, :, hf * (H // 2):(hf + 1) * (H // 2)])
                        r.wait("ve", d_a5, m_a)
                        m_ac = r.op("ve", ve_s, lambda e, hf=hf: e.tensor_copy(
                            gabf[:, hf * (H // 2):(hf + 1) * (H // 2)], gast[:, :]))
                    m_agupT = 0
                    for k in range(16):
                        r.wait("pe", ve_s, max(m_ac, m_agupT))
                        r.wait("pe", sc_s, prev_t5_sc)
                        r.op("pe", pe_s, lambda e, k=k: e.matmul(
                            psT5[:, 0:16], gabf[0:16, k * 128:(k + 1) * 128],
                            idbf[0:16, 0:16], start=True, stop=True))
                        m_t = pe_s.n
                        r.wait("ve", pe_s, m_t)
                        m_agupT = r.op("ve", ve_s, lambda e, k=k: e.tensor_copy(
                            agupT[:, k * 16:(k + 1) * 16], psT5[:, 0:16]))
                    m_prev_gabf_rd = pe_s.n
                    r.wait("sy", ve_s, m_bc_prev)
                    r.wait("sy", pe_s, m_prev_gbbf_rd)
                    m_b = r.dma("sy", d_b5, gbst2.ap().rearrange("p (c r) -> p c r", r=16),
                                dr_gupB[e_i, :, :].rearrange("(c p) r -> p c r", p=128))
                    r.wait("ve", d_b5, m_b)
                    r.wait("ve", pe_s, m_prev_gbbf_rd)
                    m_bc = r.op("ve", ve_s, lambda e: e.tensor_copy(gbbf2[:, :], gbst2[:, :]))
                    m_bc_prev = m_bc
                    last_ev = 0
                    for k in range(16):
                        r.wait("pe", ve_s, max(m_bc, last_ev))
                        r.op("pe", pe_s, lambda e, k=k: e.matmul(
                            psT5[0:16, 0:128], gbbf2[:, k * 16:(k + 1) * 16],
                            idbf[:, :], start=True, stop=True))
                        m_t = pe_s.n
                        r.wait("ve", pe_s, m_t)
                        last_ev = r.op("ve", ve_s, lambda e, k=k: e.tensor_copy(
                            bgupT[0:16, k * 128:(k + 1) * 128], psT5[0:16, 0:128]))
                    m_prev_gbbf_rd = pe_s.n
                    r.wait("sy", d_sp, m_wsp)
                    last_wr = 0
                    last_w1_rd = 0
                    for t in range(4):
                        r.wait("sy", pe_s, last_w1_rd)
                        m_w1 = r.dma("sy", d_w1, w1r[:, :], dr_wr[e_i:e_i + 1, t * 512:(t + 1) * 512])
                        r.wait("pe", d_w1, m_w1)
                        r.wait("pe", ve_s, max(last_wr, last_ev))
                        r.op("pe", pe_s, lambda e: e.matmul(
                            psW5[:, 0:512], ones_row[:, :], w1r[0:1, 0:512],
                            start=True, stop=True))
                        m_t = pe_s.n
                        last_w1_rd = m_t
                        r.wait("ve", pe_s, m_t)
                        last_wr = r.op("ve", ve_s, lambda e, t=t: e.tensor_copy(
                            wrep[:, t * 512:(t + 1) * 512], psW5[:, 0:512]))
                    m_wrep = last_wr
                    last_terg = prev_t5_sc
                    for t in range(4):
                        r.wait("pe", sc_s, last_terg)
                        for k in range(16):
                            r.op("pe", pe_s, lambda e, k=k, t=t: e.matmul(
                                psT5[0:16, 0:512], agupT[:, k * 16:(k + 1) * 16],
                                xn[:, k * T + t * 512: k * T + (t + 1) * 512],
                                start=(k == 0), stop=(k == 15)))
                        m_tm = pe_s.n
                        r.wait("sc", pe_s, m_tm)
                        last_terg = r.op("sc", sc_s, lambda e, t=t: e.activation(
                            gter[:, t * 512:(t + 1) * 512], psT5[0:16, 0:512], AF.Copy, scale=SCALING))
                    prev_t5_sc = last_terg
                    m_gter = last_terg
                    for mp in range(8):
                        par = mp % 2
                        gpar = (e_i * 8 + mp) % 2
                        r.wait("sy", pe_s, free_gw[par])
                        r.wait("sy", ve_s, m_gw_conv)
                        r.dma("sy", d_w5[gpar],
                              gwst[:, 0:2048].rearrange("p (kc m) -> p kc m", m=128),
                              gw3[e_i, :, :, mp * 128:(mp + 1) * 128])
                        m_w2 = r.dma("sy", d_w5[gpar],
                                     gwst[:, 2048:4096].rearrange("p (kc m) -> p kc m", m=128),
                                     gw3[e_i, :, :, (mp + 8) * 128:(mp + 9) * 128])
                        r.wait("ve", d_w5[gpar], m_w2)
                        m_wc = r.op("ve", ve_s, lambda e, par=par: e.tensor_copy(
                            gwbf[:, par * 4096:(par + 1) * 4096], gwst[:, :]))
                        m_gw_conv = m_wc
                        for t in range(4):
                            tp = t % 2
                            r.wait("pe", ve_s, max(m_wc, m_wrep))
                            r.wait("pe", sc_s, max(m_gter, free_psg[tp]))
                            for k in range(16):
                                r.opn("pe", lambda e, par=par, k=k, t=t, tp=tp: e.matmul(
                                    psG[:, tp * 512:(tp + 1) * 512],
                                    gwbf[:, par * 4096 + k * 128: par * 4096 + (k + 1) * 128],
                                    xn[:, k * T + t * 512: k * T + (t + 1) * 512],
                                    start=(k == 0), stop=False))
                            r.op("pe", pe_s, lambda e, mp=mp, t=t, tp=tp: e.matmul(
                                psG[:, tp * 512:(tp + 1) * 512],
                                bgupT[0:16, mp * 128:(mp + 1) * 128],
                                gter[:, t * 512:(t + 1) * 512], start=False, stop=True))
                            m_gmm = pe_s.n
                            r.wait("pe", ve_s, free_psu[tp])
                            for k in range(16):
                                r.opn("pe", lambda e, par=par, k=k, t=t, tp=tp: e.matmul(
                                    psU5[:, tp * 512:(tp + 1) * 512],
                                    gwbf[:, par * 4096 + 2048 + k * 128: par * 4096 + 2048 + (k + 1) * 128],
                                    xn[:, k * T + t * 512: k * T + (t + 1) * 512],
                                    start=(k == 0), stop=False))
                            r.op("pe", pe_s, lambda e, mp=mp, t=t, tp=tp: e.matmul(
                                psU5[:, tp * 512:(tp + 1) * 512],
                                bgupT[0:16, (mp + 8) * 128:(mp + 9) * 128],
                                gter[:, t * 512:(t + 1) * 512], start=False, stop=True))
                            m_umm = pe_s.n
                            r.wait("sc", pe_s, m_gmm)
                            r.wait("sc", ve_s, free_sgx[tp])
                            m_sg = r.op("sc", sc_s, lambda e, tp=tp: e.activation(
                                sgx[:, tp * 512:(tp + 1) * 512], psG[:, tp * 512:(tp + 1) * 512], AF.Silu))
                            free_psg[tp] = m_sg
                            r.wait("ve", pe_s, m_umm)
                            r.wait("ve", ve_s, free_upx[tp])
                            m_ux = r.op("ve", ve_s, lambda e, t=t, tp=tp: e.tensor_tensor(
                                upx[:, tp * 512:(tp + 1) * 512], psU5[:, tp * 512:(tp + 1) * 512],
                                wrep[:, t * 512:(t + 1) * 512], OP.mult))
                            free_psu[tp] = m_ux
                            r.wait("ve", sc_s, m_sg)
                            r.wait("ve", d_h[tp], free_hst[tp])
                            m_h = r.op("ve", ve_s, lambda e, tp=tp: e.tensor_tensor(
                                hst[:, tp * 512:(tp + 1) * 512], sgx[:, tp * 512:(tp + 1) * 512],
                                upx[:, tp * 512:(tp + 1) * 512], OP.mult))
                            free_sgx[tp] = m_h
                            free_upx[tp] = m_h
                            r.wait("gp", ve_s, m_h)
                            m_hd = r.dma("gp", d_h[tp],
                                         dr_h[e_i, mp * 128:(mp + 1) * 128, t * 512:(t + 1) * 512],
                                         hst[:, tp * 512:(tp + 1) * 512])
                            free_hst[tp] = m_hd
                        free_gw[par] = pe_s.n
            m_gup_done = (d_h[0].n, d_h[1].n)
            if "h" in tap_aps:
                r.wait("gp", d_h[0], m_gup_done[0])
                r.wait("gp", d_h[1], m_gup_done[1])
                r.dma("gp", d_sp, tap_aps["h"], dr_h)

        if phases >= 6:
            phase_barrier()
        w45_es.close()
        xn_es.close()

        if phases >= 6:
            with ExitStack() as _ph:
                kbuf = _ph.enter_context(nc.sbuf_tensor("kbuf", [128, 2 * 1024], BF16))
                vbuf = _ph.enter_context(nc.sbuf_tensor("vbuf", [128, 2 * 1024], BF16))
                ptb = _ph.enter_context(nc.sbuf_tensor("ptb", [128, 2 * 1024], BF16))
                smask = _ph.enter_context(nc.sbuf_tensor("smask", [128, 2 * 512], F32))
                masks = _ph.enter_context(nc.sbuf_tensor("masks", [128, 4 * 512], F32))
                ssum = _ph.enter_context(nc.sbuf_tensor("ssum", [128, 2 * 2048], F32))
                ssumC = _ph.enter_context(nc.sbuf_tensor("ssumC", [128, 2 * 2048], F32))
                knrow = _ph.enter_context(nc.sbuf_tensor("knrow", [1, NHEAD], F32))
                kdup = _ph.enter_context(nc.sbuf_tensor("kdup", [1, 32], F32))
                crow = _ph.enter_context(nc.sbuf_tensor("crow", [1, 32], F32))
                csq = _ph.enter_context(nc.sbuf_tensor("csq", [1, 32], F32))
                rrow = _ph.enter_context(nc.sbuf_tensor("rrow", [1, 1024], F32))
                rrep = _ph.enter_context(nc.sbuf_tensor("rrep", [128, 1024], F32))
                psS0 = _ph.enter_context(nc.psum_tensor("psS0", [128, 1024], F32))
                psS1 = _ph.enter_context(nc.psum_tensor("psS1", [128, 1024], F32))
                psUa = _ph.enter_context(nc.psum_tensor("psUa", [128, 2048], F32))
                psS = [psS0, psS1]
                m_masks = r.dma("gp", d_in, masks[:, :], dr_mask)
                r.wait("sy", c_nn, 1)
                m_kn = r.dma("sy", d_kn6, knrow[:, :], dr_kno)
                kd3 = kdup.ap().rearrange("o (hh two) -> o hh two", two=2)
                kn3 = knrow.ap().rearrange("o (h u) -> o h u", u=1)
                r.wait("sc", d_kn6, m_kn)
                r.opn("sc", lambda e: e.activation(kd3[:, :, 0:1], kn3, AF.Copy))
                m_kd = r.op("sc", sc_s, lambda e: e.activation(kd3[:, :, 1:2], kn3, AF.Copy))
                r.wait("ve", sc_s, m_kd)
                r.wait("ve", ve_s, m_qkv_ve)
                m_cp = r.op("ve", ve_s, lambda e: e.tensor_tensor(csq[:, :], qn2[:, :], kdup[:, :], OP.mult))
                r.wait("sc", ve_s, m_cp)
                m_cr = r.op("sc", sc_s, lambda e: e.activation(crow[:, :], csq[:, :], AF.Sqrt))
                r.wait("ve", sc_s, m_cr)
                m_ng = r.op("ve", ve_s, lambda e: e.tensor_scalar(csq[:, :], crow[:, :], -1.0, None, OP.mult))
                r.wait("pe", ve_s, m_ng)
                r.wait("pe", pe_s, m_qkv_pe)
                r.op("pe", pe_s, lambda e: e.matmul(psS0[:, 0:32], ones_row[:, :], csq[0:1, :],
                                               start=True, stop=True))
                m_nr = pe_s.n
                r.wait("ve", pe_s, m_nr)
                m_negc = r.op("ve", ve_s, lambda e: e.tensor_copy(negc_rep[:, :], psS0[:, 0:32]))
                if "negc" in tap_aps:
                    r.wait("gp", ve_s, m_negc)
                    r.dma("gp", d_sp, tap_aps["negc"], negc_rep[:, :])
                r.wait("pe", ve_s, m_negc)
                r.wait("sy", c_k, 1)
                r.wait("sy", c_v, 1)
                r.wait("ve", d_in, m_masks)
                r.wait("sc", ve_s, m_negc)

                gv3 = dr_gv.rearrange("(s p) d -> p s d", p=128)

                def emit_attn_variant(rv, cl):
                    units = [(0, cl), (1, 7 - cl)]
                    for jj in range(cl):
                        units.append((0, jj))
                    for jj in range(7 - cl):
                        units.append((1, jj))
                    kv_pe_done = {}
                    g_seq = 0
                    for h_i in range(NHEAD):
                        rv.wait("pe", ve_s, ve_s.n)
                        for j4 in range(4):
                            rv.op("pe", pe_s, lambda e, j4=j4: e.matmul(
                                psUa[:, j4 * 512:(j4 + 1) * 512], zero_bf[:, 0:128], zero_bf[:, :],
                                start=True, stop=False))
                        pe_s.n
                        m_ssz = rv.op("ve", ve_s, lambda e: e.memset(ssum[:, 0:2048], 0.0))
                        rv.wait("gp", pe_s, pe_s.n)
                        rv.op("gp", gp_s, lambda e: e.memset(ssumC[:, 0:2048], 0.0))
                        m_exp_prev = [0, 0]
                        m_psS_ve = [0, 0]
                        m_pt_pe = [0, 0]
                        m_pt_ve = [0, 0]
                        m_pt_gp = [0, 0]
                        m_smask_rd = [0, 0]
                        cnt = {("A", 0): 0, ("A", 1): 0, ("C", 0): 0, ("C", 1): 0}
                        for u in range(9):
                            qc, jch = units[u]
                            up = g_seq % 2
                            rk = jch if jch < 4 else 7 - jch
                            lci = 0 if jch < 4 else 1
                            rv.wait("sy", pe_s, kv_pe_done.get(g_seq - 2, 0))
                            rv.dma("sy", d_kv[up],
                                   kbuf[:, up * 1024:(up + 1) * 1024],
                                   dr_gk[rk * H + h_i * 128: rk * H + (h_i + 1) * 128,
                                         lci * 1024:(lci + 1) * 1024])
                            vs = (rk * T + lci * 1024) // 128
                            m_kvd = rv.dma("sy", d_kv[up],
                                           vbuf[:, up * 1024:(up + 1) * 1024].rearrange("p (s d) -> p s d", d=128),
                                           gv3[:, vs:vs + 8, h_i * 128:(h_i + 1) * 128])
                            nbias = negc_rep[:, h_i * 2 + qc: h_i * 2 + qc + 1]
                            diag = u < 2
                            pending = []

                            def emit_av(rv=rv, up=up, qc=qc):
                                sss, pp, segs, m_exp = pending.pop(0)
                                rv.wait("pe", sc_s, m_exp if m_exp else sc_s.n)
                                for qh in range(2):
                                    if segs[qh] == "c":
                                        continue
                                    rv.op("pe", pe_s, lambda e, sss=sss, qh=qh, pp=pp: e.matmul(
                                        psUa[:, qc * 1024 + qh * 512: qc * 1024 + (qh + 1) * 512],
                                        vbuf[:, up * 1024 + sss * 128: up * 1024 + (sss + 1) * 128],
                                        ptb[:, pp * 1024 + qh * 512: pp * 1024 + (qh + 1) * 512],
                                        start=False, stop=False))
                                m_pt_pe[pp] = pe_s.n

                            for sss in range(8):
                                pp = sss % 2
                                segs = [_diag_status(sss, qh) for qh in range(2)] if diag else ["f", "f"]
                                rv.wait("pe", d_kv[up], m_kvd)
                                rv.wait("pe", sc_s, m_exp_prev[pp])
                                rv.wait("pe", ve_s, m_psS_ve[pp])
                                for qh in range(2):
                                    if segs[qh] == "c":
                                        continue
                                    rv.op("pe", pe_s, lambda e, sss=sss, qh=qh, up=up, pp=pp, h_i=h_i, qc=qc: e.matmul(
                                        psS[pp][:, qh * 512:(qh + 1) * 512],
                                        kbuf[:, up * 1024 + sss * 128: up * 1024 + (sss + 1) * 128],
                                        QO[:, h_i * T + qc * 1024 + qh * 512: h_i * T + qc * 1024 + (qh + 1) * 512],
                                        start=True, stop=True))
                                m_smm = pe_s.n
                                if pending:
                                    emit_av()
                                m_msk = 0
                                if diag:
                                    for qh in range(2):
                                        if segs[qh] != "p":
                                            continue
                                        mi = (sss * 128 - qh * 512) // 128
                                        rv.wait("ve", pe_s, m_smm)
                                        rv.wait("ve", sc_s, m_smask_rd[qh])
                                        m_msk = rv.op("ve", ve_s, lambda e, pp=pp, qh=qh, mi=mi: e.tensor_tensor(
                                            smask[:, qh * 512:(qh + 1) * 512],
                                            psS[pp][:, qh * 512:(qh + 1) * 512],
                                            masks[:, mi * 512:(mi + 1) * 512], OP.add))
                                        m_psS_ve[pp] = m_msk
                                    for qh in range(2):
                                        if segs[qh] == "c":
                                            rv.wait("ve", pe_s, m_pt_pe[pp])
                                            if m_pt_gp[pp]:
                                                rv.wait("ve", gp_s, m_pt_gp[pp])
                                            m_msk = rv.op("ve", ve_s, lambda e, pp=pp, qh=qh: e.memset(
                                                ptb[:, pp * 1024 + qh * 512: pp * 1024 + (qh + 1) * 512], 0.0))
                                rv.wait("sc", pe_s, max(m_smm, m_pt_pe[pp]))
                                rv.wait("sc", ve_s, max(m_msk, m_pt_ve[pp]))
                                if m_pt_gp[pp]:
                                    rv.wait("sc", gp_s, m_pt_gp[pp])
                                if not diag:
                                    m_exp = rv.op("sc", sc_s, lambda e, pp=pp, nbias=nbias: e.activation(
                                        ptb[:, pp * 1024:(pp + 1) * 1024], psS[pp][:, :],
                                        AF.Exp, bias=nbias))
                                    m_exp_prev[pp] = m_exp
                                else:
                                    m_exp = 0
                                    for qh in range(2):
                                        if segs[qh] == "c":
                                            continue
                                        if segs[qh] == "p":
                                            m_exp = rv.op("sc", sc_s, lambda e, pp=pp, qh=qh, nbias=nbias: e.activation(
                                                ptb[:, pp * 1024 + qh * 512: pp * 1024 + (qh + 1) * 512],
                                                smask[:, qh * 512:(qh + 1) * 512], AF.Exp, bias=nbias))
                                            m_smask_rd[qh] = m_exp
                                        else:
                                            m_exp = rv.op("sc", sc_s, lambda e, pp=pp, qh=qh, nbias=nbias: e.activation(
                                                ptb[:, pp * 1024 + qh * 512: pp * 1024 + (qh + 1) * 512],
                                                psS[pp][:, qh * 512:(qh + 1) * 512], AF.Exp, bias=nbias))
                                            m_exp_prev[pp] = m_exp
                                ch = "A" if sss in (0, 2, 4, 6, 7) else "C"
                                aeng = "ve" if ch == "A" else "gp"
                                asem = ve_s if ch == "A" else gp_s
                                abuf = ssum if ch == "A" else ssumC
                                c0 = cnt[(ch, qc)]
                                spp, dpp = c0 % 2, (c0 + 1) % 2
                                cnt[(ch, qc)] = c0 + 1
                                rv.wait(aeng, sc_s, m_exp if m_exp else sc_s.n)
                                if ch == "C" and diag and m_msk:
                                    rv.wait("gp", ve_s, m_msk)
                                if u == 0 and sss == 0:
                                    rv.wait("ve", ve_s, m_ssz)
                                rv.opn(aeng, lambda e, spp=spp, dpp=dpp, pp=pp, qc=qc, abuf=abuf: e.tensor_tensor(
                                    abuf[:, dpp * 2048 + qc * 1024: dpp * 2048 + (qc + 1) * 1024],
                                    abuf[:, spp * 2048 + qc * 1024: spp * 2048 + (qc + 1) * 1024],
                                    ptb[:, pp * 1024:(pp + 1) * 1024], OP.add))
                                m_ssa = rv.op(aeng, asem, lambda e: e.nop())
                                if ch == "A":
                                    m_pt_ve[pp] = m_ssa
                                else:
                                    m_pt_gp[pp] = m_ssa
                                pending.append((sss, pp, segs, m_exp))
                            while pending:
                                emit_av()
                            kv_pe_done[g_seq] = pe_s.n
                            g_seq += 1
                        for j4 in range(4):
                            if j4 == 3:
                                rv.op("pe", pe_s, lambda e, j4=j4: e.matmul(
                                    psUa[:, j4 * 512:(j4 + 1) * 512], zero_bf[:, 0:128], zero_bf[:, :],
                                    start=False, stop=True))
                            else:
                                rv.opn("pe", lambda e, j4=j4: e.matmul(
                                    psUa[:, j4 * 512:(j4 + 1) * 512], zero_bf[:, 0:128], zero_bf[:, :],
                                    start=False, stop=True))
                        m_last_exp = sc_s.n
                        m_last_ssa = ve_s.n
                        m_last_av = pe_s.n
                        m_last_gadd = gp_s.n
                        for qc in range(2):
                            ppf = qc
                            pA = (cnt[("A", qc)] % 2) * 2048
                            pC = (cnt[("C", qc)] % 2) * 2048
                            rv.wait("pe", sc_s, m_last_exp)
                            rv.wait("pe", ve_s, max(m_last_ssa, m_psS_ve[ppf]))
                            rv.wait("pe", gp_s, m_last_gadd)
                            for qh in range(2):
                                rv.opn("pe", lambda e, qc=qc, qh=qh, ppf=ppf, pA=pA: e.matmul(
                                    psS[ppf][0:1, qh * 512:(qh + 1) * 512], ones_col[:, :],
                                    ssum[:, pA + qc * 1024 + qh * 512: pA + qc * 1024 + (qh + 1) * 512],
                                    start=True, stop=False))
                                rv.op("pe", pe_s, lambda e, qc=qc, qh=qh, ppf=ppf, pC=pC: e.matmul(
                                    psS[ppf][0:1, qh * 512:(qh + 1) * 512], ones_col[:, :],
                                    ssumC[:, pC + qc * 1024 + qh * 512: pC + qc * 1024 + (qh + 1) * 512],
                                    start=False, stop=True))
                            m_sr = pe_s.n
                            rv.wait("ve", pe_s, m_sr)
                            m_rr = rv.op("ve", ve_s, lambda e, ppf=ppf: e.reciprocal(
                                rrow[0:1, :], psS[ppf][0:1, :]))
                            rv.wait("pe", ve_s, m_rr)
                            for qh in range(2):
                                rv.op("pe", pe_s, lambda e, qh=qh, ppf=ppf: e.matmul(
                                    psS[ppf][:, qh * 512:(qh + 1) * 512], ones_row[:, :],
                                    rrow[0:1, qh * 512:(qh + 1) * 512], start=True, stop=True))
                            m_rp = pe_s.n
                            rv.wait("ve", pe_s, m_rp)
                            m_rrep = rv.op("ve", ve_s, lambda e, ppf=ppf: e.tensor_copy(
                                rrep[:, :], psS[ppf][:, :]))
                            rv.wait("ve", pe_s, m_last_av)
                            rv.op("ve", ve_s, lambda e, h_i=h_i, qc=qc: e.tensor_tensor(
                                QO[:, h_i * T + qc * 1024: h_i * T + (qc + 1) * 1024],
                                psUa[:, qc * 1024:(qc + 1) * 1024], rrep[:, :], OP.mult))

                all_sems = [d_in, d_kv[0], d_kv[1], d_sp, c_k, c_v, c_nn, pe_s, ve_s, sc_s, gp_s]
                snap0 = {id(sm): sm.n for sm in all_sems}
                variants = []
                final0 = None
                for v in range(4):
                    for sm in all_sems:
                        sm.n = snap0[id(sm)]
                    rvv = Rec()
                    emit_attn_variant(rvv, v)
                    fin = {id(sm): sm.n for sm in all_sems}
                    if final0 is None:
                        final0 = fin
                    else:
                        assert fin == final0, "attention variants diverge in sem counts"
                    variants.append(rvv)
                for sm in all_sems:
                    sm.n = final0[id(sm)]
                eng_regs = {"sy": rg_sy[0], "pe": rg_pe, "ve": rg_ve, "sc": rg_sc, "gp": rg_gp}
                for eng in ("sy", "pe", "ve", "sc", "gp"):
                    bodies = [variants[v].ops[eng] for v in range(4)]
                    r.wait(eng, d_in, m_consts)
                    def sw(e, reg=eng_regs[eng], bodies=bodies):
                        e.reg_load(reg, cfgs[0:1, 40:41])
                        idx = e.snap(reg, min_val=0, max_val=3)
                        for jv in nc.Switch(engines=[e], index=[idx], n=4):
                            for fn in bodies[jv]:
                                fn(e)
                    r.raw(eng, sw)
            m_attn_ve = ve_s.n
            if "qo" in tap_aps:
                r.wait("gp", ve_s, m_attn_ve)
                r.dma("gp", d_sp, tap_aps["qo"], QO[:, :])
        if "qo" in tap_aps and phases < 6:
            r.wait("gp", sc_s, sc_s.n)
            r.dma("gp", d_sp, tap_aps["qo"], QO[:, :])
        if "gk" in tap_aps and phases >= 3:
            r.wait("gp", c_k, 1)
            r.dma("gp", d_sp, tap_aps["gk"], dr_gk)
        if "gv" in tap_aps and phases >= 3:
            r.wait("gp", c_v, 1)
            r.dma("gp", d_sp, tap_aps["gv"], dr_gv)

        if phases >= 7:
            phase_barrier()

        if phases >= 7:
            with ExitStack() as _ph:
                dATst = _ph.enter_context(nc.sbuf_tensor("dATst", [16, II], F32))
                ter2P = [_ph.enter_context(nc.sbuf_tensor(f"ter2_{j}", [128, T], BF16)) for j in range(3)]
                dBTP = [_ph.enter_context(nc.sbuf_tensor(f"dBT_{j}", [128, H], BF16)) for j in range(3)]
                dATbf = _ph.enter_context(nc.sbuf_tensor("dATbf", [16, II], BF16))
                dAT = _ph.enter_context(nc.sbuf_tensor("dAT", [128, 128], BF16))
                dBst = _ph.enter_context(nc.sbuf_tensor("dBst", [128, 256], F32))
                dBbf = _ph.enter_context(nc.sbuf_tensor("dBbf", [128, 256], BF16))
                wost = _ph.enter_context(nc.sbuf_tensor("wost", [128, 512], F32))
                wobf = _ph.enter_context(nc.sbuf_tensor("wobf", [128, 2 * 16 * 512], BF16))
                dnst = _ph.enter_context(nc.sbuf_tensor("dnst", [128, 4096], F32))
                dnbf = _ph.enter_context(nc.sbuf_tensor("dnbf", [128, 2 * 4096], BF16))
                hbb = _ph.enter_context(nc.sbuf_tensor("hbb", [128, 2 * 8192], BF16))
                xres = _ph.enter_context(nc.sbuf_tensor("xres", [128, 2 * 512], F32))
                osb = _ph.enter_context(nc.sbuf_tensor("osb", [128, 2 * 512], F32))
                ps7_es = ExitStack()
                psT7 = ps7_es.enter_context(nc.psum_tensor("psT7", [128, 512], F32))
                m_dbt = 0
                m_bc7 = 0
                m_dbbf_rd = 0
                for e_i in range(E):
                    r.wait("sy", ve_s, m_bc7)
                    r.wait("sy", pe_s, m_dbbf_rd)
                    m_b = r.dma("sy", d_b7, dBst.ap().rearrange("p (c r) -> p c r", r=16),
                                dr_downB[e_i, :, :].rearrange("(c p) r -> p c r", p=128))
                    r.wait("ve", d_b7, m_b)
                    r.wait("ve", pe_s, m_dbbf_rd)
                    m_bc = r.op("ve", ve_s, lambda e: e.tensor_copy(dBbf[:, :], dBst[:, :]))
                    m_bc7 = m_bc
                    last_ev = 0
                    for k in range(16):
                        r.wait("pe", ve_s, max(m_bc, last_ev))
                        r.op("pe", pe_s, lambda e, k=k: e.matmul(
                            psT7[0:16, 0:128], dBbf[:, k * 16:(k + 1) * 16],
                            idbf[:, :], start=True, stop=True))
                        m_t = pe_s.n
                        r.wait("ve", pe_s, m_t)
                        last_ev = r.op("ve", ve_s, lambda e, e_i=e_i, k=k: e.tensor_copy(
                            dBTP[e_i // 3][(e_i % 3) * 32:(e_i % 3) * 32 + 16, k * 128:(k + 1) * 128],
                            psT7[0:16, 0:128]))
                    m_dbt = last_ev
                    m_dbbf_rd = pe_s.n
                free_hb = [0, 0]
                prev_t7_sc = 0
                m_ac7 = 0
                m_datbf_rd = 0
                for e_i in range(E):
                    r.wait("sy", ve_s, m_ac7)
                    r.wait("sy", pe_s, m_datbf_rd)
                    m_a = r.dma("sy", d_a7, dATst[:, :], dr_downA[e_i, :, :])
                    r.wait("ve", d_a7, m_a)
                    r.wait("ve", pe_s, m_datbf_rd)
                    m_ac = r.op("ve", ve_s, lambda e: e.tensor_copy(dATbf[:, :], dATst[:, :]))
                    m_ac7 = m_ac
                    last_ev = 0
                    for k in range(8):
                        r.wait("pe", ve_s, max(m_ac, last_ev, m_dbt))
                        r.wait("pe", sc_s, prev_t7_sc)
                        r.op("pe", pe_s, lambda e, k=k: e.matmul(
                            psT7[:, 0:16], dATbf[0:16, k * 128:(k + 1) * 128],
                            idbf[0:16, 0:16], start=True, stop=True))
                        m_t = pe_s.n
                        r.wait("ve", pe_s, m_t)
                        last_ev = r.op("ve", ve_s, lambda e, k=k: e.tensor_copy(
                            dAT[:, k * 16:(k + 1) * 16], psT7[:, 0:16]))
                    m_dat = last_ev
                    m_datbf_rd = pe_s.n
                    for t in range(4):
                        hp = t % 2
                        r.wait("sy", pe_s, free_hb[hp])
                        r.wait("sy", d_h[0], m_gup_done[0])
                        r.wait("sy", d_h[1], m_gup_done[1])
                        m_hb = r.dma("sy", d_hbs[hp],
                                     hbb[:, hp * 8192: hp * 8192 + 4096].rearrange("p (ic t) -> p ic t", t=512),
                                     dr_h[e_i, :, t * 512:(t + 1) * 512].rearrange("(ic p) t -> p ic t", p=128))
                        r.wait("pe", d_hbs[hp], m_hb)
                        r.wait("pe", ve_s, m_dat)
                        r.wait("pe", sc_s, prev_t7_sc)
                        for k in range(8):
                            r.op("pe", pe_s, lambda e, k=k, hp=hp: e.matmul(
                                psT7[0:16, 0:512], dAT[:, k * 16:(k + 1) * 16],
                                hbb[:, hp * 8192 + k * 512: hp * 8192 + (k + 1) * 512],
                                start=(k == 0), stop=(k == 7)))
                        m_tm = pe_s.n
                        free_hb[hp] = m_tm
                        r.wait("sc", pe_s, m_tm)
                        prev_t7_sc = r.op("sc", sc_s, lambda e, e_i=e_i, t=t: e.activation(
                            ter2P[e_i // 3][(e_i % 3) * 32:(e_i % 3) * 32 + 16, t * 512:(t + 1) * 512],
                            psT7[0:16, 0:512], AF.Copy, scale=SCALING))
                m_ter2_sc = prev_t7_sc
                ps7_es.close()
                psF = _ph.enter_context(nc.psum_tensor("psF", [128, 8 * 512], F32))
                free_hbb_init = pe_s.n
                wo3 = dr_wout.rearrange("(kc p) f -> p kc f", p=128)
                dn3 = dr_down.rearrange("e (ic p) f -> e p ic f", p=128)
                free_bank = [0] * 8
                free_wo = 0
                free_dn = [0, 0]
                free_hbb = [free_hbb_init, free_hbb_init]
                free_xres = [0, 0]
                free_osb = [0, 0]
                free_wo_par = [0, 0]
                m_dnc_last = 0
                m_woc = 0
                for f in range(4):
                    fpar = f % 2
                    r.wait("sy", pe_s, free_wo_par[fpar])
                    for k in range(16):
                        wp = k % 2
                        r.wait("sy", ve_s, m_woc)
                        m_wd = r.dma("sy", d_wo[wp], wost[:, :],
                                     wo3[:, k, f * 512:(f + 1) * 512])
                        r.wait("ve", d_wo[wp], m_wd)
                        m_woc = r.op("ve", ve_s, lambda e, k=k, fpar=fpar: e.tensor_copy(
                            wobf[:, fpar * 8192 + k * 512: fpar * 8192 + (k + 1) * 512],
                            wost[:, :]))
                    m_wo_ready = m_woc
                    for th in range(2):
                        for e_i in range(E):
                            ep = e_i % 2
                            r.wait("sy", pe_s, free_dn[ep])
                            r.wait("sy", ve_s, m_dnc_last)
                            m_dd = r.dma("sy", d_dn[ep],
                                         dnst.ap().rearrange("p (k c) -> p k c", c=512),
                                         dn3[e_i, :, :, f * 512:(f + 1) * 512])
                            r.wait("ve", d_dn[ep], m_dd)
                            m_dnc = r.op("ve", ve_s, lambda e, ep=ep: e.tensor_copy(
                                dnbf[:, ep * 4096:(ep + 1) * 4096], dnst[:, :]))
                            m_dnc_last = m_dnc
                            r.wait("gp", pe_s, free_hbb[ep])
                            r.wait("gp", d_h[0], m_gup_done[0])
                            r.wait("gp", d_h[1], m_gup_done[1])
                            m_hh = r.dma("gp", d_hb[ep],
                                         hbb[:, ep * 8192:(ep + 1) * 8192].rearrange("p (ic t) -> p ic t", t=1024),
                                         dr_h[e_i, :, th * 1024:(th + 1) * 1024].rearrange("(ic p) t -> p ic t", p=128))
                            for t in range(8):
                                tt = th * 8 + t
                                if e_i == 0:
                                    r.wait("pe", ve_s, max(m_wo_ready, free_bank[t], m_attn_ve))
                                    r.wait("pe", sc_s, m_ter2_sc)
                                    for k in range(16):
                                        r.opn("pe", lambda e, k=k, tt=tt, t=t, fpar=fpar: e.matmul(
                                            psF[:, t * 512:(t + 1) * 512],
                                            QO[:, k * T + tt * 128: k * T + (tt + 1) * 128],
                                            wobf[:, fpar * 8192 + k * 512: fpar * 8192 + (k + 1) * 512],
                                            start=(k == 0), stop=False))
                                r.wait("pe", ve_s, m_dnc)
                                r.wait("pe", d_hb[ep], m_hh)
                                r.wait("pe", sc_s, m_ter2_sc)
                                for k in range(8):
                                    r.opn("pe", lambda e, k=k, t=t, ep=ep: e.matmul(
                                        psF[:, t * 512:(t + 1) * 512],
                                        hbb[:, ep * 8192 + k * 1024 + t * 128: ep * 8192 + k * 1024 + (t + 1) * 128],
                                        dnbf[:, ep * 4096 + k * 512: ep * 4096 + (k + 1) * 512],
                                        start=False, stop=False))
                                r.op("pe", pe_s, lambda e, e_i=e_i, tt=tt, t=t, f=f: e.matmul(
                                    psF[:, t * 512:(t + 1) * 512],
                                    ter2P[e_i // 3][(e_i % 3) * 32:(e_i % 3) * 32 + 16, tt * 128:(tt + 1) * 128],
                                    dBTP[e_i // 3][(e_i % 3) * 32:(e_i % 3) * 32 + 16, f * 512:(f + 1) * 512],
                                    start=False, stop=(e_i == E - 1)))
                                m_pk = pe_s.n
                                if e_i == E - 1:
                                    xp = t % 2
                                    r.wait("gp", ve_s, free_xres[xp])
                                    m_xr = r.dma("gp", d_xr[xp], xres[:, xp * 512:(xp + 1) * 512],
                                                 dr_x[tt * 128:(tt + 1) * 128, f * 512:(f + 1) * 512])
                                    r.wait("ve", pe_s, m_pk)
                                    r.wait("ve", d_xr[xp], m_xr)
                                    r.wait("ve", d_ot[xp], free_osb[xp])
                                    m_ov = r.op("ve", ve_s, lambda e, t=t, xp=xp: e.tensor_tensor(
                                        osb[:, xp * 512:(xp + 1) * 512], psF[:, t * 512:(t + 1) * 512],
                                        xres[:, xp * 512:(xp + 1) * 512], OP.add))
                                    free_bank[t] = m_ov
                                    free_xres[xp] = m_ov
                                    r.wait("gp", ve_s, m_ov)
                                    m_os = r.dma("gp", d_ot[xp],
                                                 dr_out[tt * 128:(tt + 1) * 128, f * 512:(f + 1) * 512],
                                                 osb[:, xp * 512:(xp + 1) * 512])
                                    free_osb[xp] = m_os
                            free_dn[ep] = pe_s.n
                            free_hbb[ep] = pe_s.n
                        free_wo_par[fpar] = pe_s.n
            r.wait("gp", d_ot[0], d_ot[0].n)
            r.wait("gp", d_ot[1], d_ot[1].n)
        else:
            r.wait("gp", d_sp, d_sp.n)

        r.emit(block)
    return nc



def _host_consts(core):
    cl = core % 4
    idbf = np.eye(128, dtype=ml_dtypes.bfloat16)
    idf = np.eye(128, dtype=np.float32)
    mask = np.zeros((128, 4 * 512), np.float32)
    for mi in range(4):
        o = mi * 128
        p = np.arange(128)[:, None]
        q = np.arange(512)[None, :]
        mask[:, mi * 512:(mi + 1) * 512] = np.where(q >= p + o, 0.0, NEG)
    cfg = np.zeros((1, 64), np.int32)
    chA, chB = cl, 7 - cl
    units = [(0, chA), (1, chB)]
    for j in range(chA):
        units.append((0, j))
    for j in range(chB):
        units.append((1, j))
    assert len(units) == 9
    for u, (qc, j) in enumerate(units):
        rk = j if j < 4 else 7 - j
        lci = 0 if j < 4 else 1
        cfg[0, u * 4 + 0] = qc
        cfg[0, u * 4 + 1] = j
    cfg[0, 40] = cl
    return {"c_idbf": idbf, "c_idf": idf, "c_mask": mask, "c_cfg": cfg}


def _shard_x(x, core):
    cl = core % 4
    b = core // 4
    chA, chB = cl, 7 - cl
    return np.ascontiguousarray(np.concatenate(
        [x[b, chA * 1024:(chA + 1) * 1024], x[b, chB * 1024:(chB + 1) * 1024]], axis=0))


def kernel(L0_attn_x, L0_attn_w_qkv, L0_attn_w_out, L0_ffn_gate_weight,
           L0_ffn_gate_lora_A, L0_ffn_gate_lora_B, L0_ffn_gate_up_proj,
           L0_ffn_down_proj, L0_ffn_gup_lora_A, L0_ffn_gup_lora_B,
           L0_ffn_down_lora_A, L0_ffn_down_lora_B):
    x = np.asarray(L0_attn_x, np.float32)
    weights = {
        "wqkv": np.ascontiguousarray(np.asarray(L0_attn_w_qkv, np.float32)),
        "wout": np.ascontiguousarray(np.asarray(L0_attn_w_out, np.float32)),
        "gatew": np.ascontiguousarray(np.asarray(L0_ffn_gate_weight, np.float32)),
        "gateA": np.ascontiguousarray(np.asarray(L0_ffn_gate_lora_A, np.float32)),
        "gateB": np.ascontiguousarray(np.asarray(L0_ffn_gate_lora_B, np.float32)),
        "gup": np.ascontiguousarray(np.asarray(L0_ffn_gate_up_proj, np.float32)),
        "down": np.ascontiguousarray(np.asarray(L0_ffn_down_proj, np.float32)),
        "gupA": np.ascontiguousarray(np.asarray(L0_ffn_gup_lora_A, np.float32)),
        "gupB": np.ascontiguousarray(np.asarray(L0_ffn_gup_lora_B, np.float32)),
        "downA": np.ascontiguousarray(np.asarray(L0_ffn_down_lora_A, np.float32)),
        "downB": np.ascontiguousarray(np.asarray(L0_ffn_down_lora_B, np.float32)),
    }
    nc = build(taps=TAPS, phases=PHASES)
    in_maps = []
    for c in range(8):
        m = dict(weights)
        m["x"] = _shard_x(x, c)
        m.update(_host_consts(c))
        in_maps.append(m)
    res = run_bass_kernel_spmd(nc, in_maps, core_ids=list(range(8)))
    out = np.zeros((2, 8192, H), np.float32)
    for c in range(8):
        cl, b = c % 4, c // 4
        chA, chB = cl, 7 - cl
        o = res.results[c]["out"]
        out[b, chA * 1024:(chA + 1) * 1024] = o[0:1024]
        out[b, chB * 1024:(chB + 1) * 1024] = o[1024:2048]
    return out


# version 05
# speedup vs baseline: 1.1290x; 1.0024x over previous
import numpy as np
import ml_dtypes
from contextlib import ExitStack

import concourse.bass as bass
import concourse.mybir as mybir
from concourse.bass_utils import run_bass_kernel_spmd

F32 = mybir.dt.float32
BF16 = mybir.dt.bfloat16
I32 = mybir.dt.int32
AF = mybir.ActivationFunctionType
OP = mybir.AluOpType
AX = mybir.AxisListType.X
ds = bass.ds

H = 2048
NHEAD = 16
DH = 128
T = 2048
E = 8
II = 1024
R = 16
SCALING = 2.0
NEG = -1e9
EPS = 1e-5
ISQD = float(1.0 / np.sqrt(DH))
GROUPS = [[0, 1, 2, 3], [4, 5, 6, 7]]

PHASES = 7
TAPS = ()


def _diag_status(s, qh):
    if s * 128 + 127 <= qh * 512:
        return "o"
    if s * 128 > qh * 512 + 511:
        return "c"
    return "p"


class Sem:
    def __init__(self, nc, es, name):
        self.h = es.enter_context(nc.semaphore(name))
        self.n = 0


class Rec:

    def __init__(self):
        self.ops = {k: [] for k in ("sy", "pe", "ve", "sc", "gp")}

    def raw(self, eng, fn):
        self.ops[eng].append(fn)

    def wait(self, eng, sem, val):
        if val > 0:
            self.ops[eng].append(lambda e, s=sem, v=val: e.wait_ge(s.h, v))

    def op(self, eng, sem, fn):
        def g(e, fn=fn, sem=sem):
            fn(e).then_inc(sem.h, 1)
        self.ops[eng].append(g)
        sem.n += 1
        return sem.n

    def opn(self, eng, fn):
        self.ops[eng].append(fn)

    def barrier(self, bar, drains):
        for eng, sems in drains.items():
            for sm in sems:
                self.wait(eng, sm, sm.n)
        for eng in ("sy", "pe", "ve", "sc", "gp"):
            def g(e, bar=bar):
                e.nop().then_inc(bar.h, 1)
            self.ops[eng].append(g)
            bar.n += 1
        tgt = bar.n
        for eng in ("sy", "pe", "ve", "sc", "gp"):
            self.wait(eng, bar, tgt)

    def vsync(self, eng, sem):
        def g(e, sem=sem):
            e.nop().then_inc(sem.h, 1)
        self.ops[eng].append(g)
        sem.n += 1
        self.wait(eng, sem, sem.n)

    def dma(self, eng, sem, out, in_, **kw):
        def g(e, out=out, in_=in_, kw=kw, sem=sem):
            e.dma_start(out=out, in_=in_, **kw).then_inc(sem.h, 16)
        self.ops[eng].append(g)
        sem.n += 16
        return sem.n

    def emit(self, block):
        ops = self.ops

        @block.sync
        def _(e):
            for fn in ops["sy"]:
                fn(e)

        @block.tensor
        def _(e):
            for fn in ops["pe"]:
                fn(e)

        @block.vector
        def _(e):
            for fn in ops["ve"]:
                fn(e)

        @block.scalar
        def _(e):
            for fn in ops["sc"]:
                fn(e)

        @block.gpsimd
        def _(e):
            for fn in ops["gp"]:
                fn(e)


def build(taps=(), phases=7):
    nc = bass.Bass("TRN2", target_bir_lowering=False, debug=False, detect_race_conditions=False)
    es = ExitStack()

    dr_x = nc.dram_tensor("x", [T, H], F32, kind="ExternalInput").ap()
    dr_wqkv = nc.dram_tensor("wqkv", [H, 3 * H], F32, kind="ExternalInput").ap()
    dr_wout = nc.dram_tensor("wout", [H, H], F32, kind="ExternalInput").ap()
    dr_gw = nc.dram_tensor("gatew", [E, H], F32, kind="ExternalInput").ap()
    dr_gA = nc.dram_tensor("gateA", [R, H], F32, kind="ExternalInput").ap()
    dr_gB = nc.dram_tensor("gateB", [E, R], F32, kind="ExternalInput").ap()
    dr_gup = nc.dram_tensor("gup", [E, H, 2 * II], F32, kind="ExternalInput").ap()
    dr_down = nc.dram_tensor("down", [E, II, H], F32, kind="ExternalInput").ap()
    dr_gupA = nc.dram_tensor("gupA", [E, R, H], F32, kind="ExternalInput").ap()
    dr_gupB = nc.dram_tensor("gupB", [E, 2 * II, R], F32, kind="ExternalInput").ap()
    dr_downA = nc.dram_tensor("downA", [E, R, II], F32, kind="ExternalInput").ap()
    dr_downB = nc.dram_tensor("downB", [E, H, R], F32, kind="ExternalInput").ap()
    dr_idbf = nc.dram_tensor("c_idbf", [128, 128], BF16, kind="ExternalInput").ap()
    dr_idf = nc.dram_tensor("c_idf", [128, 128], F32, kind="ExternalInput").ap()
    dr_mask = nc.dram_tensor("c_mask", [128, 4 * 512], F32, kind="ExternalInput").ap()
    dr_cfg = nc.dram_tensor("c_cfg", [1, 64], I32, kind="ExternalInput").ap()
    dr_out = nc.dram_tensor("out", [T, H], F32, kind="ExternalOutput").ap()

    dr_ck = nc.dram_tensor("contribK", [H, T], BF16, kind="Internal").ap()
    dr_cv = nc.dram_tensor("contribV", [T, H], BF16, kind="Internal").ap()
    dr_gk = nc.dram_tensor("gatheredK", [4 * H, T], BF16, kind="Internal").ap()
    dr_gv = nc.dram_tensor("gatheredV", [4 * T, H], BF16, kind="Internal").ap()
    dr_h = nc.dram_tensor("hspill", [E, II, T], BF16, kind="Internal").ap()
    dr_wr = nc.dram_tensor("wrout", [E, T], F32, kind="Internal").ap()
    dr_kni = nc.dram_tensor("kn_in", [1, NHEAD], F32, kind="Internal").ap()
    dr_kno = nc.dram_tensor("kn_out", [1, NHEAD], F32, kind="Internal").ap()

    tap_aps = {}
    if taps:
        tshapes = {"xn": ([128, 16 * T], BF16), "qo": ([128, NHEAD * T], BF16),
                   "gk": ([4 * H, T], BF16), "gv": ([4 * T, H], BF16),
                   "h": ([E, II, T], BF16), "w": ([E, T], F32),
                   "negc": ([128, 32], F32), "lnst": ([128, 16], F32)}
        for t in taps:
            sh, dt = tshapes[t]
            tap_aps[t] = nc.dram_tensor("tap_" + t, sh, dt, kind="ExternalOutput").ap()

    m_xn_done = 0
    m_qkv_pe = 0
    m_qkv_ve = 0
    m_wsp = 0
    m_gup_done = (0, 0)
    m_attn_ve = 0

    with es:
        sbt = lambda name, shape, dtype: es.enter_context(nc.sbuf_tensor(name, shape, dtype))
        xn_es = ExitStack()
        xn = xn_es.enter_context(nc.sbuf_tensor("xn", [128, 16 * T], BF16, side="right"))
        QO = sbt("QO", [128, NHEAD * T], BF16)
        idbf = sbt("idbf", [128, 128], BF16)
        idf = sbt("idf", [128, 128], F32)
        cfgs = sbt("cfgs", [1, 64], I32)
        ones_col = sbt("ones_col", [128, 1], F32)
        ones_col_bf = sbt("ones_col_bf", [128, 1], BF16)
        ones_row = sbt("ones_row", [1, 128], F32)
        zero_bf = sbt("zero_bf", [128, 512], BF16)
        eps_t = sbt("eps_t", [128, 1], F32)
        qn2 = sbt("qn2", [1, 32], F32)
        kn2 = sbt("kn2", [1, NHEAD], F32)
        negc_rep = sbt("negc_rep", [128, 32], F32)
        w45_es = ExitStack()
        wT8 = w45_es.enter_context(nc.sbuf_tensor("wT8", [E, T], F32, side="right"))

        S = lambda name: Sem(nc, es, name)
        d_in = S("d_in")
        d_x = [S("d_x0"), S("d_x1")]
        d_w2 = [S("d_w20"), S("d_w21")]
        d_kst = [S("d_kst0"), S("d_kst1")]
        d_vtt = [S("d_vtt0"), S("d_vtt1")]
        d_g4 = S("d_g4")
        d_a5 = S("d_a5")
        d_b5 = S("d_b5")
        d_w1 = S("d_w1")
        d_w5 = [S("d_w50"), S("d_w51")]
        d_h = [S("d_h0"), S("d_h1")]
        d_kv = [S("d_kv0"), S("d_kv1")]
        d_kn6 = S("d_kn6")
        d_b7 = S("d_b7")
        d_a7 = S("d_a7")
        d_hbs = [S("d_hbs0"), S("d_hbs1")]
        d_wo = [S("d_wo0"), S("d_wo1")]
        d_dn = [S("d_dn0"), S("d_dn1")]
        d_hb = [S("d_hb0"), S("d_hb1")]
        d_xr = [S("d_xr0"), S("d_xr1")]
        d_ot = [S("d_ot0"), S("d_ot1")]
        d_sp = S("d_sp")
        c_k = S("c_k")
        c_v = S("c_v")
        c_nn = S("c_nn")
        pe_s = S("pe_s")
        ve_s = S("ve_s")
        sc_s = S("sc_s")
        gp_s = S("gp_s")
        bar = S("bar")

        def phase_barrier():
            r.barrier(bar, {
                "sy": [d_w2[0], d_w2[1], d_kv[0], d_kv[1], d_g4, d_a5, d_b5,
                       d_w1, d_w5[0], d_w5[1], d_hbs[0], d_hbs[1], d_wo[0],
                       d_wo[1], d_dn[0], d_dn[1], d_kn6, d_b7, d_a7],
                "gp": [d_in, d_x[0], d_x[1], d_kst[0], d_kst[1], d_vtt[0],
                       d_vtt[1], d_sp, d_h[0], d_h[1], d_hb[0], d_hb[1],
                       d_xr[0], d_xr[1], d_ot[0], d_ot[1]],
            })

        rg_sy = [es.enter_context(nc.sync.register(f"rs{i}")) for i in range(3)]
        rg_pe = es.enter_context(nc.tensor.register("rt0"))
        rg_ve = es.enter_context(nc.vector.register("rv0"))
        rg_sc = es.enter_context(nc.scalar.register("rc0"))
        rg_gp = es.enter_context(nc.gpsimd.register("rg0"))

        block = es.enter_context(nc.Block())
        r = Rec()

        r.dma("gp", d_in, idbf[:, :], dr_idbf)
        r.dma("gp", d_in, idf[:, :], dr_idf)
        m_consts = r.dma("gp", d_in, cfgs[:, :], dr_cfg)
        r.opn("ve", lambda e: e.memset(ones_col[:, :], 1.0))
        r.opn("ve", lambda e: e.memset(ones_col_bf[:, :], 1.0))
        r.opn("ve", lambda e: e.memset(ones_row[:, :], 1.0))
        r.opn("ve", lambda e: e.memset(zero_bf[:, :], 0.0))
        r.opn("ve", lambda e: e.memset(eps_t[:, :], EPS))
        r.opn("ve", lambda e: e.memset(qn2[:, :], 0.0))
        m_vec_consts = r.op("ve", ve_s, lambda e: e.memset(kn2[:, :], 0.0))

        with ExitStack() as _ph:
            xbuf = _ph.enter_context(nc.sbuf_tensor("xbuf", [128, 2 * H], F32))
            junk = _ph.enter_context(nc.sbuf_tensor("junk", [128, H], BF16))
            lnst = _ph.enter_context(nc.sbuf_tensor("lnst", [128, 16], F32))
            xntm = _ph.enter_context(nc.sbuf_tensor("xntm", [128, 2 * H], BF16))
            lntmp = _ph.enter_context(nc.sbuf_tensor("lntmp", [128, H], F32))
            lnac = _ph.enter_context(nc.sbuf_tensor("lnac", [128, 1], F32))
            psLN = _ph.enter_context(nc.psum_tensor("psLN", [128, 1024], F32))
            r.opn("ve", lambda e: e.memset(lnst[:, :], 0.0))
            free_xb = [0, 0]
            free_half = [0, 0]
            prev_var = 0
            xn3 = xn.ap().rearrange("p (k t) -> p k t", t=T)
            ps3 = psLN.ap().rearrange("p (g c) -> p g c", c=128)
            for t in range(16):
                par = t % 2
                xb = xbuf[:, par * H:(par + 1) * H]
                xt = xntm[:, par * H:(par + 1) * H]
                r.wait("gp", ve_s, free_xb[par])
                m_x = r.dma("gp", d_x[par], xb, dr_x[t * 128:(t + 1) * 128, :])
                r.wait("ve", d_x[par], m_x)
                m_s1 = r.op("ve", ve_s, lambda e, xb=xb: e.reduce_sum(lnst[:, 0:1], xb, axis=AX))
                r.wait("sc", d_x[par], m_x)
                r.wait("sc", ve_s, m_s1)
                r.opn("sc", lambda e: e.activation(lnst[:, 1:2], lnst[:, 0:1], AF.Copy, scale=1.0 / H))
                m_sq = r.op("sc", sc_s, lambda e, xb=xb: e.activation(
                    junk[:, :], xb, AF.Square, accum_out=lnac[:, 0:1]))
                r.wait("ve", sc_s, m_sq)
                r.vsync("ve", ve_s)
                r.opn("ve", lambda e: e.tensor_scalar(lnst[:, 9:10], lnst[:, 1:2], lnst[:, 0:1], None, OP.mult))
                r.vsync("ve", ve_s)
                m_var = r.op("ve", ve_s, lambda e: e.tensor_scalar(
                    lnst[:, 5:6], lnac[:, 0:1], lnst[:, 9:10], None, OP.subtract))
                prev_var = m_var
                r.wait("sc", ve_s, m_var)
                m_sd = r.op("sc", sc_s, lambda e: e.activation(
                    lnst[:, 6:7], lnst[:, 5:6], AF.Sqrt, scale=1.0 / H, bias=eps_t[:, 0:1]))
                r.wait("ve", sc_s, m_sd)
                r.opn("ve", lambda e: e.reciprocal(lnst[:, 7:8], lnst[:, 6:7]))
                r.opn("ve", lambda e: e.tensor_scalar(lnst[:, 8:9], lnst[:, 1:2], -1.0, None, OP.mult))
                r.vsync("ve", ve_s)
                r.opn("ve", lambda e, xb=xb: e.tensor_scalar(
                    lntmp[:, :], xb, lnst[:, 8:9], None, OP.add))
                m_xnt = r.op("ve", ve_s, lambda e, xt=xt: e.tensor_scalar(
                    xt, lntmp[:, :], lnst[:, 7:8], None, OP.mult))
                free_xb[par] = m_xnt
                for g in range(4):
                    half = g % 2
                    r.wait("pe", ve_s, max(m_xnt, free_half[half], m_vec_consts))
                    r.wait("pe", d_in, m_consts)
                    for kk in range(4):
                        k = g * 4 + kk
                        r.op("pe", pe_s, lambda e, xt=xt, k=k, half=half, kk=kk: e.matmul(
                            psLN[:, half * 512 + kk * 128: half * 512 + (kk + 1) * 128],
                            xt[:, k * 128:(k + 1) * 128], idbf[:, :],
                            start=True, stop=True))
                    m_tr = pe_s.n
                    r.wait("ve", pe_s, m_tr)
                    m_ev = r.op("ve", ve_s, lambda e, g=g, t=t, half=half: e.tensor_copy(
                        xn3[:, g * 4:(g + 1) * 4, t * 128:(t + 1) * 128],
                        ps3[:, half * 4:(half + 1) * 4, :]))
                    free_half[half] = m_ev
            m_xn_done = ve_s.n
            if "lnst" in tap_aps:
                r.wait("gp", ve_s, m_xn_done)
                r.dma("gp", d_sp, tap_aps["lnst"], lnst[:, :])

        if "xn" in tap_aps:
            r.wait("gp", ve_s, m_xn_done)
            r.dma("gp", d_sp, tap_aps["xn"], xn[:, :])

        phase_barrier()

        if phases >= 2:
            with ExitStack() as _ph:
                wst = _ph.enter_context(nc.sbuf_tensor("wst", [128, 2 * H], F32))
                wbf = _ph.enter_context(nc.sbuf_tensor("wbf", [128, 2 * H], BF16))
                kst = _ph.enter_context(nc.sbuf_tensor("kst", [128, 2 * 512], BF16))
                vst = _ph.enter_context(nc.sbuf_tensor("vst", [128, 2 * 512], BF16))
                vtt = _ph.enter_context(nc.sbuf_tensor("vtt", [128, 2 * 512], BF16))
                sqb = _ph.enter_context(nc.sbuf_tensor("sqb", [128, 512], BF16))
                nrm = _ph.enter_context(nc.sbuf_tensor("nrm", [1, 8], F32))
                nsb = _ph.enter_context(nc.sbuf_tensor("nsb", [1, 2048], F32))
                psQ = _ph.enter_context(nc.psum_tensor("psQ", [128, 4 * 512], F32))
                psN = _ph.enter_context(nc.psum_tensor("psN", [128, 1024], F32))
                psVT = _ph.enter_context(nc.psum_tensor("psVT", [128, 512], F32))
                wq3 = dr_wqkv.rearrange("(kc p) m -> p kc m", p=128)
                free_w = [0, 0]
                free_kst = [0, 0]
                free_vst = [0, 0]
                free_vtt = [0, 0]
                free_psq = [0, 0, 0, 0]
                free_psn = [0, 0]
                free_sqb = 0
                free_psvt = 0
                for m in range(48):
                    par = m % 2
                    r.wait("sy", pe_s, free_w[par])
                    m_w = r.dma("sy", d_w2[par],
                                wst[:, par * H:(par + 1) * H].rearrange("p (kc m) -> p kc m", m=128),
                                wq3[:, :, m * 128:(m + 1) * 128])
                    r.wait("ve", d_w2[par], m_w)
                    m_cv = r.op("ve", ve_s, lambda e, par=par: e.tensor_copy(
                        wbf[:, par * H:(par + 1) * H], wst[:, par * H:(par + 1) * H]))
                    last_mm = 0
                    for t in range(4):
                        r.wait("pe", ve_s, max(m_cv, m_xn_done))
                        r.wait("pe", sc_s, free_psq[t])
                        for k in range(16):
                            r.op("pe", pe_s, lambda e, par=par, k=k, t=t: e.matmul(
                                psQ[:, t * 512:(t + 1) * 512],
                                wbf[:, par * H + k * 128: par * H + (k + 1) * 128],
                                xn[:, k * T + t * 512: k * T + (t + 1) * 512],
                                start=(k == 0), stop=(k == 15)))
                        m_mm = pe_s.n
                        last_mm = m_mm
                        if m < 16:
                            hq = m
                            r.wait("sc", pe_s, m_mm)
                            m_ev = r.op("sc", sc_s, lambda e, t=t, hq=hq: e.activation(
                                QO[:, hq * T + t * 512: hq * T + (t + 1) * 512],
                                psQ[:, t * 512:(t + 1) * 512], AF.Copy, scale=ISQD))
                            free_psq[t] = m_ev
                            r.wait("sc", pe_s, free_sqb)
                            m_sq = r.op("sc", sc_s, lambda e, t=t, hq=hq: e.activation(
                                sqb[:, :], QO[:, hq * T + t * 512: hq * T + (t + 1) * 512], AF.Square))
                            nh = t % 2
                            r.wait("pe", sc_s, m_sq)
                            r.wait("pe", ve_s, free_psn[nh])
                            m_on = r.op("pe", pe_s, lambda e, nh=nh: e.matmul(
                                psN[0:1, nh * 512:(nh + 1) * 512], ones_col_bf[:, :], sqb[:, :],
                                start=True, stop=True))
                            free_sqb = m_on
                            r.wait("ve", pe_s, m_on)
                            m_nm = r.op("ve", ve_s, lambda e, nh=nh, t=t: e.tensor_copy(
                                nsb[0:1, t * 512:(t + 1) * 512], psN[0:1, nh * 512:(nh + 1) * 512]))
                            free_psn[nh] = m_nm
                            if t == 3:
                                r.opn("ve", lambda e: e.reduce_max(nrm[0:1, 0:1], nsb[0:1, 0:1024], axis=AX))
                                m_nm2 = r.op("ve", ve_s, lambda e: e.reduce_max(nrm[0:1, 1:2], nsb[0:1, 1024:2048], axis=AX))
                                r.wait("sc", ve_s, m_nm2)
                                r.opn("sc", lambda e, hq=hq: e.activation(qn2[0:1, hq * 2: hq * 2 + 1], nrm[0:1, 0:1], AF.Copy))
                                m_qn = r.op("sc", sc_s, lambda e, hq=hq: e.activation(qn2[0:1, hq * 2 + 1: hq * 2 + 2], nrm[0:1, 1:2], AF.Copy))
                                r.wait("ve", sc_s, m_qn)
                                r.op("ve", ve_s, lambda e: e.nop())
                        elif m < 32:
                            hk = m - 16
                            kp = t % 2
                            r.wait("sc", pe_s, m_mm)
                            r.wait("sc", d_kst[kp], free_kst[kp])
                            m_ev = r.op("sc", sc_s, lambda e, t=t, kp=kp: e.activation(
                                kst[:, kp * 512:(kp + 1) * 512], psQ[:, t * 512:(t + 1) * 512], AF.Copy))
                            free_psq[t] = m_ev
                            r.wait("gp", sc_s, m_ev)
                            m_kd = r.dma("gp", d_kst[kp],
                                         dr_ck[hk * 128:(hk + 1) * 128, t * 512:(t + 1) * 512],
                                         kst[:, kp * 512:(kp + 1) * 512])
                            free_kst[kp] = m_kd
                            r.wait("sc", pe_s, free_sqb)
                            m_sq = r.op("sc", sc_s, lambda e, kp=kp: e.activation(
                                sqb[:, :], kst[:, kp * 512:(kp + 1) * 512], AF.Square))
                            nh = t % 2
                            r.wait("pe", sc_s, m_sq)
                            r.wait("pe", ve_s, free_psn[nh])
                            m_on = r.op("pe", pe_s, lambda e, nh=nh: e.matmul(
                                psN[0:1, nh * 512:(nh + 1) * 512], ones_col_bf[:, :], sqb[:, :],
                                start=True, stop=True))
                            free_sqb = m_on
                            r.wait("ve", pe_s, m_on)
                            m_nm = r.op("ve", ve_s, lambda e, nh=nh, t=t: e.tensor_copy(
                                nsb[0:1, t * 512:(t + 1) * 512], psN[0:1, nh * 512:(nh + 1) * 512]))
                            free_psn[nh] = m_nm
                            if t == 3:
                                m_nm2 = r.op("ve", ve_s, lambda e: e.reduce_max(nrm[0:1, 2:3], nsb[0:1, 0:2048], axis=AX))
                                r.wait("sc", ve_s, m_nm2)
                                m_kn = r.op("sc", sc_s, lambda e, hk=hk: e.activation(kn2[0:1, hk:hk + 1], nrm[0:1, 2:3], AF.Copy))
                                r.wait("ve", sc_s, m_kn)
                                r.op("ve", ve_s, lambda e: e.nop())
                        else:
                            hv = m - 32
                            vp = t % 2
                            r.wait("sc", pe_s, m_mm)
                            r.wait("sc", pe_s, free_vst[vp])
                            m_ev = r.op("sc", sc_s, lambda e, t=t, vp=vp: e.activation(
                                vst[:, vp * 512:(vp + 1) * 512], psQ[:, t * 512:(t + 1) * 512], AF.Copy))
                            free_psq[t] = m_ev
                            r.wait("pe", sc_s, m_ev)
                            r.wait("pe", ve_s, free_psvt)
                            for i in range(4):
                                r.op("pe", pe_s, lambda e, vp=vp, i=i: e.matmul(
                                    psVT[:, i * 128:(i + 1) * 128],
                                    vst[:, vp * 512 + i * 128: vp * 512 + (i + 1) * 128],
                                    idbf[:, :], start=True, stop=True))
                            m_tr = pe_s.n
                            free_vst[vp] = m_tr
                            r.wait("ve", pe_s, m_tr)
                            r.wait("ve", d_vtt[vp], free_vtt[vp])
                            m_vv = r.op("ve", ve_s, lambda e, vp=vp: e.tensor_copy(
                                vtt[:, vp * 512:(vp + 1) * 512], psVT[:, :]))
                            free_psvt = m_vv
                            r.wait("gp", ve_s, m_vv)
                            m_vd = r.dma("gp", d_vtt[vp],
                                         dr_cv[t * 512:(t + 1) * 512, hv * 128:(hv + 1) * 128]
                                         .rearrange("(i p) d -> p i d", p=128),
                                         vtt[:, vp * 512:(vp + 1) * 512].rearrange("p (i d) -> p i d", d=128))
                            free_vtt[vp] = m_vd
                    free_w[par] = last_mm
                m_qkv_pe = pe_s.n
                m_qkv_ve = ve_s.n
            r.wait("gp", ve_s, m_qkv_ve)
            r.dma("gp", d_sp, dr_kni, kn2[:, :])

        if phases >= 3:
            r.wait("gp", d_sp, d_sp.n)
            r.wait("gp", d_kst[0], d_kst[0].n)
            r.wait("gp", d_kst[1], d_kst[1].n)
            r.wait("gp", d_vtt[0], d_vtt[0].n)
            r.wait("gp", d_vtt[1], d_vtt[1].n)
            r.op("gp", c_k, lambda e: e.collective_compute(
                "AllGather", OP.bypass, replica_groups=GROUPS,
                ins=[dr_ck], outs=[dr_gk]))
            r.op("gp", c_v, lambda e: e.collective_compute(
                "AllGather", OP.bypass, replica_groups=GROUPS,
                ins=[dr_cv], outs=[dr_gv]))
            r.op("gp", c_nn, lambda e: e.collective_compute(
                "AllReduce", OP.max, replica_groups=GROUPS,
                ins=[dr_kni], outs=[dr_kno]))

        if phases >= 4:
            phase_barrier()

        if phases >= 4:
            with ExitStack() as _ph:
                gst = _ph.enter_context(nc.sbuf_tensor("gst", [16, 2 * H], F32))
                gbst = _ph.enter_context(nc.sbuf_tensor("gbst", [8, 16], F32))
                gbf = _ph.enter_context(nc.sbuf_tensor("gbf", [16, 2 * H], BF16))
                gbbf = _ph.enter_context(nc.sbuf_tensor("gbbf", [8, 16], BF16))
                wgT = _ph.enter_context(nc.sbuf_tensor("wgT", [128, 16 * 8], BF16))
                agT = _ph.enter_context(nc.sbuf_tensor("agT", [128, 16 * 16], BF16))
                bgT = _ph.enter_context(nc.sbuf_tensor("bgT", [16, 8], BF16))
                rterg = _ph.enter_context(nc.sbuf_tensor("rterg", [16, 512], BF16))
                lgsb = _ph.enter_context(nc.sbuf_tensor("lgsb", [8, 512], F32))
                rst = _ph.enter_context(nc.sbuf_tensor("rst", [128, 96], F32))
                wtm = _ph.enter_context(nc.sbuf_tensor("wtm", [128, 8], F32))
                psR = _ph.enter_context(nc.psum_tensor("psR", [128, 1024], F32))
                psR2 = _ph.enter_context(nc.psum_tensor("psR2", [128, 512], F32))
                r.dma("sy", d_g4, gst[0:8, 0:H], dr_gw)
                r.dma("sy", d_g4, gst[0:16, H:2 * H], dr_gA)
                m_gl = r.dma("sy", d_g4, gbst[:, :], dr_gB)
                r.wait("ve", d_g4, m_gl)
                r.opn("ve", lambda e: e.tensor_copy(gbf[0:8, 0:H], gst[0:8, 0:H]))
                r.opn("ve", lambda e: e.tensor_copy(gbf[0:16, H:2 * H], gst[0:16, H:2 * H]))
                m_gc = r.op("ve", ve_s, lambda e: e.tensor_copy(gbbf[:, :], gbst[:, :]))
                last_ev = [0, 0]
                r.wait("pe", ve_s, m_gc)
                for k in range(16):
                    pp = k % 2
                    r.wait("pe", ve_s, last_ev[pp])
                    r.op("pe", pe_s, lambda e, k=k, pp=pp: e.matmul(
                        psR[:, pp * 512: pp * 512 + 8],
                        gbf[0:8, k * 128:(k + 1) * 128], idbf[0:8, 0:8],
                        start=True, stop=True))
                    m_t = pe_s.n
                    r.wait("ve", pe_s, m_t)
                    last_ev[pp] = r.op("ve", ve_s, lambda e, k=k, pp=pp: e.tensor_copy(
                        wgT[:, k * 8:(k + 1) * 8], psR[:, pp * 512: pp * 512 + 8]))
                for k in range(16):
                    pp = k % 2
                    r.wait("pe", ve_s, last_ev[pp])
                    r.op("pe", pe_s, lambda e, k=k, pp=pp: e.matmul(
                        psR[:, pp * 512: pp * 512 + 16],
                        gbf[0:16, H + k * 128: H + (k + 1) * 128], idbf[0:16, 0:16],
                        start=True, stop=True))
                    m_t = pe_s.n
                    r.wait("ve", pe_s, m_t)
                    last_ev[pp] = r.op("ve", ve_s, lambda e, k=k, pp=pp: e.tensor_copy(
                        agT[:, k * 16:(k + 1) * 16], psR[:, pp * 512: pp * 512 + 16]))
                r.wait("pe", ve_s, last_ev[0])
                r.op("pe", pe_s, lambda e: e.matmul(psR[0:16, 0:8], gbbf[:, :], idbf[0:8, 0:8],
                                               start=True, stop=True))
                m_t3 = pe_s.n
                r.wait("ve", pe_s, m_t3)
                m_bgT = r.op("ve", ve_s, lambda e: e.tensor_copy(bgT[:, :], psR[0:16, 0:8]))

                prev_psR_sc = 0
                prev_psR_ve = m_bgT
                for t in range(4):
                    r.wait("pe", ve_s, prev_psR_ve)
                    r.wait("pe", sc_s, prev_psR_sc)
                    for k in range(16):
                        r.op("pe", pe_s, lambda e, k=k, t=t: e.matmul(
                            psR[0:16, 0:512], agT[:, k * 16:(k + 1) * 16],
                            xn[:, k * T + t * 512: k * T + (t + 1) * 512],
                            start=(k == 0), stop=(k == 15)))
                    m_ter = pe_s.n
                    r.wait("sc", pe_s, m_ter)
                    m_terg = r.op("sc", sc_s, lambda e: e.activation(
                        rterg[:, :], psR[0:16, 0:512], AF.Copy, scale=SCALING))
                    r.wait("pe", sc_s, m_terg)
                    for k in range(16):
                        r.opn("pe", lambda e, k=k, t=t: e.matmul(
                            psR2[0:8, 0:512], wgT[:, k * 8:(k + 1) * 8],
                            xn[:, k * T + t * 512: k * T + (t + 1) * 512],
                            start=(k == 0), stop=False))
                    r.op("pe", pe_s, lambda e: e.matmul(psR2[0:8, 0:512], bgT[:, :], rterg[:, :],
                                                   start=False, stop=True))
                    m_lg = pe_s.n
                    r.wait("ve", pe_s, m_lg)
                    m_lgs = r.op("ve", ve_s, lambda e: e.tensor_copy(lgsb[:, :], psR2[0:8, 0:512]))
                    r.wait("pe", ve_s, m_lgs)
                    for i in range(4):
                        r.op("pe", pe_s, lambda e, i=i: e.matmul(
                            psR[:, 512 + i * 128: 512 + i * 128 + 8],
                            lgsb[0:8, i * 128:(i + 1) * 128],
                            idf[0:8, 0:8], start=True, stop=True))
                    m_lt = pe_s.n
                    for i in range(4):
                        sl = psR[:, 512 + i * 128: 512 + i * 128 + 8]
                        r.wait("ve", pe_s, m_lt)
                        m_nm = r.op("ve", ve_s, lambda e, sl=sl: e.reduce_max(
                            rst[:, 0:1], sl, axis=AX, negate=True))
                        r.wait("sc", ve_s, m_nm)
                        m_px = r.op("sc", sc_s, lambda e, sl=sl: e.activation(
                            rst[:, 8:16], sl, AF.Exp, bias=rst[:, 0:1], accum_out=rst[:, 1:2]))
                        prev_psR_sc = m_px
                        r.wait("ve", sc_s, m_px)
                        r.opn("ve", lambda e: e.reciprocal(rst[:, 2:3], rst[:, 1:2]))
                        r.vsync("ve", ve_s)
                        m_p = r.op("ve", ve_s, lambda e: e.tensor_scalar(rst[:, 16:24], rst[:, 8:16], rst[:, 2:3], None, OP.mult))
                        r.wait("sc", ve_s, m_p)
                        m_pc = r.op("sc", sc_s, lambda e: e.activation(rst[:, 48:56], rst[:, 16:24], AF.Copy))
                        r.wait("ve", sc_s, m_pc)
                        r.opn("ve", lambda e: e.reduce_max(rst[:, 24:25], rst[:, 48:56], axis=AX))
                        r.vsync("ve", ve_s)
                        m_pm = r.op("ve", ve_s, lambda e: e.scalar_tensor_tensor(
                            rst[:, 32:40], rst[:, 48:56], rst[:, 24:25], rst[:, 48:56], OP.is_lt, OP.mult))
                        r.wait("sc", ve_s, m_pm)
                        r.opn("sc", lambda e: e.activation(rst[:, 56:64], rst[:, 32:40], AF.Copy))
                        m_m1c = r.op("sc", sc_s, lambda e: e.activation(rst[:, 28:29], rst[:, 24:25], AF.Copy))
                        r.wait("ve", sc_s, m_m1c)
                        r.opn("ve", lambda e: e.reduce_max(rst[:, 25:26], rst[:, 56:64], axis=AX))
                        r.vsync("ve", ve_s)
                        m_den = r.op("ve", ve_s, lambda e: e.tensor_scalar(rst[:, 26:27], rst[:, 28:29], rst[:, 25:26], None, OP.add))
                        r.wait("sc", ve_s, m_den)
                        m_dc = r.op("sc", sc_s, lambda e: e.activation(rst[:, 29:30], rst[:, 26:27], AF.Copy))
                        r.wait("ve", sc_s, m_dc)
                        r.opn("ve", lambda e: e.reciprocal(rst[:, 27:28], rst[:, 29:30]))
                        r.vsync("ve", ve_s)
                        m_sel = r.op("ve", ve_s, lambda e: e.scalar_tensor_tensor(
                            rst[:, 40:48], rst[:, 48:56], rst[:, 25:26], rst[:, 48:56], OP.is_ge, OP.mult))
                        r.wait("sc", ve_s, m_sel)
                        m_slc = r.op("sc", sc_s, lambda e: e.activation(rst[:, 64:72], rst[:, 40:48], AF.Copy))
                        r.wait("ve", sc_s, m_slc)
                        m_wtm = r.op("ve", ve_s, lambda e: e.tensor_scalar(
                            wtm[:, 0:8], rst[:, 64:72], rst[:, 27:28], None, OP.mult))
                        prev_psR_ve = m_wtm
                        r.wait("pe", ve_s, m_wtm)
                        r.op("pe", pe_s, lambda e: e.matmul(psR2[0:8, 0:128], wtm[:, 0:8], idf[:, :],
                                                       start=True, stop=True))
                        m_wt = pe_s.n
                        r.wait("ve", pe_s, m_wt)
                        m_w8 = r.op("ve", ve_s, lambda e, t=t, i=i: e.tensor_copy(
                            wT8[:, t * 512 + i * 128: t * 512 + (i + 1) * 128], psR2[0:8, 0:128]))
                        r.wait("pe", ve_s, m_w8)
                m_rout_ve = ve_s.n
            r.wait("gp", ve_s, m_rout_ve)
            m_wsp = r.dma("gp", d_sp, dr_wr, wT8[:, :])
            if "w" in tap_aps:
                r.wait("gp", d_sp, m_wsp)
                r.dma("gp", d_sp, tap_aps["w"], dr_wr)

        if phases >= 5:
            phase_barrier()

        if phases >= 5:
            with ExitStack() as _ph:
                agupT = _ph.enter_context(nc.sbuf_tensor("agupT", [128, 256], BF16))
                gast = _ph.enter_context(nc.sbuf_tensor("gast", [16, H // 2], F32))
                gabf = _ph.enter_context(nc.sbuf_tensor("gabf", [16, H], BF16))
                gbst2 = _ph.enter_context(nc.sbuf_tensor("gbst2", [128, 256], F32))
                gbbf2 = _ph.enter_context(nc.sbuf_tensor("gbbf2", [128, 256], BF16))
                bgupT = _ph.enter_context(nc.sbuf_tensor("bgupT", [16, H], BF16))
                gter = _ph.enter_context(nc.sbuf_tensor("gter", [16, T], BF16))
                w1r = _ph.enter_context(nc.sbuf_tensor("w1r", [1, 512], F32))
                wrep = _ph.enter_context(nc.sbuf_tensor("wrep", [128, 4 * 512], BF16))
                gwst = _ph.enter_context(nc.sbuf_tensor("gwst", [128, 4096], F32))
                gwbf = _ph.enter_context(nc.sbuf_tensor("gwbf", [128, 2 * 4096], BF16))
                sgx = _ph.enter_context(nc.sbuf_tensor("sgx", [128, 2 * 512], BF16))
                upx = _ph.enter_context(nc.sbuf_tensor("upx", [128, 2 * 512], BF16))
                hst = _ph.enter_context(nc.sbuf_tensor("hst", [128, 2 * 512], BF16))
                psT5 = _ph.enter_context(nc.psum_tensor("psT5", [128, 512], F32))
                psW5 = _ph.enter_context(nc.psum_tensor("psW5", [128, 512], F32))
                psG = _ph.enter_context(nc.psum_tensor("psG", [128, 2 * 512], F32))
                psU5 = _ph.enter_context(nc.psum_tensor("psU5", [128, 2 * 512], F32))
                prev_t5_sc = 0
                m_prev_gabf_rd = 0
                m_prev_gbbf_rd = 0
                gw3 = dr_gup.rearrange("e (kc p) m -> e p kc m", p=128)
                free_gw = [0, 0]
                m_gw_conv = 0
                free_psg = [0, 0]
                free_psu = [0, 0]
                free_sgx = [0, 0]
                free_upx = [0, 0]
                free_hst = [0, 0]
                m_bc_prev = 0
                for e_i in range(E):
                    m_ac = 0
                    for hf in range(2):
                        r.wait("sy", ve_s, m_ac)
                        r.wait("sy", pe_s, m_prev_gabf_rd)
                        m_a = r.dma("sy", d_a5, gast[:, :], dr_gupA[e_i, :, hf * (H // 2):(hf + 1) * (H // 2)])
                        r.wait("ve", d_a5, m_a)
                        m_ac = r.op("ve", ve_s, lambda e, hf=hf: e.tensor_copy(
                            gabf[:, hf * (H // 2):(hf + 1) * (H // 2)], gast[:, :]))
                    m_agupT = 0
                    for k in range(16):
                        r.wait("pe", ve_s, max(m_ac, m_agupT))
                        r.wait("pe", sc_s, prev_t5_sc)
                        r.op("pe", pe_s, lambda e, k=k: e.matmul(
                            psT5[:, 0:16], gabf[0:16, k * 128:(k + 1) * 128],
                            idbf[0:16, 0:16], start=True, stop=True))
                        m_t = pe_s.n
                        r.wait("ve", pe_s, m_t)
                        m_agupT = r.op("ve", ve_s, lambda e, k=k: e.tensor_copy(
                            agupT[:, k * 16:(k + 1) * 16], psT5[:, 0:16]))
                    m_prev_gabf_rd = pe_s.n
                    r.wait("sy", ve_s, m_bc_prev)
                    r.wait("sy", pe_s, m_prev_gbbf_rd)
                    m_b = r.dma("sy", d_b5, gbst2.ap().rearrange("p (c r) -> p c r", r=16),
                                dr_gupB[e_i, :, :].rearrange("(c p) r -> p c r", p=128))
                    r.wait("ve", d_b5, m_b)
                    r.wait("ve", pe_s, m_prev_gbbf_rd)
                    m_bc = r.op("ve", ve_s, lambda e: e.tensor_copy(gbbf2[:, :], gbst2[:, :]))
                    m_bc_prev = m_bc
                    last_ev = 0
                    for k in range(16):
                        r.wait("pe", ve_s, max(m_bc, last_ev))
                        r.op("pe", pe_s, lambda e, k=k: e.matmul(
                            psT5[0:16, 0:128], gbbf2[:, k * 16:(k + 1) * 16],
                            idbf[:, :], start=True, stop=True))
                        m_t = pe_s.n
                        r.wait("ve", pe_s, m_t)
                        last_ev = r.op("ve", ve_s, lambda e, k=k: e.tensor_copy(
                            bgupT[0:16, k * 128:(k + 1) * 128], psT5[0:16, 0:128]))
                    m_prev_gbbf_rd = pe_s.n
                    r.wait("sy", d_sp, m_wsp)
                    last_wr = 0
                    last_w1_rd = 0
                    for t in range(4):
                        r.wait("sy", pe_s, last_w1_rd)
                        m_w1 = r.dma("sy", d_w1, w1r[:, :], dr_wr[e_i:e_i + 1, t * 512:(t + 1) * 512])
                        r.wait("pe", d_w1, m_w1)
                        r.wait("pe", ve_s, max(last_wr, last_ev))
                        r.op("pe", pe_s, lambda e: e.matmul(
                            psW5[:, 0:512], ones_row[:, :], w1r[0:1, 0:512],
                            start=True, stop=True))
                        m_t = pe_s.n
                        last_w1_rd = m_t
                        r.wait("ve", pe_s, m_t)
                        last_wr = r.op("ve", ve_s, lambda e, t=t: e.tensor_copy(
                            wrep[:, t * 512:(t + 1) * 512], psW5[:, 0:512]))
                    m_wrep = last_wr
                    last_terg = prev_t5_sc
                    for t in range(4):
                        r.wait("pe", sc_s, last_terg)
                        for k in range(16):
                            r.op("pe", pe_s, lambda e, k=k, t=t: e.matmul(
                                psT5[0:16, 0:512], agupT[:, k * 16:(k + 1) * 16],
                                xn[:, k * T + t * 512: k * T + (t + 1) * 512],
                                start=(k == 0), stop=(k == 15)))
                        m_tm = pe_s.n
                        r.wait("sc", pe_s, m_tm)
                        last_terg = r.op("sc", sc_s, lambda e, t=t: e.activation(
                            gter[:, t * 512:(t + 1) * 512], psT5[0:16, 0:512], AF.Copy, scale=SCALING))
                    prev_t5_sc = last_terg
                    m_gter = last_terg
                    for mp in range(8):
                        par = mp % 2
                        gpar = (e_i * 8 + mp) % 2
                        r.wait("sy", pe_s, free_gw[par])
                        r.wait("sy", ve_s, m_gw_conv)
                        r.dma("sy", d_w5[gpar],
                              gwst[:, 0:2048].rearrange("p (kc m) -> p kc m", m=128),
                              gw3[e_i, :, :, mp * 128:(mp + 1) * 128])
                        m_w2 = r.dma("sy", d_w5[gpar],
                                     gwst[:, 2048:4096].rearrange("p (kc m) -> p kc m", m=128),
                                     gw3[e_i, :, :, (mp + 8) * 128:(mp + 9) * 128])
                        r.wait("ve", d_w5[gpar], m_w2)
                        m_wc = r.op("ve", ve_s, lambda e, par=par: e.tensor_copy(
                            gwbf[:, par * 4096:(par + 1) * 4096], gwst[:, :]))
                        m_gw_conv = m_wc
                        for t in range(4):
                            tp = t % 2
                            r.wait("pe", ve_s, max(m_wc, m_wrep))
                            r.wait("pe", sc_s, max(m_gter, free_psg[tp]))
                            for k in range(16):
                                r.opn("pe", lambda e, par=par, k=k, t=t, tp=tp: e.matmul(
                                    psG[:, tp * 512:(tp + 1) * 512],
                                    gwbf[:, par * 4096 + k * 128: par * 4096 + (k + 1) * 128],
                                    xn[:, k * T + t * 512: k * T + (t + 1) * 512],
                                    start=(k == 0), stop=False))
                            r.op("pe", pe_s, lambda e, mp=mp, t=t, tp=tp: e.matmul(
                                psG[:, tp * 512:(tp + 1) * 512],
                                bgupT[0:16, mp * 128:(mp + 1) * 128],
                                gter[:, t * 512:(t + 1) * 512], start=False, stop=True))
                            m_gmm = pe_s.n
                            r.wait("pe", ve_s, free_psu[tp])
                            for k in range(16):
                                r.opn("pe", lambda e, par=par, k=k, t=t, tp=tp: e.matmul(
                                    psU5[:, tp * 512:(tp + 1) * 512],
                                    gwbf[:, par * 4096 + 2048 + k * 128: par * 4096 + 2048 + (k + 1) * 128],
                                    xn[:, k * T + t * 512: k * T + (t + 1) * 512],
                                    start=(k == 0), stop=False))
                            r.op("pe", pe_s, lambda e, mp=mp, t=t, tp=tp: e.matmul(
                                psU5[:, tp * 512:(tp + 1) * 512],
                                bgupT[0:16, (mp + 8) * 128:(mp + 9) * 128],
                                gter[:, t * 512:(t + 1) * 512], start=False, stop=True))
                            m_umm = pe_s.n
                            r.wait("sc", pe_s, m_gmm)
                            r.wait("sc", ve_s, free_sgx[tp])
                            m_sg = r.op("sc", sc_s, lambda e, tp=tp: e.activation(
                                sgx[:, tp * 512:(tp + 1) * 512], psG[:, tp * 512:(tp + 1) * 512], AF.Silu))
                            free_psg[tp] = m_sg
                            r.wait("ve", pe_s, m_umm)
                            r.wait("ve", ve_s, free_upx[tp])
                            m_ux = r.op("ve", ve_s, lambda e, t=t, tp=tp: e.tensor_tensor(
                                upx[:, tp * 512:(tp + 1) * 512], psU5[:, tp * 512:(tp + 1) * 512],
                                wrep[:, t * 512:(t + 1) * 512], OP.mult))
                            free_psu[tp] = m_ux
                            r.wait("ve", sc_s, m_sg)
                            r.wait("ve", d_h[tp], free_hst[tp])
                            m_h = r.op("ve", ve_s, lambda e, tp=tp: e.tensor_tensor(
                                hst[:, tp * 512:(tp + 1) * 512], sgx[:, tp * 512:(tp + 1) * 512],
                                upx[:, tp * 512:(tp + 1) * 512], OP.mult))
                            free_sgx[tp] = m_h
                            free_upx[tp] = m_h
                            r.wait("gp", ve_s, m_h)
                            m_hd = r.dma("gp", d_h[tp],
                                         dr_h[e_i, mp * 128:(mp + 1) * 128, t * 512:(t + 1) * 512],
                                         hst[:, tp * 512:(tp + 1) * 512])
                            free_hst[tp] = m_hd
                        free_gw[par] = pe_s.n
            m_gup_done = (d_h[0].n, d_h[1].n)
            if "h" in tap_aps:
                r.wait("gp", d_h[0], m_gup_done[0])
                r.wait("gp", d_h[1], m_gup_done[1])
                r.dma("gp", d_sp, tap_aps["h"], dr_h)

        if phases >= 6:
            phase_barrier()
        w45_es.close()
        xn_es.close()

        if phases >= 6:
            with ExitStack() as _ph:
                kbuf = _ph.enter_context(nc.sbuf_tensor("kbuf", [128, 2 * 1024], BF16))
                vbuf = _ph.enter_context(nc.sbuf_tensor("vbuf", [128, 2 * 1024], BF16))
                ptb = _ph.enter_context(nc.sbuf_tensor("ptb", [128, 4 * 1024], BF16))
                smask = _ph.enter_context(nc.sbuf_tensor("smask", [128, 2 * 512], F32))
                masks = _ph.enter_context(nc.sbuf_tensor("masks", [128, 4 * 512], F32))
                ssum = _ph.enter_context(nc.sbuf_tensor("ssum", [128, 2 * 2048], F32))
                ssumC = _ph.enter_context(nc.sbuf_tensor("ssumC", [128, 2 * 2048], F32))
                knrow = _ph.enter_context(nc.sbuf_tensor("knrow", [1, NHEAD], F32))
                kdup = _ph.enter_context(nc.sbuf_tensor("kdup", [1, 32], F32))
                crow = _ph.enter_context(nc.sbuf_tensor("crow", [1, 32], F32))
                csq = _ph.enter_context(nc.sbuf_tensor("csq", [1, 32], F32))
                rrow = _ph.enter_context(nc.sbuf_tensor("rrow", [1, 1024], F32))
                rrep = _ph.enter_context(nc.sbuf_tensor("rrep", [128, 1024], F32))
                psS0 = _ph.enter_context(nc.psum_tensor("psS0", [128, 1024], F32))
                psS1 = _ph.enter_context(nc.psum_tensor("psS1", [128, 1024], F32))
                psUa = _ph.enter_context(nc.psum_tensor("psUa", [128, 2048], F32))
                psS = [psS0, psS1]
                m_masks = r.dma("gp", d_in, masks[:, :], dr_mask)
                r.wait("sy", c_nn, 1)
                m_kn = r.dma("sy", d_kn6, knrow[:, :], dr_kno)
                kd3 = kdup.ap().rearrange("o (hh two) -> o hh two", two=2)
                kn3 = knrow.ap().rearrange("o (h u) -> o h u", u=1)
                r.wait("sc", d_kn6, m_kn)
                r.opn("sc", lambda e: e.activation(kd3[:, :, 0:1], kn3, AF.Copy))
                m_kd = r.op("sc", sc_s, lambda e: e.activation(kd3[:, :, 1:2], kn3, AF.Copy))
                r.wait("ve", sc_s, m_kd)
                r.wait("ve", ve_s, m_qkv_ve)
                m_cp = r.op("ve", ve_s, lambda e: e.tensor_tensor(csq[:, :], qn2[:, :], kdup[:, :], OP.mult))
                r.wait("sc", ve_s, m_cp)
                m_cr = r.op("sc", sc_s, lambda e: e.activation(crow[:, :], csq[:, :], AF.Sqrt))
                r.wait("ve", sc_s, m_cr)
                m_ng = r.op("ve", ve_s, lambda e: e.tensor_scalar(csq[:, :], crow[:, :], -1.0, None, OP.mult))
                r.wait("pe", ve_s, m_ng)
                r.wait("pe", pe_s, m_qkv_pe)
                r.op("pe", pe_s, lambda e: e.matmul(psS0[:, 0:32], ones_row[:, :], csq[0:1, :],
                                               start=True, stop=True))
                m_nr = pe_s.n
                r.wait("ve", pe_s, m_nr)
                m_negc = r.op("ve", ve_s, lambda e: e.tensor_copy(negc_rep[:, :], psS0[:, 0:32]))
                if "negc" in tap_aps:
                    r.wait("gp", ve_s, m_negc)
                    r.dma("gp", d_sp, tap_aps["negc"], negc_rep[:, :])
                r.wait("pe", ve_s, m_negc)
                r.wait("sy", c_k, 1)
                r.wait("sy", c_v, 1)
                r.wait("ve", d_in, m_masks)
                r.wait("sc", ve_s, m_negc)

                gv3 = dr_gv.rearrange("(s p) d -> p s d", p=128)

                def emit_attn_variant(rv, cl):
                    units = [(0, cl), (1, 7 - cl)]
                    for jj in range(cl):
                        units.append((0, jj))
                    for jj in range(7 - cl):
                        units.append((1, jj))
                    kv_pe_done = {}
                    g_seq = 0
                    for h_i in range(NHEAD):
                        rv.wait("pe", ve_s, ve_s.n)
                        for j4 in range(4):
                            rv.op("pe", pe_s, lambda e, j4=j4: e.matmul(
                                psUa[:, j4 * 512:(j4 + 1) * 512], zero_bf[:, 0:128], zero_bf[:, :],
                                start=True, stop=False))
                        pe_s.n
                        m_ssz = rv.op("ve", ve_s, lambda e: e.memset(ssum[:, 0:2048], 0.0))
                        rv.wait("gp", pe_s, pe_s.n)
                        rv.op("gp", gp_s, lambda e: e.memset(ssumC[:, 0:2048], 0.0))
                        m_exp_prev = [0, 0]
                        m_psS_ve = [0, 0]
                        m_pt_pe = [0, 0, 0, 0]
                        m_pt_ve = [0, 0, 0, 0]
                        m_pt_gp = [0, 0, 0, 0]
                        m_smask_rd = [0, 0]
                        cnt = {("A", 0): 0, ("A", 1): 0, ("C", 0): 0, ("C", 1): 0}
                        for u in range(9):
                            qc, jch = units[u]
                            up = g_seq % 2
                            rk = jch if jch < 4 else 7 - jch
                            lci = 0 if jch < 4 else 1
                            rv.wait("sy", pe_s, kv_pe_done.get(g_seq - 2, 0))
                            rv.dma("sy", d_kv[up],
                                   kbuf[:, up * 1024:(up + 1) * 1024],
                                   dr_gk[rk * H + h_i * 128: rk * H + (h_i + 1) * 128,
                                         lci * 1024:(lci + 1) * 1024])
                            vs = (rk * T + lci * 1024) // 128
                            m_kvd = rv.dma("sy", d_kv[up],
                                           vbuf[:, up * 1024:(up + 1) * 1024].rearrange("p (s d) -> p s d", d=128),
                                           gv3[:, vs:vs + 8, h_i * 128:(h_i + 1) * 128])
                            nbias = negc_rep[:, h_i * 2 + qc: h_i * 2 + qc + 1]
                            diag = u < 2
                            pending = []

                            def emit_av(rv=rv, up=up, qc=qc):
                                sss, p4, segs, m_exp = pending.pop(0)
                                rv.wait("pe", sc_s, m_exp if m_exp else sc_s.n)
                                for qh in range(2):
                                    if segs[qh] == "c":
                                        continue
                                    rv.op("pe", pe_s, lambda e, sss=sss, qh=qh, p4=p4: e.matmul(
                                        psUa[:, qc * 1024 + qh * 512: qc * 1024 + (qh + 1) * 512],
                                        vbuf[:, up * 1024 + sss * 128: up * 1024 + (sss + 1) * 128],
                                        ptb[:, p4 * 1024 + qh * 512: p4 * 1024 + (qh + 1) * 512],
                                        start=False, stop=False))
                                m_pt_pe[p4] = pe_s.n

                            for sss in range(8):
                                p2 = sss % 2
                                p4 = sss % 4
                                segs = [_diag_status(sss, qh) for qh in range(2)] if diag else ["f", "f"]
                                rv.wait("pe", d_kv[up], m_kvd)
                                rv.wait("pe", sc_s, m_exp_prev[p2])
                                rv.wait("pe", ve_s, m_psS_ve[p2])
                                for qh in range(2):
                                    if segs[qh] == "c":
                                        continue
                                    rv.op("pe", pe_s, lambda e, sss=sss, qh=qh, up=up, p2=p2, h_i=h_i, qc=qc: e.matmul(
                                        psS[p2][:, qh * 512:(qh + 1) * 512],
                                        kbuf[:, up * 1024 + sss * 128: up * 1024 + (sss + 1) * 128],
                                        QO[:, h_i * T + qc * 1024 + qh * 512: h_i * T + qc * 1024 + (qh + 1) * 512],
                                        start=True, stop=True))
                                m_smm = pe_s.n
                                if pending:
                                    emit_av()
                                m_msk = 0
                                if diag:
                                    for qh in range(2):
                                        if segs[qh] != "p":
                                            continue
                                        mi = (sss * 128 - qh * 512) // 128
                                        rv.wait("ve", pe_s, m_smm)
                                        rv.wait("ve", sc_s, m_smask_rd[qh])
                                        m_msk = rv.op("ve", ve_s, lambda e, p2=p2, qh=qh, mi=mi: e.tensor_tensor(
                                            smask[:, qh * 512:(qh + 1) * 512],
                                            psS[p2][:, qh * 512:(qh + 1) * 512],
                                            masks[:, mi * 512:(mi + 1) * 512], OP.add))
                                        m_psS_ve[p2] = m_msk
                                    for qh in range(2):
                                        if segs[qh] == "c":
                                            rv.wait("ve", pe_s, m_pt_pe[p4])
                                            if m_pt_gp[p4]:
                                                rv.wait("ve", gp_s, m_pt_gp[p4])
                                            m_msk = rv.op("ve", ve_s, lambda e, p4=p4, qh=qh: e.memset(
                                                ptb[:, p4 * 1024 + qh * 512: p4 * 1024 + (qh + 1) * 512], 0.0))
                                rv.wait("sc", pe_s, max(m_smm, m_pt_pe[p4]))
                                rv.wait("sc", ve_s, max(m_msk, m_pt_ve[p4]))
                                if m_pt_gp[p4]:
                                    rv.wait("sc", gp_s, m_pt_gp[p4])
                                if not diag:
                                    m_exp = rv.op("sc", sc_s, lambda e, p2=p2, p4=p4, nbias=nbias: e.activation(
                                        ptb[:, p4 * 1024:(p4 + 1) * 1024], psS[p2][:, :],
                                        AF.Exp, bias=nbias))
                                    m_exp_prev[p2] = m_exp
                                else:
                                    m_exp = 0
                                    for qh in range(2):
                                        if segs[qh] == "c":
                                            continue
                                        if segs[qh] == "p":
                                            m_exp = rv.op("sc", sc_s, lambda e, p4=p4, qh=qh, nbias=nbias: e.activation(
                                                ptb[:, p4 * 1024 + qh * 512: p4 * 1024 + (qh + 1) * 512],
                                                smask[:, qh * 512:(qh + 1) * 512], AF.Exp, bias=nbias))
                                            m_smask_rd[qh] = m_exp
                                        else:
                                            m_exp = rv.op("sc", sc_s, lambda e, p2=p2, p4=p4, qh=qh, nbias=nbias: e.activation(
                                                ptb[:, p4 * 1024 + qh * 512: p4 * 1024 + (qh + 1) * 512],
                                                psS[p2][:, qh * 512:(qh + 1) * 512], AF.Exp, bias=nbias))
                                            m_exp_prev[p2] = m_exp
                                ch = "A" if sss in (0, 2, 4, 6, 7) else "C"
                                aeng = "ve" if ch == "A" else "gp"
                                asem = ve_s if ch == "A" else gp_s
                                abuf = ssum if ch == "A" else ssumC
                                c0 = cnt[(ch, qc)]
                                spp, dpp = c0 % 2, (c0 + 1) % 2
                                cnt[(ch, qc)] = c0 + 1
                                rv.wait(aeng, sc_s, m_exp if m_exp else sc_s.n)
                                if ch == "C" and diag and m_msk:
                                    rv.wait("gp", ve_s, m_msk)
                                if u == 0 and sss == 0:
                                    rv.wait("ve", ve_s, m_ssz)
                                rv.opn(aeng, lambda e, spp=spp, dpp=dpp, p4=p4, qc=qc, abuf=abuf: e.tensor_tensor(
                                    abuf[:, dpp * 2048 + qc * 1024: dpp * 2048 + (qc + 1) * 1024],
                                    abuf[:, spp * 2048 + qc * 1024: spp * 2048 + (qc + 1) * 1024],
                                    ptb[:, p4 * 1024:(p4 + 1) * 1024], OP.add))
                                m_ssa = rv.op(aeng, asem, lambda e: e.nop())
                                if ch == "A":
                                    m_pt_ve[p4] = m_ssa
                                else:
                                    m_pt_gp[p4] = m_ssa
                                pending.append((sss, p4, segs, m_exp))
                            while pending:
                                emit_av()
                            kv_pe_done[g_seq] = pe_s.n
                            g_seq += 1
                        for j4 in range(4):
                            if j4 == 3:
                                rv.op("pe", pe_s, lambda e, j4=j4: e.matmul(
                                    psUa[:, j4 * 512:(j4 + 1) * 512], zero_bf[:, 0:128], zero_bf[:, :],
                                    start=False, stop=True))
                            else:
                                rv.opn("pe", lambda e, j4=j4: e.matmul(
                                    psUa[:, j4 * 512:(j4 + 1) * 512], zero_bf[:, 0:128], zero_bf[:, :],
                                    start=False, stop=True))
                        m_last_exp = sc_s.n
                        m_last_ssa = ve_s.n
                        m_last_av = pe_s.n
                        m_last_gadd = gp_s.n
                        for qc in range(2):
                            ppf = qc
                            pA = (cnt[("A", qc)] % 2) * 2048
                            pC = (cnt[("C", qc)] % 2) * 2048
                            rv.wait("pe", sc_s, m_last_exp)
                            rv.wait("pe", ve_s, max(m_last_ssa, m_psS_ve[ppf]))
                            rv.wait("pe", gp_s, m_last_gadd)
                            for qh in range(2):
                                rv.opn("pe", lambda e, qc=qc, qh=qh, ppf=ppf, pA=pA: e.matmul(
                                    psS[ppf][0:1, qh * 512:(qh + 1) * 512], ones_col[:, :],
                                    ssum[:, pA + qc * 1024 + qh * 512: pA + qc * 1024 + (qh + 1) * 512],
                                    start=True, stop=False))
                                rv.op("pe", pe_s, lambda e, qc=qc, qh=qh, ppf=ppf, pC=pC: e.matmul(
                                    psS[ppf][0:1, qh * 512:(qh + 1) * 512], ones_col[:, :],
                                    ssumC[:, pC + qc * 1024 + qh * 512: pC + qc * 1024 + (qh + 1) * 512],
                                    start=False, stop=True))
                            m_sr = pe_s.n
                            rv.wait("ve", pe_s, m_sr)
                            m_rr = rv.op("ve", ve_s, lambda e, ppf=ppf: e.reciprocal(
                                rrow[0:1, :], psS[ppf][0:1, :]))
                            rv.wait("pe", ve_s, m_rr)
                            for qh in range(2):
                                rv.op("pe", pe_s, lambda e, qh=qh, ppf=ppf: e.matmul(
                                    psS[ppf][:, qh * 512:(qh + 1) * 512], ones_row[:, :],
                                    rrow[0:1, qh * 512:(qh + 1) * 512], start=True, stop=True))
                            m_rp = pe_s.n
                            rv.wait("ve", pe_s, m_rp)
                            m_rrep = rv.op("ve", ve_s, lambda e, ppf=ppf: e.tensor_copy(
                                rrep[:, :], psS[ppf][:, :]))
                            rv.wait("ve", pe_s, m_last_av)
                            rv.op("ve", ve_s, lambda e, h_i=h_i, qc=qc: e.tensor_tensor(
                                QO[:, h_i * T + qc * 1024: h_i * T + (qc + 1) * 1024],
                                psUa[:, qc * 1024:(qc + 1) * 1024], rrep[:, :], OP.mult))

                all_sems = [d_in, d_kv[0], d_kv[1], d_sp, c_k, c_v, c_nn, pe_s, ve_s, sc_s, gp_s]
                snap0 = {id(sm): sm.n for sm in all_sems}
                variants = []
                final0 = None
                for v in range(4):
                    for sm in all_sems:
                        sm.n = snap0[id(sm)]
                    rvv = Rec()
                    emit_attn_variant(rvv, v)
                    fin = {id(sm): sm.n for sm in all_sems}
                    if final0 is None:
                        final0 = fin
                    else:
                        assert fin == final0, "attention variants diverge in sem counts"
                    variants.append(rvv)
                for sm in all_sems:
                    sm.n = final0[id(sm)]
                eng_regs = {"sy": rg_sy[0], "pe": rg_pe, "ve": rg_ve, "sc": rg_sc, "gp": rg_gp}
                for eng in ("sy", "pe", "ve", "sc", "gp"):
                    bodies = [variants[v].ops[eng] for v in range(4)]
                    r.wait(eng, d_in, m_consts)
                    def sw(e, reg=eng_regs[eng], bodies=bodies):
                        e.reg_load(reg, cfgs[0:1, 40:41])
                        idx = e.snap(reg, min_val=0, max_val=3)
                        for jv in nc.Switch(engines=[e], index=[idx], n=4):
                            for fn in bodies[jv]:
                                fn(e)
                    r.raw(eng, sw)
            m_attn_ve = ve_s.n
            if "qo" in tap_aps:
                r.wait("gp", ve_s, m_attn_ve)
                r.dma("gp", d_sp, tap_aps["qo"], QO[:, :])
        if "qo" in tap_aps and phases < 6:
            r.wait("gp", sc_s, sc_s.n)
            r.dma("gp", d_sp, tap_aps["qo"], QO[:, :])
        if "gk" in tap_aps and phases >= 3:
            r.wait("gp", c_k, 1)
            r.dma("gp", d_sp, tap_aps["gk"], dr_gk)
        if "gv" in tap_aps and phases >= 3:
            r.wait("gp", c_v, 1)
            r.dma("gp", d_sp, tap_aps["gv"], dr_gv)

        if phases >= 7:
            phase_barrier()

        if phases >= 7:
            with ExitStack() as _ph:
                dATst = _ph.enter_context(nc.sbuf_tensor("dATst", [16, II], F32))
                ter2P = [_ph.enter_context(nc.sbuf_tensor(f"ter2_{j}", [128, T], BF16)) for j in range(3)]
                dBTP = [_ph.enter_context(nc.sbuf_tensor(f"dBT_{j}", [128, H], BF16)) for j in range(3)]
                dATbf = _ph.enter_context(nc.sbuf_tensor("dATbf", [16, II], BF16))
                dAT = _ph.enter_context(nc.sbuf_tensor("dAT", [128, 128], BF16))
                dBst = _ph.enter_context(nc.sbuf_tensor("dBst", [128, 256], F32))
                dBbf = _ph.enter_context(nc.sbuf_tensor("dBbf", [128, 256], BF16))
                wost = _ph.enter_context(nc.sbuf_tensor("wost", [128, 512], F32))
                wobf = _ph.enter_context(nc.sbuf_tensor("wobf", [128, 2 * 16 * 512], BF16))
                dnst = _ph.enter_context(nc.sbuf_tensor("dnst", [128, 4096], F32))
                dnbf = _ph.enter_context(nc.sbuf_tensor("dnbf", [128, 2 * 4096], BF16))
                hbb = _ph.enter_context(nc.sbuf_tensor("hbb", [128, 2 * 8192], BF16))
                xres = _ph.enter_context(nc.sbuf_tensor("xres", [128, 2 * 512], F32))
                osb = _ph.enter_context(nc.sbuf_tensor("osb", [128, 2 * 512], F32))
                ps7_es = ExitStack()
                psT7 = ps7_es.enter_context(nc.psum_tensor("psT7", [128, 512], F32))
                m_dbt = 0
                m_bc7 = 0
                m_dbbf_rd = 0
                for e_i in range(E):
                    r.wait("sy", ve_s, m_bc7)
                    r.wait("sy", pe_s, m_dbbf_rd)
                    m_b = r.dma("sy", d_b7, dBst.ap().rearrange("p (c r) -> p c r", r=16),
                                dr_downB[e_i, :, :].rearrange("(c p) r -> p c r", p=128))
                    r.wait("ve", d_b7, m_b)
                    r.wait("ve", pe_s, m_dbbf_rd)
                    m_bc = r.op("ve", ve_s, lambda e: e.tensor_copy(dBbf[:, :], dBst[:, :]))
                    m_bc7 = m_bc
                    last_ev = 0
                    for k in range(16):
                        r.wait("pe", ve_s, max(m_bc, last_ev))
                        r.op("pe", pe_s, lambda e, k=k: e.matmul(
                            psT7[0:16, 0:128], dBbf[:, k * 16:(k + 1) * 16],
                            idbf[:, :], start=True, stop=True))
                        m_t = pe_s.n
                        r.wait("ve", pe_s, m_t)
                        last_ev = r.op("ve", ve_s, lambda e, e_i=e_i, k=k: e.tensor_copy(
                            dBTP[e_i // 3][(e_i % 3) * 32:(e_i % 3) * 32 + 16, k * 128:(k + 1) * 128],
                            psT7[0:16, 0:128]))
                    m_dbt = last_ev
                    m_dbbf_rd = pe_s.n
                free_hb = [0, 0]
                prev_t7_sc = 0
                m_ac7 = 0
                m_datbf_rd = 0
                for e_i in range(E):
                    r.wait("sy", ve_s, m_ac7)
                    r.wait("sy", pe_s, m_datbf_rd)
                    m_a = r.dma("sy", d_a7, dATst[:, :], dr_downA[e_i, :, :])
                    r.wait("ve", d_a7, m_a)
                    r.wait("ve", pe_s, m_datbf_rd)
                    m_ac = r.op("ve", ve_s, lambda e: e.tensor_copy(dATbf[:, :], dATst[:, :]))
                    m_ac7 = m_ac
                    last_ev = 0
                    for k in range(8):
                        r.wait("pe", ve_s, max(m_ac, last_ev, m_dbt))
                        r.wait("pe", sc_s, prev_t7_sc)
                        r.op("pe", pe_s, lambda e, k=k: e.matmul(
                            psT7[:, 0:16], dATbf[0:16, k * 128:(k + 1) * 128],
                            idbf[0:16, 0:16], start=True, stop=True))
                        m_t = pe_s.n
                        r.wait("ve", pe_s, m_t)
                        last_ev = r.op("ve", ve_s, lambda e, k=k: e.tensor_copy(
                            dAT[:, k * 16:(k + 1) * 16], psT7[:, 0:16]))
                    m_dat = last_ev
                    m_datbf_rd = pe_s.n
                    for t in range(4):
                        hp = t % 2
                        r.wait("sy", pe_s, free_hb[hp])
                        r.wait("sy", d_h[0], m_gup_done[0])
                        r.wait("sy", d_h[1], m_gup_done[1])
                        m_hb = r.dma("sy", d_hbs[hp],
                                     hbb[:, hp * 8192: hp * 8192 + 4096].rearrange("p (ic t) -> p ic t", t=512),
                                     dr_h[e_i, :, t * 512:(t + 1) * 512].rearrange("(ic p) t -> p ic t", p=128))
                        r.wait("pe", d_hbs[hp], m_hb)
                        r.wait("pe", ve_s, m_dat)
                        r.wait("pe", sc_s, prev_t7_sc)
                        for k in range(8):
                            r.op("pe", pe_s, lambda e, k=k, hp=hp: e.matmul(
                                psT7[0:16, 0:512], dAT[:, k * 16:(k + 1) * 16],
                                hbb[:, hp * 8192 + k * 512: hp * 8192 + (k + 1) * 512],
                                start=(k == 0), stop=(k == 7)))
                        m_tm = pe_s.n
                        free_hb[hp] = m_tm
                        r.wait("sc", pe_s, m_tm)
                        prev_t7_sc = r.op("sc", sc_s, lambda e, e_i=e_i, t=t: e.activation(
                            ter2P[e_i // 3][(e_i % 3) * 32:(e_i % 3) * 32 + 16, t * 512:(t + 1) * 512],
                            psT7[0:16, 0:512], AF.Copy, scale=SCALING))
                m_ter2_sc = prev_t7_sc
                ps7_es.close()
                psF = _ph.enter_context(nc.psum_tensor("psF", [128, 8 * 512], F32))
                free_hbb_init = pe_s.n
                wo3 = dr_wout.rearrange("(kc p) f -> p kc f", p=128)
                dn3 = dr_down.rearrange("e (ic p) f -> e p ic f", p=128)
                free_bank = [0] * 8
                free_wo = 0
                free_dn = [0, 0]
                free_hbb = [free_hbb_init, free_hbb_init]
                free_xres = [0, 0]
                free_osb = [0, 0]
                free_wo_par = [0, 0]
                m_dnc_last = 0
                m_woc = 0
                for f in range(4):
                    fpar = f % 2
                    r.wait("sy", pe_s, free_wo_par[fpar])
                    for k in range(16):
                        wp = k % 2
                        r.wait("sy", ve_s, m_woc)
                        m_wd = r.dma("sy", d_wo[wp], wost[:, :],
                                     wo3[:, k, f * 512:(f + 1) * 512])
                        r.wait("ve", d_wo[wp], m_wd)
                        m_woc = r.op("ve", ve_s, lambda e, k=k, fpar=fpar: e.tensor_copy(
                            wobf[:, fpar * 8192 + k * 512: fpar * 8192 + (k + 1) * 512],
                            wost[:, :]))
                    m_wo_ready = m_woc
                    for th in range(2):
                        for e_i in range(E):
                            ep = e_i % 2
                            r.wait("sy", pe_s, free_dn[ep])
                            r.wait("sy", ve_s, m_dnc_last)
                            m_dd = r.dma("sy", d_dn[ep],
                                         dnst.ap().rearrange("p (k c) -> p k c", c=512),
                                         dn3[e_i, :, :, f * 512:(f + 1) * 512])
                            r.wait("ve", d_dn[ep], m_dd)
                            m_dnc = r.op("ve", ve_s, lambda e, ep=ep: e.tensor_copy(
                                dnbf[:, ep * 4096:(ep + 1) * 4096], dnst[:, :]))
                            m_dnc_last = m_dnc
                            r.wait("gp", pe_s, free_hbb[ep])
                            r.wait("gp", d_h[0], m_gup_done[0])
                            r.wait("gp", d_h[1], m_gup_done[1])
                            m_hh = r.dma("gp", d_hb[ep],
                                         hbb[:, ep * 8192:(ep + 1) * 8192].rearrange("p (ic t) -> p ic t", t=1024),
                                         dr_h[e_i, :, th * 1024:(th + 1) * 1024].rearrange("(ic p) t -> p ic t", p=128))
                            for t in range(8):
                                tt = th * 8 + t
                                if e_i == 0:
                                    r.wait("pe", ve_s, max(m_wo_ready, free_bank[t], m_attn_ve))
                                    r.wait("pe", sc_s, m_ter2_sc)
                                    for k in range(16):
                                        r.opn("pe", lambda e, k=k, tt=tt, t=t, fpar=fpar: e.matmul(
                                            psF[:, t * 512:(t + 1) * 512],
                                            QO[:, k * T + tt * 128: k * T + (tt + 1) * 128],
                                            wobf[:, fpar * 8192 + k * 512: fpar * 8192 + (k + 1) * 512],
                                            start=(k == 0), stop=False))
                                r.wait("pe", ve_s, m_dnc)
                                r.wait("pe", d_hb[ep], m_hh)
                                r.wait("pe", sc_s, m_ter2_sc)
                                for k in range(8):
                                    r.opn("pe", lambda e, k=k, t=t, ep=ep: e.matmul(
                                        psF[:, t * 512:(t + 1) * 512],
                                        hbb[:, ep * 8192 + k * 1024 + t * 128: ep * 8192 + k * 1024 + (t + 1) * 128],
                                        dnbf[:, ep * 4096 + k * 512: ep * 4096 + (k + 1) * 512],
                                        start=False, stop=False))
                                r.op("pe", pe_s, lambda e, e_i=e_i, tt=tt, t=t, f=f: e.matmul(
                                    psF[:, t * 512:(t + 1) * 512],
                                    ter2P[e_i // 3][(e_i % 3) * 32:(e_i % 3) * 32 + 16, tt * 128:(tt + 1) * 128],
                                    dBTP[e_i // 3][(e_i % 3) * 32:(e_i % 3) * 32 + 16, f * 512:(f + 1) * 512],
                                    start=False, stop=(e_i == E - 1)))
                                m_pk = pe_s.n
                                if e_i == E - 1:
                                    xp = t % 2
                                    r.wait("gp", ve_s, free_xres[xp])
                                    m_xr = r.dma("gp", d_xr[xp], xres[:, xp * 512:(xp + 1) * 512],
                                                 dr_x[tt * 128:(tt + 1) * 128, f * 512:(f + 1) * 512])
                                    r.wait("ve", pe_s, m_pk)
                                    r.wait("ve", d_xr[xp], m_xr)
                                    r.wait("ve", d_ot[xp], free_osb[xp])
                                    m_ov = r.op("ve", ve_s, lambda e, t=t, xp=xp: e.tensor_tensor(
                                        osb[:, xp * 512:(xp + 1) * 512], psF[:, t * 512:(t + 1) * 512],
                                        xres[:, xp * 512:(xp + 1) * 512], OP.add))
                                    free_bank[t] = m_ov
                                    free_xres[xp] = m_ov
                                    r.wait("gp", ve_s, m_ov)
                                    m_os = r.dma("gp", d_ot[xp],
                                                 dr_out[tt * 128:(tt + 1) * 128, f * 512:(f + 1) * 512],
                                                 osb[:, xp * 512:(xp + 1) * 512])
                                    free_osb[xp] = m_os
                            free_dn[ep] = pe_s.n
                            free_hbb[ep] = pe_s.n
                        free_wo_par[fpar] = pe_s.n
            r.wait("gp", d_ot[0], d_ot[0].n)
            r.wait("gp", d_ot[1], d_ot[1].n)
        else:
            r.wait("gp", d_sp, d_sp.n)

        r.emit(block)
    return nc



def _host_consts(core):
    cl = core % 4
    idbf = np.eye(128, dtype=ml_dtypes.bfloat16)
    idf = np.eye(128, dtype=np.float32)
    mask = np.zeros((128, 4 * 512), np.float32)
    for mi in range(4):
        o = mi * 128
        p = np.arange(128)[:, None]
        q = np.arange(512)[None, :]
        mask[:, mi * 512:(mi + 1) * 512] = np.where(q >= p + o, 0.0, NEG)
    cfg = np.zeros((1, 64), np.int32)
    chA, chB = cl, 7 - cl
    units = [(0, chA), (1, chB)]
    for j in range(chA):
        units.append((0, j))
    for j in range(chB):
        units.append((1, j))
    assert len(units) == 9
    for u, (qc, j) in enumerate(units):
        rk = j if j < 4 else 7 - j
        lci = 0 if j < 4 else 1
        cfg[0, u * 4 + 0] = qc
        cfg[0, u * 4 + 1] = j
    cfg[0, 40] = cl
    return {"c_idbf": idbf, "c_idf": idf, "c_mask": mask, "c_cfg": cfg}


def _shard_x(x, core):
    cl = core % 4
    b = core // 4
    chA, chB = cl, 7 - cl
    return np.ascontiguousarray(np.concatenate(
        [x[b, chA * 1024:(chA + 1) * 1024], x[b, chB * 1024:(chB + 1) * 1024]], axis=0))


def kernel(L0_attn_x, L0_attn_w_qkv, L0_attn_w_out, L0_ffn_gate_weight,
           L0_ffn_gate_lora_A, L0_ffn_gate_lora_B, L0_ffn_gate_up_proj,
           L0_ffn_down_proj, L0_ffn_gup_lora_A, L0_ffn_gup_lora_B,
           L0_ffn_down_lora_A, L0_ffn_down_lora_B):
    x = np.asarray(L0_attn_x, np.float32)
    weights = {
        "wqkv": np.ascontiguousarray(np.asarray(L0_attn_w_qkv, np.float32)),
        "wout": np.ascontiguousarray(np.asarray(L0_attn_w_out, np.float32)),
        "gatew": np.ascontiguousarray(np.asarray(L0_ffn_gate_weight, np.float32)),
        "gateA": np.ascontiguousarray(np.asarray(L0_ffn_gate_lora_A, np.float32)),
        "gateB": np.ascontiguousarray(np.asarray(L0_ffn_gate_lora_B, np.float32)),
        "gup": np.ascontiguousarray(np.asarray(L0_ffn_gate_up_proj, np.float32)),
        "down": np.ascontiguousarray(np.asarray(L0_ffn_down_proj, np.float32)),
        "gupA": np.ascontiguousarray(np.asarray(L0_ffn_gup_lora_A, np.float32)),
        "gupB": np.ascontiguousarray(np.asarray(L0_ffn_gup_lora_B, np.float32)),
        "downA": np.ascontiguousarray(np.asarray(L0_ffn_down_lora_A, np.float32)),
        "downB": np.ascontiguousarray(np.asarray(L0_ffn_down_lora_B, np.float32)),
    }
    nc = build(taps=TAPS, phases=PHASES)
    in_maps = []
    for c in range(8):
        m = dict(weights)
        m["x"] = _shard_x(x, c)
        m.update(_host_consts(c))
        in_maps.append(m)
    res = run_bass_kernel_spmd(nc, in_maps, core_ids=list(range(8)))
    out = np.zeros((2, 8192, H), np.float32)
    for c in range(8):
        cl, b = c % 4, c // 4
        chA, chB = cl, 7 - cl
        o = res.results[c]["out"]
        out[b, chA * 1024:(chA + 1) * 1024] = o[0:1024]
        out[b, chB * 1024:(chB + 1) * 1024] = o[1024:2048]
    return out


# version 06
# speedup vs baseline: 1.1709x; 1.0371x over previous
import numpy as np
import ml_dtypes
from contextlib import ExitStack

import concourse.bass as bass
import concourse.mybir as mybir
from concourse.bass_utils import run_bass_kernel_spmd

F32 = mybir.dt.float32
BF16 = mybir.dt.bfloat16
I32 = mybir.dt.int32
AF = mybir.ActivationFunctionType
OP = mybir.AluOpType
AX = mybir.AxisListType.X
ds = bass.ds

H = 2048
NHEAD = 16
DH = 128
T = 2048
E = 8
II = 1024
R = 16
SCALING = 2.0
NEG = -1e9
EPS = 1e-5
ISQD = float(1.0 / np.sqrt(DH))
GROUPS = [[0, 1, 2, 3], [4, 5, 6, 7]]

PHASES = 7
TAPS = ()


def _diag_status(s, qh):
    if s * 128 + 127 <= qh * 512:
        return "o"
    if s * 128 > qh * 512 + 511:
        return "c"
    return "p"


class Sem:
    def __init__(self, nc, es, name):
        self.h = es.enter_context(nc.semaphore(name))
        self.n = 0


class Rec:

    def __init__(self):
        self.ops = {k: [] for k in ("sy", "pe", "ve", "sc", "gp")}

    def raw(self, eng, fn):
        self.ops[eng].append(fn)

    def wait(self, eng, sem, val):
        if val > 0:
            self.ops[eng].append(lambda e, s=sem, v=val: e.wait_ge(s.h, v))

    def op(self, eng, sem, fn):
        def g(e, fn=fn, sem=sem):
            fn(e).then_inc(sem.h, 1)
        self.ops[eng].append(g)
        sem.n += 1
        return sem.n

    def opn(self, eng, fn):
        self.ops[eng].append(fn)

    def barrier(self, bar, drains):
        for eng, sems in drains.items():
            for sm in sems:
                self.wait(eng, sm, sm.n)
        for eng in ("sy", "pe", "ve", "sc", "gp"):
            def g(e, bar=bar):
                e.nop().then_inc(bar.h, 1)
            self.ops[eng].append(g)
            bar.n += 1
        tgt = bar.n
        for eng in ("sy", "pe", "ve", "sc", "gp"):
            self.wait(eng, bar, tgt)

    def vsync(self, eng, sem):
        def g(e, sem=sem):
            e.nop().then_inc(sem.h, 1)
        self.ops[eng].append(g)
        sem.n += 1
        self.wait(eng, sem, sem.n)

    def dma(self, eng, sem, out, in_, **kw):
        def g(e, out=out, in_=in_, kw=kw, sem=sem):
            e.dma_start(out=out, in_=in_, **kw).then_inc(sem.h, 16)
        self.ops[eng].append(g)
        sem.n += 16
        return sem.n

    def emit(self, block):
        ops = self.ops

        @block.sync
        def _(e):
            for fn in ops["sy"]:
                fn(e)

        @block.tensor
        def _(e):
            for fn in ops["pe"]:
                fn(e)

        @block.vector
        def _(e):
            for fn in ops["ve"]:
                fn(e)

        @block.scalar
        def _(e):
            for fn in ops["sc"]:
                fn(e)

        @block.gpsimd
        def _(e):
            for fn in ops["gp"]:
                fn(e)


def build(taps=(), phases=7):
    nc = bass.Bass("TRN2", target_bir_lowering=False, debug=False, detect_race_conditions=False)
    es = ExitStack()

    dr_x = nc.dram_tensor("x", [T, H], F32, kind="ExternalInput").ap()
    dr_wqkv = nc.dram_tensor("wqkv", [H, 3 * H], F32, kind="ExternalInput").ap()
    dr_wout = nc.dram_tensor("wout", [H, H], F32, kind="ExternalInput").ap()
    dr_gw = nc.dram_tensor("gatew", [E, H], F32, kind="ExternalInput").ap()
    dr_gA = nc.dram_tensor("gateA", [R, H], F32, kind="ExternalInput").ap()
    dr_gB = nc.dram_tensor("gateB", [E, R], F32, kind="ExternalInput").ap()
    dr_gup = nc.dram_tensor("gup", [E, H, 2 * II], F32, kind="ExternalInput").ap()
    dr_down = nc.dram_tensor("down", [E, II, H], F32, kind="ExternalInput").ap()
    dr_gupA = nc.dram_tensor("gupA", [E, R, H], F32, kind="ExternalInput").ap()
    dr_gupB = nc.dram_tensor("gupB", [E, 2 * II, R], F32, kind="ExternalInput").ap()
    dr_downA = nc.dram_tensor("downA", [E, R, II], F32, kind="ExternalInput").ap()
    dr_downB = nc.dram_tensor("downB", [E, H, R], F32, kind="ExternalInput").ap()
    dr_idbf = nc.dram_tensor("c_idbf", [128, 128], BF16, kind="ExternalInput").ap()
    dr_idf = nc.dram_tensor("c_idf", [128, 128], F32, kind="ExternalInput").ap()
    dr_mask = nc.dram_tensor("c_mask", [128, 4 * 512], F32, kind="ExternalInput").ap()
    dr_cfg = nc.dram_tensor("c_cfg", [1, 64], I32, kind="ExternalInput").ap()
    dr_out = nc.dram_tensor("out", [T, H], F32, kind="ExternalOutput").ap()

    dr_ck = nc.dram_tensor("contribK", [H, T], BF16, kind="Internal").ap()
    dr_cv = nc.dram_tensor("contribV", [T, H], BF16, kind="Internal").ap()
    dr_gk = nc.dram_tensor("gatheredK", [4 * H, T], BF16, kind="Internal").ap()
    dr_gv = nc.dram_tensor("gatheredV", [4 * T, H], BF16, kind="Internal").ap()
    dr_h = nc.dram_tensor("hspill", [E, II, T], BF16, kind="Internal").ap()
    dr_wr = nc.dram_tensor("wrout", [E, T], F32, kind="Internal").ap()
    dr_kni = nc.dram_tensor("kn_in", [1, NHEAD], F32, kind="Internal").ap()
    dr_kno = nc.dram_tensor("kn_out", [1, NHEAD], F32, kind="Internal").ap()

    tap_aps = {}
    if taps:
        tshapes = {"xn": ([128, 16 * T], BF16), "qo": ([128, NHEAD * T], BF16),
                   "gk": ([4 * H, T], BF16), "gv": ([4 * T, H], BF16),
                   "h": ([E, II, T], BF16), "w": ([E, T], F32),
                   "negc": ([128, 32], F32), "lnst": ([128, 16], F32)}
        for t in taps:
            sh, dt = tshapes[t]
            tap_aps[t] = nc.dram_tensor("tap_" + t, sh, dt, kind="ExternalOutput").ap()

    m_xn_done = 0
    m_qkv_pe = 0
    m_qkv_ve = 0
    m_wsp = 0
    m_gup_done = (0, 0)
    m_attn_ve = 0

    with es:
        sbt = lambda name, shape, dtype: es.enter_context(nc.sbuf_tensor(name, shape, dtype))
        xn_es = ExitStack()
        xn = xn_es.enter_context(nc.sbuf_tensor("xn", [128, 16 * T], BF16, side="right"))
        QO = sbt("QO", [128, NHEAD * T], BF16)
        idbf = sbt("idbf", [128, 128], BF16)
        idf = sbt("idf", [128, 128], F32)
        cfgs = sbt("cfgs", [1, 64], I32)
        ones_col = sbt("ones_col", [128, 1], F32)
        ones_col_bf = sbt("ones_col_bf", [128, 1], BF16)
        ones_row = sbt("ones_row", [1, 128], F32)
        zero_bf = sbt("zero_bf", [128, 512], BF16)
        eps_t = sbt("eps_t", [128, 1], F32)
        qn2 = sbt("qn2", [1, 32], F32)
        kn2 = sbt("kn2", [1, NHEAD], F32)
        negc_rep = sbt("negc_rep", [128, 32], F32)
        w45_es = ExitStack()
        wT8 = w45_es.enter_context(nc.sbuf_tensor("wT8", [E, T], F32, side="right"))

        S = lambda name: Sem(nc, es, name)
        d_in = S("d_in")
        d_x = [S("d_x0"), S("d_x1")]
        d_w2 = [S("d_w20"), S("d_w21")]
        d_kst = [S("d_kst0"), S("d_kst1")]
        d_vtt = [S("d_vtt0"), S("d_vtt1")]
        d_g4 = S("d_g4")
        d_a5 = S("d_a5")
        d_b5 = S("d_b5")
        d_w1 = S("d_w1")
        d_w5 = [S("d_w50"), S("d_w51")]
        d_h = [S("d_h0"), S("d_h1")]
        d_kv = [S("d_kv0"), S("d_kv1")]
        d_kn6 = S("d_kn6")
        d_b7 = S("d_b7")
        d_a7 = S("d_a7")
        d_hbs = [S("d_hbs0"), S("d_hbs1")]
        d_wo = [S("d_wo0"), S("d_wo1")]
        d_dn = [S("d_dn0"), S("d_dn1")]
        d_hb = [S("d_hb0"), S("d_hb1")]
        d_xr = [S("d_xr0"), S("d_xr1")]
        d_ot = [S("d_ot0"), S("d_ot1")]
        d_sp = S("d_sp")
        c_k = S("c_k")
        c_v = S("c_v")
        c_nn = S("c_nn")
        pe_s = S("pe_s")
        ve_s = S("ve_s")
        sc_s = S("sc_s")
        gp_s = S("gp_s")
        bar = S("bar")

        def phase_barrier():
            r.barrier(bar, {
                "sy": [d_w2[0], d_w2[1], d_kv[0], d_kv[1], d_g4, d_a5, d_b5,
                       d_w1, d_w5[0], d_w5[1], d_hbs[0], d_hbs[1], d_wo[0],
                       d_wo[1], d_dn[0], d_dn[1], d_kn6, d_b7, d_a7],
                "gp": [d_in, d_x[0], d_x[1], d_kst[0], d_kst[1], d_vtt[0],
                       d_vtt[1], d_sp, d_h[0], d_h[1], d_hb[0], d_hb[1],
                       d_xr[0], d_xr[1], d_ot[0], d_ot[1]],
            })

        rg_sy = [es.enter_context(nc.sync.register(f"rs{i}")) for i in range(3)]
        rg_pe = es.enter_context(nc.tensor.register("rt0"))
        rg_ve = es.enter_context(nc.vector.register("rv0"))
        rg_sc = es.enter_context(nc.scalar.register("rc0"))
        rg_gp = es.enter_context(nc.gpsimd.register("rg0"))

        block = es.enter_context(nc.Block())
        r = Rec()

        r.dma("gp", d_in, idbf[:, :], dr_idbf)
        r.dma("gp", d_in, idf[:, :], dr_idf)
        m_consts = r.dma("gp", d_in, cfgs[:, :], dr_cfg)
        r.opn("ve", lambda e: e.memset(ones_col[:, :], 1.0))
        r.opn("ve", lambda e: e.memset(ones_col_bf[:, :], 1.0))
        r.opn("ve", lambda e: e.memset(ones_row[:, :], 1.0))
        r.opn("ve", lambda e: e.memset(zero_bf[:, :], 0.0))
        r.opn("ve", lambda e: e.memset(eps_t[:, :], EPS))
        r.opn("ve", lambda e: e.memset(qn2[:, :], 0.0))
        m_vec_consts = r.op("ve", ve_s, lambda e: e.memset(kn2[:, :], 0.0))

        with ExitStack() as _ph:
            xbuf = _ph.enter_context(nc.sbuf_tensor("xbuf", [128, 2 * H], F32))
            junk = _ph.enter_context(nc.sbuf_tensor("junk", [128, H], BF16))
            lnst = _ph.enter_context(nc.sbuf_tensor("lnst", [128, 16], F32))
            xntm = _ph.enter_context(nc.sbuf_tensor("xntm", [128, 2 * H], BF16))
            lntmp = _ph.enter_context(nc.sbuf_tensor("lntmp", [128, H], F32))
            lnac = _ph.enter_context(nc.sbuf_tensor("lnac", [128, 1], F32))
            psLN = _ph.enter_context(nc.psum_tensor("psLN", [128, 1024], F32))
            r.opn("ve", lambda e: e.memset(lnst[:, :], 0.0))
            free_xb = [0, 0]
            free_half = [0, 0]
            prev_var = 0
            xn3 = xn.ap().rearrange("p (k t) -> p k t", t=T)
            ps3 = psLN.ap().rearrange("p (g c) -> p g c", c=128)
            for t in range(16):
                par = t % 2
                xb = xbuf[:, par * H:(par + 1) * H]
                xt = xntm[:, par * H:(par + 1) * H]
                r.wait("gp", ve_s, free_xb[par])
                m_x = r.dma("gp", d_x[par], xb, dr_x[t * 128:(t + 1) * 128, :])
                r.wait("ve", d_x[par], m_x)
                m_s1 = r.op("ve", ve_s, lambda e, xb=xb: e.reduce_sum(lnst[:, 0:1], xb, axis=AX))
                r.wait("sc", d_x[par], m_x)
                r.wait("sc", ve_s, m_s1)
                r.opn("sc", lambda e: e.activation(lnst[:, 1:2], lnst[:, 0:1], AF.Copy, scale=1.0 / H))
                m_sq = r.op("sc", sc_s, lambda e, xb=xb: e.activation(
                    junk[:, :], xb, AF.Square, accum_out=lnac[:, 0:1]))
                r.wait("ve", sc_s, m_sq)
                r.vsync("ve", ve_s)
                r.opn("ve", lambda e: e.tensor_scalar(lnst[:, 9:10], lnst[:, 1:2], lnst[:, 0:1], None, OP.mult))
                r.vsync("ve", ve_s)
                m_var = r.op("ve", ve_s, lambda e: e.tensor_scalar(
                    lnst[:, 5:6], lnac[:, 0:1], lnst[:, 9:10], None, OP.subtract))
                prev_var = m_var
                r.wait("sc", ve_s, m_var)
                m_sd = r.op("sc", sc_s, lambda e: e.activation(
                    lnst[:, 6:7], lnst[:, 5:6], AF.Sqrt, scale=1.0 / H, bias=eps_t[:, 0:1]))
                r.wait("ve", sc_s, m_sd)
                r.opn("ve", lambda e: e.reciprocal(lnst[:, 7:8], lnst[:, 6:7]))
                r.opn("ve", lambda e: e.tensor_scalar(lnst[:, 8:9], lnst[:, 1:2], -1.0, None, OP.mult))
                r.vsync("ve", ve_s)
                r.opn("ve", lambda e, xb=xb: e.tensor_scalar(
                    lntmp[:, :], xb, lnst[:, 8:9], None, OP.add))
                m_xnt = r.op("ve", ve_s, lambda e, xt=xt: e.tensor_scalar(
                    xt, lntmp[:, :], lnst[:, 7:8], None, OP.mult))
                free_xb[par] = m_xnt
                for g in range(4):
                    half = g % 2
                    r.wait("pe", ve_s, max(m_xnt, free_half[half], m_vec_consts))
                    r.wait("pe", d_in, m_consts)
                    for kk in range(4):
                        k = g * 4 + kk
                        r.op("pe", pe_s, lambda e, xt=xt, k=k, half=half, kk=kk: e.matmul(
                            psLN[:, half * 512 + kk * 128: half * 512 + (kk + 1) * 128],
                            xt[:, k * 128:(k + 1) * 128], idbf[:, :],
                            start=True, stop=True))
                    m_tr = pe_s.n
                    r.wait("ve", pe_s, m_tr)
                    m_ev = r.op("ve", ve_s, lambda e, g=g, t=t, half=half: e.tensor_copy(
                        xn3[:, g * 4:(g + 1) * 4, t * 128:(t + 1) * 128],
                        ps3[:, half * 4:(half + 1) * 4, :]))
                    free_half[half] = m_ev
            m_xn_done = ve_s.n
            if "lnst" in tap_aps:
                r.wait("gp", ve_s, m_xn_done)
                r.dma("gp", d_sp, tap_aps["lnst"], lnst[:, :])

        if "xn" in tap_aps:
            r.wait("gp", ve_s, m_xn_done)
            r.dma("gp", d_sp, tap_aps["xn"], xn[:, :])

        phase_barrier()

        if phases >= 2:
            with ExitStack() as _ph:
                wst = _ph.enter_context(nc.sbuf_tensor("wst", [128, 2 * H], F32))
                wbf = _ph.enter_context(nc.sbuf_tensor("wbf", [128, 2 * H], BF16))
                kst = _ph.enter_context(nc.sbuf_tensor("kst", [128, 2 * 512], BF16))
                vst = _ph.enter_context(nc.sbuf_tensor("vst", [128, 2 * 512], BF16))
                vtt = _ph.enter_context(nc.sbuf_tensor("vtt", [128, 2 * 512], BF16))
                sqb = _ph.enter_context(nc.sbuf_tensor("sqb", [128, 4 * 512], BF16))
                nrm = _ph.enter_context(nc.sbuf_tensor("nrm", [1, 8], F32))
                nsb = _ph.enter_context(nc.sbuf_tensor("nsb", [1, 2048], F32))
                psQ = _ph.enter_context(nc.psum_tensor("psQ", [128, 4 * 512], F32))
                psN = _ph.enter_context(nc.psum_tensor("psN", [128, 1024], F32))
                psVT = _ph.enter_context(nc.psum_tensor("psVT", [128, 512], F32))
                wq3 = dr_wqkv.rearrange("(kc p) m -> p kc m", p=128)
                free_w = [0, 0]
                free_kst = [0, 0]
                free_vst = [0, 0]
                free_vtt = [0, 0]
                free_psq = [0, 0, 0, 0]
                free_psn = [0, 0]
                free_sqb = [0, 0, 0, 0]
                free_psvt = 0
                norm_q = []
                for m in range(48):
                    par = m % 2
                    r.wait("sy", pe_s, free_w[par])
                    m_w = r.dma("sy", d_w2[par],
                                wst[:, par * H:(par + 1) * H].rearrange("p (kc m) -> p kc m", m=128),
                                wq3[:, :, m * 128:(m + 1) * 128])
                    r.wait("ve", d_w2[par], m_w)
                    m_cv = r.op("ve", ve_s, lambda e, par=par: e.tensor_copy(
                        wbf[:, par * H:(par + 1) * H], wst[:, par * H:(par + 1) * H]))
                    last_mm = 0
                    for t in range(4):
                        r.wait("pe", ve_s, max(m_cv, m_xn_done))
                        r.wait("pe", sc_s, free_psq[t])
                        for k in range(16):
                            r.op("pe", pe_s, lambda e, par=par, k=k, t=t: e.matmul(
                                psQ[:, t * 512:(t + 1) * 512],
                                wbf[:, par * H + k * 128: par * H + (k + 1) * 128],
                                xn[:, k * T + t * 512: k * T + (t + 1) * 512],
                                start=(k == 0), stop=(k == 15)))
                        m_mm = pe_s.n
                        last_mm = m_mm
                        if norm_q:
                            norm_q.pop(0)()
                        if m < 16:
                            hq = m
                            r.wait("sc", pe_s, m_mm)
                            m_ev = r.op("sc", sc_s, lambda e, t=t, hq=hq: e.activation(
                                QO[:, hq * T + t * 512: hq * T + (t + 1) * 512],
                                psQ[:, t * 512:(t + 1) * 512], AF.Copy, scale=ISQD))
                            free_psq[t] = m_ev
                            r.wait("sc", pe_s, free_sqb[t])
                            m_sq = r.op("sc", sc_s, lambda e, t=t, hq=hq: e.activation(
                                sqb[:, t * 512:(t + 1) * 512],
                                QO[:, hq * T + t * 512: hq * T + (t + 1) * 512], AF.Square))

                            def norm_fn(m_sq=m_sq, t=t, hq=hq):
                                nh = t % 2
                                r.wait("pe", sc_s, m_sq)
                                r.wait("pe", ve_s, free_psn[nh])
                                m_on = r.op("pe", pe_s, lambda e, nh=nh, t=t: e.matmul(
                                    psN[0:1, nh * 512:(nh + 1) * 512], ones_col_bf[:, :],
                                    sqb[:, t * 512:(t + 1) * 512], start=True, stop=True))
                                free_sqb[t] = m_on
                                r.wait("ve", pe_s, m_on)
                                m_nm = r.op("ve", ve_s, lambda e, nh=nh, t=t: e.tensor_copy(
                                    nsb[0:1, t * 512:(t + 1) * 512], psN[0:1, nh * 512:(nh + 1) * 512]))
                                free_psn[nh] = m_nm
                                if t == 3:
                                    r.opn("ve", lambda e: e.reduce_max(nrm[0:1, 0:1], nsb[0:1, 0:1024], axis=AX))
                                    m_nm2 = r.op("ve", ve_s, lambda e: e.reduce_max(nrm[0:1, 1:2], nsb[0:1, 1024:2048], axis=AX))
                                    r.wait("sc", ve_s, m_nm2)
                                    r.opn("sc", lambda e, hq=hq: e.activation(qn2[0:1, hq * 2: hq * 2 + 1], nrm[0:1, 0:1], AF.Copy))
                                    m_qn = r.op("sc", sc_s, lambda e, hq=hq: e.activation(qn2[0:1, hq * 2 + 1: hq * 2 + 2], nrm[0:1, 1:2], AF.Copy))
                                    r.wait("ve", sc_s, m_qn)
                                    r.op("ve", ve_s, lambda e: e.nop())
                            norm_q.append(norm_fn)
                        elif m < 32:
                            hk = m - 16
                            kp = t % 2
                            r.wait("sc", pe_s, m_mm)
                            r.wait("sc", d_kst[kp], free_kst[kp])
                            m_ev = r.op("sc", sc_s, lambda e, t=t, kp=kp: e.activation(
                                kst[:, kp * 512:(kp + 1) * 512], psQ[:, t * 512:(t + 1) * 512], AF.Copy))
                            free_psq[t] = m_ev
                            r.wait("gp", sc_s, m_ev)
                            m_kd = r.dma("gp", d_kst[kp],
                                         dr_ck[hk * 128:(hk + 1) * 128, t * 512:(t + 1) * 512],
                                         kst[:, kp * 512:(kp + 1) * 512])
                            free_kst[kp] = m_kd
                            r.wait("sc", pe_s, free_sqb[t])
                            m_sq = r.op("sc", sc_s, lambda e, t=t, kp=kp: e.activation(
                                sqb[:, t * 512:(t + 1) * 512], kst[:, kp * 512:(kp + 1) * 512], AF.Square))

                            def norm_fn(m_sq=m_sq, t=t, hk=hk):
                                nh = t % 2
                                r.wait("pe", sc_s, m_sq)
                                r.wait("pe", ve_s, free_psn[nh])
                                m_on = r.op("pe", pe_s, lambda e, nh=nh, t=t: e.matmul(
                                    psN[0:1, nh * 512:(nh + 1) * 512], ones_col_bf[:, :],
                                    sqb[:, t * 512:(t + 1) * 512], start=True, stop=True))
                                free_sqb[t] = m_on
                                r.wait("ve", pe_s, m_on)
                                m_nm = r.op("ve", ve_s, lambda e, nh=nh, t=t: e.tensor_copy(
                                    nsb[0:1, t * 512:(t + 1) * 512], psN[0:1, nh * 512:(nh + 1) * 512]))
                                free_psn[nh] = m_nm
                                if t == 3:
                                    m_nm2 = r.op("ve", ve_s, lambda e: e.reduce_max(nrm[0:1, 2:3], nsb[0:1, 0:2048], axis=AX))
                                    r.wait("sc", ve_s, m_nm2)
                                    m_kn = r.op("sc", sc_s, lambda e, hk=hk: e.activation(kn2[0:1, hk:hk + 1], nrm[0:1, 2:3], AF.Copy))
                                    r.wait("ve", sc_s, m_kn)
                                    r.op("ve", ve_s, lambda e: e.nop())
                            norm_q.append(norm_fn)
                        else:
                            hv = m - 32
                            vp = t % 2
                            r.wait("sc", pe_s, m_mm)
                            r.wait("sc", pe_s, free_vst[vp])
                            m_ev = r.op("sc", sc_s, lambda e, t=t, vp=vp: e.activation(
                                vst[:, vp * 512:(vp + 1) * 512], psQ[:, t * 512:(t + 1) * 512], AF.Copy))
                            free_psq[t] = m_ev
                            r.wait("pe", sc_s, m_ev)
                            r.wait("pe", ve_s, free_psvt)
                            for i in range(4):
                                r.op("pe", pe_s, lambda e, vp=vp, i=i: e.matmul(
                                    psVT[:, i * 128:(i + 1) * 128],
                                    vst[:, vp * 512 + i * 128: vp * 512 + (i + 1) * 128],
                                    idbf[:, :], start=True, stop=True))
                            m_tr = pe_s.n
                            free_vst[vp] = m_tr
                            r.wait("ve", pe_s, m_tr)
                            r.wait("ve", d_vtt[vp], free_vtt[vp])
                            m_vv = r.op("ve", ve_s, lambda e, vp=vp: e.tensor_copy(
                                vtt[:, vp * 512:(vp + 1) * 512], psVT[:, :]))
                            free_psvt = m_vv
                            r.wait("gp", ve_s, m_vv)
                            m_vd = r.dma("gp", d_vtt[vp],
                                         dr_cv[t * 512:(t + 1) * 512, hv * 128:(hv + 1) * 128]
                                         .rearrange("(i p) d -> p i d", p=128),
                                         vtt[:, vp * 512:(vp + 1) * 512].rearrange("p (i d) -> p i d", d=128))
                            free_vtt[vp] = m_vd
                    free_w[par] = last_mm
                while norm_q:
                    norm_q.pop(0)()
                m_qkv_pe = pe_s.n
                m_qkv_ve = ve_s.n
            r.wait("gp", ve_s, m_qkv_ve)
            r.dma("gp", d_sp, dr_kni, kn2[:, :])

        if phases >= 3:
            r.wait("gp", d_sp, d_sp.n)
            r.wait("gp", d_kst[0], d_kst[0].n)
            r.wait("gp", d_kst[1], d_kst[1].n)
            r.wait("gp", d_vtt[0], d_vtt[0].n)
            r.wait("gp", d_vtt[1], d_vtt[1].n)
            r.op("gp", c_k, lambda e: e.collective_compute(
                "AllGather", OP.bypass, replica_groups=GROUPS,
                ins=[dr_ck], outs=[dr_gk]))
            r.op("gp", c_v, lambda e: e.collective_compute(
                "AllGather", OP.bypass, replica_groups=GROUPS,
                ins=[dr_cv], outs=[dr_gv]))
            r.op("gp", c_nn, lambda e: e.collective_compute(
                "AllReduce", OP.max, replica_groups=GROUPS,
                ins=[dr_kni], outs=[dr_kno]))

        if phases >= 4:
            phase_barrier()

        if phases >= 4:
            with ExitStack() as _ph:
                gst = _ph.enter_context(nc.sbuf_tensor("gst", [16, 2 * H], F32))
                gbst = _ph.enter_context(nc.sbuf_tensor("gbst", [8, 16], F32))
                gbf = _ph.enter_context(nc.sbuf_tensor("gbf", [16, 2 * H], BF16))
                gbbf = _ph.enter_context(nc.sbuf_tensor("gbbf", [8, 16], BF16))
                wgT = _ph.enter_context(nc.sbuf_tensor("wgT", [128, 16 * 8], BF16))
                agT = _ph.enter_context(nc.sbuf_tensor("agT", [128, 16 * 16], BF16))
                bgT = _ph.enter_context(nc.sbuf_tensor("bgT", [16, 8], BF16))
                rterg = _ph.enter_context(nc.sbuf_tensor("rterg", [16, 512], BF16))
                lgsb = _ph.enter_context(nc.sbuf_tensor("lgsb", [8, 512], F32))
                rst = _ph.enter_context(nc.sbuf_tensor("rst", [128, 96], F32))
                wtm = _ph.enter_context(nc.sbuf_tensor("wtm", [128, 8], F32))
                psR = _ph.enter_context(nc.psum_tensor("psR", [128, 1024], F32))
                psR2 = _ph.enter_context(nc.psum_tensor("psR2", [128, 512], F32))
                r.dma("sy", d_g4, gst[0:8, 0:H], dr_gw)
                r.dma("sy", d_g4, gst[0:16, H:2 * H], dr_gA)
                m_gl = r.dma("sy", d_g4, gbst[:, :], dr_gB)
                r.wait("ve", d_g4, m_gl)
                r.opn("ve", lambda e: e.tensor_copy(gbf[0:8, 0:H], gst[0:8, 0:H]))
                r.opn("ve", lambda e: e.tensor_copy(gbf[0:16, H:2 * H], gst[0:16, H:2 * H]))
                m_gc = r.op("ve", ve_s, lambda e: e.tensor_copy(gbbf[:, :], gbst[:, :]))
                last_ev = [0, 0]
                r.wait("pe", ve_s, m_gc)
                for k in range(16):
                    pp = k % 2
                    r.wait("pe", ve_s, last_ev[pp])
                    r.op("pe", pe_s, lambda e, k=k, pp=pp: e.matmul(
                        psR[:, pp * 512: pp * 512 + 8],
                        gbf[0:8, k * 128:(k + 1) * 128], idbf[0:8, 0:8],
                        start=True, stop=True))
                    m_t = pe_s.n
                    r.wait("ve", pe_s, m_t)
                    last_ev[pp] = r.op("ve", ve_s, lambda e, k=k, pp=pp: e.tensor_copy(
                        wgT[:, k * 8:(k + 1) * 8], psR[:, pp * 512: pp * 512 + 8]))
                for k in range(16):
                    pp = k % 2
                    r.wait("pe", ve_s, last_ev[pp])
                    r.op("pe", pe_s, lambda e, k=k, pp=pp: e.matmul(
                        psR[:, pp * 512: pp * 512 + 16],
                        gbf[0:16, H + k * 128: H + (k + 1) * 128], idbf[0:16, 0:16],
                        start=True, stop=True))
                    m_t = pe_s.n
                    r.wait("ve", pe_s, m_t)
                    last_ev[pp] = r.op("ve", ve_s, lambda e, k=k, pp=pp: e.tensor_copy(
                        agT[:, k * 16:(k + 1) * 16], psR[:, pp * 512: pp * 512 + 16]))
                r.wait("pe", ve_s, last_ev[0])
                r.op("pe", pe_s, lambda e: e.matmul(psR[0:16, 0:8], gbbf[:, :], idbf[0:8, 0:8],
                                               start=True, stop=True))
                m_t3 = pe_s.n
                r.wait("ve", pe_s, m_t3)
                m_bgT = r.op("ve", ve_s, lambda e: e.tensor_copy(bgT[:, :], psR[0:16, 0:8]))

                prev_psR_sc = 0
                prev_psR_ve = m_bgT
                for t in range(4):
                    r.wait("pe", ve_s, prev_psR_ve)
                    r.wait("pe", sc_s, prev_psR_sc)
                    for k in range(16):
                        r.op("pe", pe_s, lambda e, k=k, t=t: e.matmul(
                            psR[0:16, 0:512], agT[:, k * 16:(k + 1) * 16],
                            xn[:, k * T + t * 512: k * T + (t + 1) * 512],
                            start=(k == 0), stop=(k == 15)))
                    m_ter = pe_s.n
                    r.wait("sc", pe_s, m_ter)
                    m_terg = r.op("sc", sc_s, lambda e: e.activation(
                        rterg[:, :], psR[0:16, 0:512], AF.Copy, scale=SCALING))
                    r.wait("pe", sc_s, m_terg)
                    for k in range(16):
                        r.opn("pe", lambda e, k=k, t=t: e.matmul(
                            psR2[0:8, 0:512], wgT[:, k * 8:(k + 1) * 8],
                            xn[:, k * T + t * 512: k * T + (t + 1) * 512],
                            start=(k == 0), stop=False))
                    r.op("pe", pe_s, lambda e: e.matmul(psR2[0:8, 0:512], bgT[:, :], rterg[:, :],
                                                   start=False, stop=True))
                    m_lg = pe_s.n
                    r.wait("ve", pe_s, m_lg)
                    m_lgs = r.op("ve", ve_s, lambda e: e.tensor_copy(lgsb[:, :], psR2[0:8, 0:512]))
                    r.wait("pe", ve_s, m_lgs)
                    for i in range(4):
                        r.op("pe", pe_s, lambda e, i=i: e.matmul(
                            psR[:, 512 + i * 128: 512 + i * 128 + 8],
                            lgsb[0:8, i * 128:(i + 1) * 128],
                            idf[0:8, 0:8], start=True, stop=True))
                    m_lt = pe_s.n
                    for i in range(4):
                        sl = psR[:, 512 + i * 128: 512 + i * 128 + 8]
                        r.wait("ve", pe_s, m_lt)
                        m_nm = r.op("ve", ve_s, lambda e, sl=sl: e.reduce_max(
                            rst[:, 0:1], sl, axis=AX, negate=True))
                        r.wait("sc", ve_s, m_nm)
                        m_px = r.op("sc", sc_s, lambda e, sl=sl: e.activation(
                            rst[:, 8:16], sl, AF.Exp, bias=rst[:, 0:1], accum_out=rst[:, 1:2]))
                        prev_psR_sc = m_px
                        r.wait("ve", sc_s, m_px)
                        r.opn("ve", lambda e: e.reciprocal(rst[:, 2:3], rst[:, 1:2]))
                        r.vsync("ve", ve_s)
                        m_p = r.op("ve", ve_s, lambda e: e.tensor_scalar(rst[:, 16:24], rst[:, 8:16], rst[:, 2:3], None, OP.mult))
                        r.wait("sc", ve_s, m_p)
                        m_pc = r.op("sc", sc_s, lambda e: e.activation(rst[:, 48:56], rst[:, 16:24], AF.Copy))
                        r.wait("ve", sc_s, m_pc)
                        r.opn("ve", lambda e: e.reduce_max(rst[:, 24:25], rst[:, 48:56], axis=AX))
                        r.vsync("ve", ve_s)
                        m_pm = r.op("ve", ve_s, lambda e: e.scalar_tensor_tensor(
                            rst[:, 32:40], rst[:, 48:56], rst[:, 24:25], rst[:, 48:56], OP.is_lt, OP.mult))
                        r.wait("sc", ve_s, m_pm)
                        r.opn("sc", lambda e: e.activation(rst[:, 56:64], rst[:, 32:40], AF.Copy))
                        m_m1c = r.op("sc", sc_s, lambda e: e.activation(rst[:, 28:29], rst[:, 24:25], AF.Copy))
                        r.wait("ve", sc_s, m_m1c)
                        r.opn("ve", lambda e: e.reduce_max(rst[:, 25:26], rst[:, 56:64], axis=AX))
                        r.vsync("ve", ve_s)
                        m_den = r.op("ve", ve_s, lambda e: e.tensor_scalar(rst[:, 26:27], rst[:, 28:29], rst[:, 25:26], None, OP.add))
                        r.wait("sc", ve_s, m_den)
                        m_dc = r.op("sc", sc_s, lambda e: e.activation(rst[:, 29:30], rst[:, 26:27], AF.Copy))
                        r.wait("ve", sc_s, m_dc)
                        r.opn("ve", lambda e: e.reciprocal(rst[:, 27:28], rst[:, 29:30]))
                        r.vsync("ve", ve_s)
                        m_sel = r.op("ve", ve_s, lambda e: e.scalar_tensor_tensor(
                            rst[:, 40:48], rst[:, 48:56], rst[:, 25:26], rst[:, 48:56], OP.is_ge, OP.mult))
                        r.wait("sc", ve_s, m_sel)
                        m_slc = r.op("sc", sc_s, lambda e: e.activation(rst[:, 64:72], rst[:, 40:48], AF.Copy))
                        r.wait("ve", sc_s, m_slc)
                        m_wtm = r.op("ve", ve_s, lambda e: e.tensor_scalar(
                            wtm[:, 0:8], rst[:, 64:72], rst[:, 27:28], None, OP.mult))
                        prev_psR_ve = m_wtm
                        r.wait("pe", ve_s, m_wtm)
                        r.op("pe", pe_s, lambda e: e.matmul(psR2[0:8, 0:128], wtm[:, 0:8], idf[:, :],
                                                       start=True, stop=True))
                        m_wt = pe_s.n
                        r.wait("ve", pe_s, m_wt)
                        m_w8 = r.op("ve", ve_s, lambda e, t=t, i=i: e.tensor_copy(
                            wT8[:, t * 512 + i * 128: t * 512 + (i + 1) * 128], psR2[0:8, 0:128]))
                        r.wait("pe", ve_s, m_w8)
                m_rout_ve = ve_s.n
            r.wait("gp", ve_s, m_rout_ve)
            m_wsp = r.dma("gp", d_sp, dr_wr, wT8[:, :])
            if "w" in tap_aps:
                r.wait("gp", d_sp, m_wsp)
                r.dma("gp", d_sp, tap_aps["w"], dr_wr)

        if phases >= 5:
            phase_barrier()

        if phases >= 5:
            with ExitStack() as _ph:
                agupT = _ph.enter_context(nc.sbuf_tensor("agupT", [128, 256], BF16))
                gast = _ph.enter_context(nc.sbuf_tensor("gast", [16, H // 2], F32))
                gabf = _ph.enter_context(nc.sbuf_tensor("gabf", [16, H], BF16))
                gbst2 = _ph.enter_context(nc.sbuf_tensor("gbst2", [128, 256], F32))
                gbbf2 = _ph.enter_context(nc.sbuf_tensor("gbbf2", [128, 256], BF16))
                bgupT = _ph.enter_context(nc.sbuf_tensor("bgupT", [16, H], BF16))
                gter = _ph.enter_context(nc.sbuf_tensor("gter", [16, T], BF16))
                w1r = _ph.enter_context(nc.sbuf_tensor("w1r", [1, 512], F32))
                wrep = _ph.enter_context(nc.sbuf_tensor("wrep", [128, 4 * 512], BF16))
                gwst = _ph.enter_context(nc.sbuf_tensor("gwst", [128, 4096], F32))
                gwbf = _ph.enter_context(nc.sbuf_tensor("gwbf", [128, 2 * 4096], BF16))
                sgx = _ph.enter_context(nc.sbuf_tensor("sgx", [128, 2 * 512], BF16))
                upx = _ph.enter_context(nc.sbuf_tensor("upx", [128, 2 * 512], BF16))
                hst = _ph.enter_context(nc.sbuf_tensor("hst", [128, 2 * 512], BF16))
                psT5 = _ph.enter_context(nc.psum_tensor("psT5", [128, 512], F32))
                psW5 = _ph.enter_context(nc.psum_tensor("psW5", [128, 512], F32))
                psG = _ph.enter_context(nc.psum_tensor("psG", [128, 2 * 512], F32))
                psU5 = _ph.enter_context(nc.psum_tensor("psU5", [128, 2 * 512], F32))
                prev_t5_sc = 0
                m_prev_gabf_rd = 0
                m_prev_gbbf_rd = 0
                gw3 = dr_gup.rearrange("e (kc p) m -> e p kc m", p=128)
                free_gw = [0, 0]
                m_gw_conv = 0
                free_psg = [0, 0]
                free_psu = [0, 0]
                free_sgx = [0, 0]
                free_upx = [0, 0]
                free_hst = [0, 0]
                m_bc_prev = 0
                for e_i in range(E):
                    m_ac = 0
                    for hf in range(2):
                        r.wait("sy", ve_s, m_ac)
                        r.wait("sy", pe_s, m_prev_gabf_rd)
                        m_a = r.dma("sy", d_a5, gast[:, :], dr_gupA[e_i, :, hf * (H // 2):(hf + 1) * (H // 2)])
                        r.wait("ve", d_a5, m_a)
                        m_ac = r.op("ve", ve_s, lambda e, hf=hf: e.tensor_copy(
                            gabf[:, hf * (H // 2):(hf + 1) * (H // 2)], gast[:, :]))
                    m_agupT = 0
                    for k in range(16):
                        r.wait("pe", ve_s, max(m_ac, m_agupT))
                        r.wait("pe", sc_s, prev_t5_sc)
                        r.op("pe", pe_s, lambda e, k=k: e.matmul(
                            psT5[:, 0:16], gabf[0:16, k * 128:(k + 1) * 128],
                            idbf[0:16, 0:16], start=True, stop=True))
                        m_t = pe_s.n
                        r.wait("ve", pe_s, m_t)
                        m_agupT = r.op("ve", ve_s, lambda e, k=k: e.tensor_copy(
                            agupT[:, k * 16:(k + 1) * 16], psT5[:, 0:16]))
                    m_prev_gabf_rd = pe_s.n
                    r.wait("sy", ve_s, m_bc_prev)
                    r.wait("sy", pe_s, m_prev_gbbf_rd)
                    m_b = r.dma("sy", d_b5, gbst2.ap().rearrange("p (c r) -> p c r", r=16),
                                dr_gupB[e_i, :, :].rearrange("(c p) r -> p c r", p=128))
                    r.wait("ve", d_b5, m_b)
                    r.wait("ve", pe_s, m_prev_gbbf_rd)
                    m_bc = r.op("ve", ve_s, lambda e: e.tensor_copy(gbbf2[:, :], gbst2[:, :]))
                    m_bc_prev = m_bc
                    last_ev = 0
                    for k in range(16):
                        r.wait("pe", ve_s, max(m_bc, last_ev))
                        r.op("pe", pe_s, lambda e, k=k: e.matmul(
                            psT5[0:16, 0:128], gbbf2[:, k * 16:(k + 1) * 16],
                            idbf[:, :], start=True, stop=True))
                        m_t = pe_s.n
                        r.wait("ve", pe_s, m_t)
                        last_ev = r.op("ve", ve_s, lambda e, k=k: e.tensor_copy(
                            bgupT[0:16, k * 128:(k + 1) * 128], psT5[0:16, 0:128]))
                    m_prev_gbbf_rd = pe_s.n
                    r.wait("sy", d_sp, m_wsp)
                    last_wr = 0
                    last_w1_rd = 0
                    for t in range(4):
                        r.wait("sy", pe_s, last_w1_rd)
                        m_w1 = r.dma("sy", d_w1, w1r[:, :], dr_wr[e_i:e_i + 1, t * 512:(t + 1) * 512])
                        r.wait("pe", d_w1, m_w1)
                        r.wait("pe", ve_s, max(last_wr, last_ev))
                        r.op("pe", pe_s, lambda e: e.matmul(
                            psW5[:, 0:512], ones_row[:, :], w1r[0:1, 0:512],
                            start=True, stop=True))
                        m_t = pe_s.n
                        last_w1_rd = m_t
                        r.wait("ve", pe_s, m_t)
                        last_wr = r.op("ve", ve_s, lambda e, t=t: e.tensor_copy(
                            wrep[:, t * 512:(t + 1) * 512], psW5[:, 0:512]))
                    m_wrep = last_wr
                    last_terg = prev_t5_sc
                    for t in range(4):
                        r.wait("pe", sc_s, last_terg)
                        for k in range(16):
                            r.op("pe", pe_s, lambda e, k=k, t=t: e.matmul(
                                psT5[0:16, 0:512], agupT[:, k * 16:(k + 1) * 16],
                                xn[:, k * T + t * 512: k * T + (t + 1) * 512],
                                start=(k == 0), stop=(k == 15)))
                        m_tm = pe_s.n
                        r.wait("sc", pe_s, m_tm)
                        last_terg = r.op("sc", sc_s, lambda e, t=t: e.activation(
                            gter[:, t * 512:(t + 1) * 512], psT5[0:16, 0:512], AF.Copy, scale=SCALING))
                    prev_t5_sc = last_terg
                    m_gter = last_terg
                    for mp in range(8):
                        par = mp % 2
                        gpar = (e_i * 8 + mp) % 2
                        r.wait("sy", pe_s, free_gw[par])
                        r.wait("sy", ve_s, m_gw_conv)
                        r.dma("sy", d_w5[gpar],
                              gwst[:, 0:2048].rearrange("p (kc m) -> p kc m", m=128),
                              gw3[e_i, :, :, mp * 128:(mp + 1) * 128])
                        m_w2 = r.dma("sy", d_w5[gpar],
                                     gwst[:, 2048:4096].rearrange("p (kc m) -> p kc m", m=128),
                                     gw3[e_i, :, :, (mp + 8) * 128:(mp + 9) * 128])
                        r.wait("ve", d_w5[gpar], m_w2)
                        m_wc = r.op("ve", ve_s, lambda e, par=par: e.tensor_copy(
                            gwbf[:, par * 4096:(par + 1) * 4096], gwst[:, :]))
                        m_gw_conv = m_wc
                        for t in range(4):
                            tp = t % 2
                            r.wait("pe", ve_s, max(m_wc, m_wrep))
                            r.wait("pe", sc_s, max(m_gter, free_psg[tp]))
                            for k in range(16):
                                r.opn("pe", lambda e, par=par, k=k, t=t, tp=tp: e.matmul(
                                    psG[:, tp * 512:(tp + 1) * 512],
                                    gwbf[:, par * 4096 + k * 128: par * 4096 + (k + 1) * 128],
                                    xn[:, k * T + t * 512: k * T + (t + 1) * 512],
                                    start=(k == 0), stop=False))
                            r.op("pe", pe_s, lambda e, mp=mp, t=t, tp=tp: e.matmul(
                                psG[:, tp * 512:(tp + 1) * 512],
                                bgupT[0:16, mp * 128:(mp + 1) * 128],
                                gter[:, t * 512:(t + 1) * 512], start=False, stop=True))
                            m_gmm = pe_s.n
                            r.wait("pe", ve_s, free_psu[tp])
                            for k in range(16):
                                r.opn("pe", lambda e, par=par, k=k, t=t, tp=tp: e.matmul(
                                    psU5[:, tp * 512:(tp + 1) * 512],
                                    gwbf[:, par * 4096 + 2048 + k * 128: par * 4096 + 2048 + (k + 1) * 128],
                                    xn[:, k * T + t * 512: k * T + (t + 1) * 512],
                                    start=(k == 0), stop=False))
                            r.op("pe", pe_s, lambda e, mp=mp, t=t, tp=tp: e.matmul(
                                psU5[:, tp * 512:(tp + 1) * 512],
                                bgupT[0:16, (mp + 8) * 128:(mp + 9) * 128],
                                gter[:, t * 512:(t + 1) * 512], start=False, stop=True))
                            m_umm = pe_s.n
                            r.wait("sc", pe_s, m_gmm)
                            r.wait("sc", ve_s, free_sgx[tp])
                            m_sg = r.op("sc", sc_s, lambda e, tp=tp: e.activation(
                                sgx[:, tp * 512:(tp + 1) * 512], psG[:, tp * 512:(tp + 1) * 512], AF.Silu))
                            free_psg[tp] = m_sg
                            r.wait("ve", pe_s, m_umm)
                            r.wait("ve", ve_s, free_upx[tp])
                            m_ux = r.op("ve", ve_s, lambda e, t=t, tp=tp: e.tensor_tensor(
                                upx[:, tp * 512:(tp + 1) * 512], psU5[:, tp * 512:(tp + 1) * 512],
                                wrep[:, t * 512:(t + 1) * 512], OP.mult))
                            free_psu[tp] = m_ux
                            r.wait("ve", sc_s, m_sg)
                            r.wait("ve", d_h[tp], free_hst[tp])
                            m_h = r.op("ve", ve_s, lambda e, tp=tp: e.tensor_tensor(
                                hst[:, tp * 512:(tp + 1) * 512], sgx[:, tp * 512:(tp + 1) * 512],
                                upx[:, tp * 512:(tp + 1) * 512], OP.mult))
                            free_sgx[tp] = m_h
                            free_upx[tp] = m_h
                            r.wait("gp", ve_s, m_h)
                            m_hd = r.dma("gp", d_h[tp],
                                         dr_h[e_i, mp * 128:(mp + 1) * 128, t * 512:(t + 1) * 512],
                                         hst[:, tp * 512:(tp + 1) * 512])
                            free_hst[tp] = m_hd
                        free_gw[par] = pe_s.n
            m_gup_done = (d_h[0].n, d_h[1].n)
            if "h" in tap_aps:
                r.wait("gp", d_h[0], m_gup_done[0])
                r.wait("gp", d_h[1], m_gup_done[1])
                r.dma("gp", d_sp, tap_aps["h"], dr_h)

        if phases >= 6:
            phase_barrier()
        w45_es.close()
        xn_es.close()

        if phases >= 6:
            with ExitStack() as _ph:
                kbuf = _ph.enter_context(nc.sbuf_tensor("kbuf", [128, 2 * 1024], BF16))
                vbuf = _ph.enter_context(nc.sbuf_tensor("vbuf", [128, 2 * 1024], BF16))
                ptb = _ph.enter_context(nc.sbuf_tensor("ptb", [128, 4 * 1024], BF16))
                smask = _ph.enter_context(nc.sbuf_tensor("smask", [128, 2 * 512], F32))
                masks = _ph.enter_context(nc.sbuf_tensor("masks", [128, 4 * 512], F32))
                ssum = _ph.enter_context(nc.sbuf_tensor("ssum", [128, 2 * 2048], F32))
                ssumC = _ph.enter_context(nc.sbuf_tensor("ssumC", [128, 2 * 2048], F32))
                knrow = _ph.enter_context(nc.sbuf_tensor("knrow", [1, NHEAD], F32))
                kdup = _ph.enter_context(nc.sbuf_tensor("kdup", [1, 32], F32))
                crow = _ph.enter_context(nc.sbuf_tensor("crow", [1, 32], F32))
                csq = _ph.enter_context(nc.sbuf_tensor("csq", [1, 32], F32))
                rrow = _ph.enter_context(nc.sbuf_tensor("rrow", [1, 1024], F32))
                rrep = _ph.enter_context(nc.sbuf_tensor("rrep", [128, 1024], F32))
                psS0 = _ph.enter_context(nc.psum_tensor("psS0", [128, 1024], F32))
                psS1 = _ph.enter_context(nc.psum_tensor("psS1", [128, 1024], F32))
                psUa = _ph.enter_context(nc.psum_tensor("psUa", [128, 2048], F32))
                psS = [psS0, psS1]
                m_masks = r.dma("gp", d_in, masks[:, :], dr_mask)
                r.wait("sy", c_nn, 1)
                m_kn = r.dma("sy", d_kn6, knrow[:, :], dr_kno)
                kd3 = kdup.ap().rearrange("o (hh two) -> o hh two", two=2)
                kn3 = knrow.ap().rearrange("o (h u) -> o h u", u=1)
                r.wait("sc", d_kn6, m_kn)
                r.opn("sc", lambda e: e.activation(kd3[:, :, 0:1], kn3, AF.Copy))
                m_kd = r.op("sc", sc_s, lambda e: e.activation(kd3[:, :, 1:2], kn3, AF.Copy))
                r.wait("ve", sc_s, m_kd)
                r.wait("ve", ve_s, m_qkv_ve)
                m_cp = r.op("ve", ve_s, lambda e: e.tensor_tensor(csq[:, :], qn2[:, :], kdup[:, :], OP.mult))
                r.wait("sc", ve_s, m_cp)
                m_cr = r.op("sc", sc_s, lambda e: e.activation(crow[:, :], csq[:, :], AF.Sqrt))
                r.wait("ve", sc_s, m_cr)
                m_ng = r.op("ve", ve_s, lambda e: e.tensor_scalar(csq[:, :], crow[:, :], -1.0, None, OP.mult))
                r.wait("pe", ve_s, m_ng)
                r.wait("pe", pe_s, m_qkv_pe)
                r.op("pe", pe_s, lambda e: e.matmul(psS0[:, 0:32], ones_row[:, :], csq[0:1, :],
                                               start=True, stop=True))
                m_nr = pe_s.n
                r.wait("ve", pe_s, m_nr)
                m_negc = r.op("ve", ve_s, lambda e: e.tensor_copy(negc_rep[:, :], psS0[:, 0:32]))
                if "negc" in tap_aps:
                    r.wait("gp", ve_s, m_negc)
                    r.dma("gp", d_sp, tap_aps["negc"], negc_rep[:, :])
                r.wait("pe", ve_s, m_negc)
                r.wait("sy", c_k, 1)
                r.wait("sy", c_v, 1)
                r.wait("ve", d_in, m_masks)
                r.wait("sc", ve_s, m_negc)

                gv3 = dr_gv.rearrange("(s p) d -> p s d", p=128)

                def emit_attn_variant(rv, cl):
                    units = [(0, cl), (1, 7 - cl)]
                    for jj in range(cl):
                        units.append((0, jj))
                    for jj in range(7 - cl):
                        units.append((1, jj))
                    kv_pe_done = {}
                    g_seq = 0
                    for h_i in range(NHEAD):
                        rv.wait("pe", ve_s, ve_s.n)
                        for j4 in range(4):
                            rv.op("pe", pe_s, lambda e, j4=j4: e.matmul(
                                psUa[:, j4 * 512:(j4 + 1) * 512], zero_bf[:, 0:128], zero_bf[:, :],
                                start=True, stop=False))
                        pe_s.n
                        m_ssz = rv.op("ve", ve_s, lambda e: e.memset(ssum[:, 0:2048], 0.0))
                        rv.wait("gp", pe_s, pe_s.n)
                        rv.op("gp", gp_s, lambda e: e.memset(ssumC[:, 0:2048], 0.0))
                        m_exp_prev = [0, 0]
                        m_psS_ve = [0, 0]
                        m_pt_pe = [0, 0, 0, 0]
                        m_pt_ve = [0, 0, 0, 0]
                        m_pt_gp = [0, 0, 0, 0]
                        m_smask_rd = [0, 0]
                        cnt = {("A", 0): 0, ("A", 1): 0, ("C", 0): 0, ("C", 1): 0}
                        for u in range(9):
                            qc, jch = units[u]
                            up = g_seq % 2
                            rk = jch if jch < 4 else 7 - jch
                            lci = 0 if jch < 4 else 1
                            rv.wait("sy", pe_s, kv_pe_done.get(g_seq - 2, 0))
                            rv.dma("sy", d_kv[up],
                                   kbuf[:, up * 1024:(up + 1) * 1024],
                                   dr_gk[rk * H + h_i * 128: rk * H + (h_i + 1) * 128,
                                         lci * 1024:(lci + 1) * 1024])
                            vs = (rk * T + lci * 1024) // 128
                            m_kvd = rv.dma("sy", d_kv[up],
                                           vbuf[:, up * 1024:(up + 1) * 1024].rearrange("p (s d) -> p s d", d=128),
                                           gv3[:, vs:vs + 8, h_i * 128:(h_i + 1) * 128])
                            nbias = negc_rep[:, h_i * 2 + qc: h_i * 2 + qc + 1]
                            diag = u < 2
                            pending = []

                            def emit_av(rv=rv, up=up, qc=qc):
                                sss, p4, segs, m_exp = pending.pop(0)
                                rv.wait("pe", sc_s, m_exp if m_exp else sc_s.n)
                                for qh in range(2):
                                    if segs[qh] == "c":
                                        continue
                                    rv.op("pe", pe_s, lambda e, sss=sss, qh=qh, p4=p4: e.matmul(
                                        psUa[:, qc * 1024 + qh * 512: qc * 1024 + (qh + 1) * 512],
                                        vbuf[:, up * 1024 + sss * 128: up * 1024 + (sss + 1) * 128],
                                        ptb[:, p4 * 1024 + qh * 512: p4 * 1024 + (qh + 1) * 512],
                                        start=False, stop=False))
                                m_pt_pe[p4] = pe_s.n

                            for sss in range(8):
                                p2 = sss % 2
                                p4 = sss % 4
                                segs = [_diag_status(sss, qh) for qh in range(2)] if diag else ["f", "f"]
                                rv.wait("pe", d_kv[up], m_kvd)
                                rv.wait("pe", sc_s, m_exp_prev[p2])
                                rv.wait("pe", ve_s, m_psS_ve[p2])
                                for qh in range(2):
                                    if segs[qh] == "c":
                                        continue
                                    rv.op("pe", pe_s, lambda e, sss=sss, qh=qh, up=up, p2=p2, h_i=h_i, qc=qc: e.matmul(
                                        psS[p2][:, qh * 512:(qh + 1) * 512],
                                        kbuf[:, up * 1024 + sss * 128: up * 1024 + (sss + 1) * 128],
                                        QO[:, h_i * T + qc * 1024 + qh * 512: h_i * T + qc * 1024 + (qh + 1) * 512],
                                        start=True, stop=True))
                                m_smm = pe_s.n
                                if pending:
                                    emit_av()
                                m_msk = 0
                                if diag:
                                    for qh in range(2):
                                        if segs[qh] != "p":
                                            continue
                                        mi = (sss * 128 - qh * 512) // 128
                                        rv.wait("ve", pe_s, m_smm)
                                        rv.wait("ve", sc_s, m_smask_rd[qh])
                                        m_msk = rv.op("ve", ve_s, lambda e, p2=p2, qh=qh, mi=mi: e.tensor_tensor(
                                            smask[:, qh * 512:(qh + 1) * 512],
                                            psS[p2][:, qh * 512:(qh + 1) * 512],
                                            masks[:, mi * 512:(mi + 1) * 512], OP.add))
                                        m_psS_ve[p2] = m_msk
                                    for qh in range(2):
                                        if segs[qh] == "c":
                                            rv.wait("ve", pe_s, m_pt_pe[p4])
                                            if m_pt_gp[p4]:
                                                rv.wait("ve", gp_s, m_pt_gp[p4])
                                            m_msk = rv.op("ve", ve_s, lambda e, p4=p4, qh=qh: e.memset(
                                                ptb[:, p4 * 1024 + qh * 512: p4 * 1024 + (qh + 1) * 512], 0.0))
                                rv.wait("sc", pe_s, max(m_smm, m_pt_pe[p4]))
                                rv.wait("sc", ve_s, max(m_msk, m_pt_ve[p4]))
                                if m_pt_gp[p4]:
                                    rv.wait("sc", gp_s, m_pt_gp[p4])
                                if not diag:
                                    m_exp = rv.op("sc", sc_s, lambda e, p2=p2, p4=p4, nbias=nbias: e.activation(
                                        ptb[:, p4 * 1024:(p4 + 1) * 1024], psS[p2][:, :],
                                        AF.Exp, bias=nbias))
                                    m_exp_prev[p2] = m_exp
                                else:
                                    m_exp = 0
                                    for qh in range(2):
                                        if segs[qh] == "c":
                                            continue
                                        if segs[qh] == "p":
                                            m_exp = rv.op("sc", sc_s, lambda e, p4=p4, qh=qh, nbias=nbias: e.activation(
                                                ptb[:, p4 * 1024 + qh * 512: p4 * 1024 + (qh + 1) * 512],
                                                smask[:, qh * 512:(qh + 1) * 512], AF.Exp, bias=nbias))
                                            m_smask_rd[qh] = m_exp
                                        else:
                                            m_exp = rv.op("sc", sc_s, lambda e, p2=p2, p4=p4, qh=qh, nbias=nbias: e.activation(
                                                ptb[:, p4 * 1024 + qh * 512: p4 * 1024 + (qh + 1) * 512],
                                                psS[p2][:, qh * 512:(qh + 1) * 512], AF.Exp, bias=nbias))
                                            m_exp_prev[p2] = m_exp
                                ch = "A" if sss in (0, 2, 4, 6, 7) else "C"
                                aeng = "ve" if ch == "A" else "gp"
                                asem = ve_s if ch == "A" else gp_s
                                abuf = ssum if ch == "A" else ssumC
                                c0 = cnt[(ch, qc)]
                                spp, dpp = c0 % 2, (c0 + 1) % 2
                                cnt[(ch, qc)] = c0 + 1
                                rv.wait(aeng, sc_s, m_exp if m_exp else sc_s.n)
                                if ch == "C" and diag and m_msk:
                                    rv.wait("gp", ve_s, m_msk)
                                if u == 0 and sss == 0:
                                    rv.wait("ve", ve_s, m_ssz)
                                rv.opn(aeng, lambda e, spp=spp, dpp=dpp, p4=p4, qc=qc, abuf=abuf: e.tensor_tensor(
                                    abuf[:, dpp * 2048 + qc * 1024: dpp * 2048 + (qc + 1) * 1024],
                                    abuf[:, spp * 2048 + qc * 1024: spp * 2048 + (qc + 1) * 1024],
                                    ptb[:, p4 * 1024:(p4 + 1) * 1024], OP.add))
                                m_ssa = rv.op(aeng, asem, lambda e: e.nop())
                                if ch == "A":
                                    m_pt_ve[p4] = m_ssa
                                else:
                                    m_pt_gp[p4] = m_ssa
                                pending.append((sss, p4, segs, m_exp))
                            while pending:
                                emit_av()
                            kv_pe_done[g_seq] = pe_s.n
                            g_seq += 1
                        for j4 in range(4):
                            if j4 == 3:
                                rv.op("pe", pe_s, lambda e, j4=j4: e.matmul(
                                    psUa[:, j4 * 512:(j4 + 1) * 512], zero_bf[:, 0:128], zero_bf[:, :],
                                    start=False, stop=True))
                            else:
                                rv.opn("pe", lambda e, j4=j4: e.matmul(
                                    psUa[:, j4 * 512:(j4 + 1) * 512], zero_bf[:, 0:128], zero_bf[:, :],
                                    start=False, stop=True))
                        m_last_exp = sc_s.n
                        m_last_ssa = ve_s.n
                        m_last_av = pe_s.n
                        m_last_gadd = gp_s.n
                        for qc in range(2):
                            ppf = qc
                            pA = (cnt[("A", qc)] % 2) * 2048
                            pC = (cnt[("C", qc)] % 2) * 2048
                            rv.wait("pe", sc_s, m_last_exp)
                            rv.wait("pe", ve_s, max(m_last_ssa, m_psS_ve[ppf]))
                            rv.wait("pe", gp_s, m_last_gadd)
                            for qh in range(2):
                                rv.opn("pe", lambda e, qc=qc, qh=qh, ppf=ppf, pA=pA: e.matmul(
                                    psS[ppf][0:1, qh * 512:(qh + 1) * 512], ones_col[:, :],
                                    ssum[:, pA + qc * 1024 + qh * 512: pA + qc * 1024 + (qh + 1) * 512],
                                    start=True, stop=False))
                                rv.op("pe", pe_s, lambda e, qc=qc, qh=qh, ppf=ppf, pC=pC: e.matmul(
                                    psS[ppf][0:1, qh * 512:(qh + 1) * 512], ones_col[:, :],
                                    ssumC[:, pC + qc * 1024 + qh * 512: pC + qc * 1024 + (qh + 1) * 512],
                                    start=False, stop=True))
                            m_sr = pe_s.n
                            rv.wait("ve", pe_s, m_sr)
                            m_rr = rv.op("ve", ve_s, lambda e, ppf=ppf: e.reciprocal(
                                rrow[0:1, :], psS[ppf][0:1, :]))
                            rv.wait("pe", ve_s, m_rr)
                            for qh in range(2):
                                rv.op("pe", pe_s, lambda e, qh=qh, ppf=ppf: e.matmul(
                                    psS[ppf][:, qh * 512:(qh + 1) * 512], ones_row[:, :],
                                    rrow[0:1, qh * 512:(qh + 1) * 512], start=True, stop=True))
                            m_rp = pe_s.n
                            rv.wait("ve", pe_s, m_rp)
                            m_rrep = rv.op("ve", ve_s, lambda e, ppf=ppf: e.tensor_copy(
                                rrep[:, :], psS[ppf][:, :]))
                            rv.wait("ve", pe_s, m_last_av)
                            rv.op("ve", ve_s, lambda e, h_i=h_i, qc=qc: e.tensor_tensor(
                                QO[:, h_i * T + qc * 1024: h_i * T + (qc + 1) * 1024],
                                psUa[:, qc * 1024:(qc + 1) * 1024], rrep[:, :], OP.mult))

                all_sems = [d_in, d_kv[0], d_kv[1], d_sp, c_k, c_v, c_nn, pe_s, ve_s, sc_s, gp_s]
                snap0 = {id(sm): sm.n for sm in all_sems}
                variants = []
                final0 = None
                for v in range(4):
                    for sm in all_sems:
                        sm.n = snap0[id(sm)]
                    rvv = Rec()
                    emit_attn_variant(rvv, v)
                    fin = {id(sm): sm.n for sm in all_sems}
                    if final0 is None:
                        final0 = fin
                    else:
                        assert fin == final0, "attention variants diverge in sem counts"
                    variants.append(rvv)
                for sm in all_sems:
                    sm.n = final0[id(sm)]
                eng_regs = {"sy": rg_sy[0], "pe": rg_pe, "ve": rg_ve, "sc": rg_sc, "gp": rg_gp}
                for eng in ("sy", "pe", "ve", "sc", "gp"):
                    bodies = [variants[v].ops[eng] for v in range(4)]
                    r.wait(eng, d_in, m_consts)
                    def sw(e, reg=eng_regs[eng], bodies=bodies):
                        e.reg_load(reg, cfgs[0:1, 40:41])
                        idx = e.snap(reg, min_val=0, max_val=3)
                        for jv in nc.Switch(engines=[e], index=[idx], n=4):
                            for fn in bodies[jv]:
                                fn(e)
                    r.raw(eng, sw)
            m_attn_ve = ve_s.n
            if "qo" in tap_aps:
                r.wait("gp", ve_s, m_attn_ve)
                r.dma("gp", d_sp, tap_aps["qo"], QO[:, :])
        if "qo" in tap_aps and phases < 6:
            r.wait("gp", sc_s, sc_s.n)
            r.dma("gp", d_sp, tap_aps["qo"], QO[:, :])
        if "gk" in tap_aps and phases >= 3:
            r.wait("gp", c_k, 1)
            r.dma("gp", d_sp, tap_aps["gk"], dr_gk)
        if "gv" in tap_aps and phases >= 3:
            r.wait("gp", c_v, 1)
            r.dma("gp", d_sp, tap_aps["gv"], dr_gv)

        if phases >= 7:
            phase_barrier()

        if phases >= 7:
            with ExitStack() as _ph:
                dATst = _ph.enter_context(nc.sbuf_tensor("dATst", [16, II], F32))
                ter2P = [_ph.enter_context(nc.sbuf_tensor(f"ter2_{j}", [128, T], BF16)) for j in range(3)]
                dBTP = [_ph.enter_context(nc.sbuf_tensor(f"dBT_{j}", [128, H], BF16)) for j in range(3)]
                dATbf = _ph.enter_context(nc.sbuf_tensor("dATbf", [16, II], BF16))
                dAT = _ph.enter_context(nc.sbuf_tensor("dAT", [128, 128], BF16))
                dBst = _ph.enter_context(nc.sbuf_tensor("dBst", [128, 256], F32))
                dBbf = _ph.enter_context(nc.sbuf_tensor("dBbf", [128, 256], BF16))
                wost = _ph.enter_context(nc.sbuf_tensor("wost", [128, 512], F32))
                wobf = _ph.enter_context(nc.sbuf_tensor("wobf", [128, 2 * 16 * 512], BF16))
                dnst = _ph.enter_context(nc.sbuf_tensor("dnst", [128, 4096], F32))
                dnbf = _ph.enter_context(nc.sbuf_tensor("dnbf", [128, 2 * 4096], BF16))
                hbb = _ph.enter_context(nc.sbuf_tensor("hbb", [128, 2 * 8192], BF16))
                xres = _ph.enter_context(nc.sbuf_tensor("xres", [128, 2 * 512], F32))
                osb = _ph.enter_context(nc.sbuf_tensor("osb", [128, 2 * 512], F32))
                ps7_es = ExitStack()
                psT7 = ps7_es.enter_context(nc.psum_tensor("psT7", [128, 512], F32))
                m_dbt = 0
                m_bc7 = 0
                m_dbbf_rd = 0
                for e_i in range(E):
                    r.wait("sy", ve_s, m_bc7)
                    r.wait("sy", pe_s, m_dbbf_rd)
                    m_b = r.dma("sy", d_b7, dBst.ap().rearrange("p (c r) -> p c r", r=16),
                                dr_downB[e_i, :, :].rearrange("(c p) r -> p c r", p=128))
                    r.wait("ve", d_b7, m_b)
                    r.wait("ve", pe_s, m_dbbf_rd)
                    m_bc = r.op("ve", ve_s, lambda e: e.tensor_copy(dBbf[:, :], dBst[:, :]))
                    m_bc7 = m_bc
                    last_ev = 0
                    for k in range(16):
                        r.wait("pe", ve_s, max(m_bc, last_ev))
                        r.op("pe", pe_s, lambda e, k=k: e.matmul(
                            psT7[0:16, 0:128], dBbf[:, k * 16:(k + 1) * 16],
                            idbf[:, :], start=True, stop=True))
                        m_t = pe_s.n
                        r.wait("ve", pe_s, m_t)
                        last_ev = r.op("ve", ve_s, lambda e, e_i=e_i, k=k: e.tensor_copy(
                            dBTP[e_i // 3][(e_i % 3) * 32:(e_i % 3) * 32 + 16, k * 128:(k + 1) * 128],
                            psT7[0:16, 0:128]))
                    m_dbt = last_ev
                    m_dbbf_rd = pe_s.n
                free_hb = [0, 0]
                prev_t7_sc = 0
                m_ac7 = 0
                m_datbf_rd = 0
                for e_i in range(E):
                    r.wait("sy", ve_s, m_ac7)
                    r.wait("sy", pe_s, m_datbf_rd)
                    m_a = r.dma("sy", d_a7, dATst[:, :], dr_downA[e_i, :, :])
                    r.wait("ve", d_a7, m_a)
                    r.wait("ve", pe_s, m_datbf_rd)
                    m_ac = r.op("ve", ve_s, lambda e: e.tensor_copy(dATbf[:, :], dATst[:, :]))
                    m_ac7 = m_ac
                    last_ev = 0
                    for k in range(8):
                        r.wait("pe", ve_s, max(m_ac, last_ev, m_dbt))
                        r.wait("pe", sc_s, prev_t7_sc)
                        r.op("pe", pe_s, lambda e, k=k: e.matmul(
                            psT7[:, 0:16], dATbf[0:16, k * 128:(k + 1) * 128],
                            idbf[0:16, 0:16], start=True, stop=True))
                        m_t = pe_s.n
                        r.wait("ve", pe_s, m_t)
                        last_ev = r.op("ve", ve_s, lambda e, k=k: e.tensor_copy(
                            dAT[:, k * 16:(k + 1) * 16], psT7[:, 0:16]))
                    m_dat = last_ev
                    m_datbf_rd = pe_s.n
                    for t in range(4):
                        hp = t % 2
                        r.wait("sy", pe_s, free_hb[hp])
                        r.wait("sy", d_h[0], m_gup_done[0])
                        r.wait("sy", d_h[1], m_gup_done[1])
                        m_hb = r.dma("sy", d_hbs[hp],
                                     hbb[:, hp * 8192: hp * 8192 + 4096].rearrange("p (ic t) -> p ic t", t=512),
                                     dr_h[e_i, :, t * 512:(t + 1) * 512].rearrange("(ic p) t -> p ic t", p=128))
                        r.wait("pe", d_hbs[hp], m_hb)
                        r.wait("pe", ve_s, m_dat)
                        r.wait("pe", sc_s, prev_t7_sc)
                        for k in range(8):
                            r.op("pe", pe_s, lambda e, k=k, hp=hp: e.matmul(
                                psT7[0:16, 0:512], dAT[:, k * 16:(k + 1) * 16],
                                hbb[:, hp * 8192 + k * 512: hp * 8192 + (k + 1) * 512],
                                start=(k == 0), stop=(k == 7)))
                        m_tm = pe_s.n
                        free_hb[hp] = m_tm
                        r.wait("sc", pe_s, m_tm)
                        prev_t7_sc = r.op("sc", sc_s, lambda e, e_i=e_i, t=t: e.activation(
                            ter2P[e_i // 3][(e_i % 3) * 32:(e_i % 3) * 32 + 16, t * 512:(t + 1) * 512],
                            psT7[0:16, 0:512], AF.Copy, scale=SCALING))
                m_ter2_sc = prev_t7_sc
                ps7_es.close()
                psF = _ph.enter_context(nc.psum_tensor("psF", [128, 8 * 512], F32))
                free_hbb_init = pe_s.n
                wo3 = dr_wout.rearrange("(kc p) f -> p kc f", p=128)
                dn3 = dr_down.rearrange("e (ic p) f -> e p ic f", p=128)
                free_bank = [0] * 8
                free_wo = 0
                free_dn = [0, 0]
                free_hbb = [free_hbb_init, free_hbb_init]
                free_xres = [0, 0]
                free_osb = [0, 0]
                free_wo_par = [0, 0]
                m_dnc_last = 0
                m_woc = 0
                for f in range(4):
                    fpar = f % 2
                    r.wait("sy", pe_s, free_wo_par[fpar])
                    for k in range(16):
                        wp = k % 2
                        r.wait("sy", ve_s, m_woc)
                        m_wd = r.dma("sy", d_wo[wp], wost[:, :],
                                     wo3[:, k, f * 512:(f + 1) * 512])
                        r.wait("ve", d_wo[wp], m_wd)
                        m_woc = r.op("ve", ve_s, lambda e, k=k, fpar=fpar: e.tensor_copy(
                            wobf[:, fpar * 8192 + k * 512: fpar * 8192 + (k + 1) * 512],
                            wost[:, :]))
                    m_wo_ready = m_woc
                    for th in range(2):
                        for e_i in range(E):
                            ep = e_i % 2
                            r.wait("sy", pe_s, free_dn[ep])
                            r.wait("sy", ve_s, m_dnc_last)
                            m_dd = r.dma("sy", d_dn[ep],
                                         dnst.ap().rearrange("p (k c) -> p k c", c=512),
                                         dn3[e_i, :, :, f * 512:(f + 1) * 512])
                            r.wait("ve", d_dn[ep], m_dd)
                            m_dnc = r.op("ve", ve_s, lambda e, ep=ep: e.tensor_copy(
                                dnbf[:, ep * 4096:(ep + 1) * 4096], dnst[:, :]))
                            m_dnc_last = m_dnc
                            r.wait("gp", pe_s, free_hbb[ep])
                            r.wait("gp", d_h[0], m_gup_done[0])
                            r.wait("gp", d_h[1], m_gup_done[1])
                            m_hh = r.dma("gp", d_hb[ep],
                                         hbb[:, ep * 8192:(ep + 1) * 8192].rearrange("p (ic t) -> p ic t", t=1024),
                                         dr_h[e_i, :, th * 1024:(th + 1) * 1024].rearrange("(ic p) t -> p ic t", p=128))
                            for t in range(8):
                                tt = th * 8 + t
                                if e_i == 0:
                                    r.wait("pe", ve_s, max(m_wo_ready, free_bank[t], m_attn_ve))
                                    r.wait("pe", sc_s, m_ter2_sc)
                                    for k in range(16):
                                        r.opn("pe", lambda e, k=k, tt=tt, t=t, fpar=fpar: e.matmul(
                                            psF[:, t * 512:(t + 1) * 512],
                                            QO[:, k * T + tt * 128: k * T + (tt + 1) * 128],
                                            wobf[:, fpar * 8192 + k * 512: fpar * 8192 + (k + 1) * 512],
                                            start=(k == 0), stop=False))
                                r.wait("pe", ve_s, m_dnc)
                                r.wait("pe", d_hb[ep], m_hh)
                                r.wait("pe", sc_s, m_ter2_sc)
                                for k in range(8):
                                    r.opn("pe", lambda e, k=k, t=t, ep=ep: e.matmul(
                                        psF[:, t * 512:(t + 1) * 512],
                                        hbb[:, ep * 8192 + k * 1024 + t * 128: ep * 8192 + k * 1024 + (t + 1) * 128],
                                        dnbf[:, ep * 4096 + k * 512: ep * 4096 + (k + 1) * 512],
                                        start=False, stop=False))
                                r.op("pe", pe_s, lambda e, e_i=e_i, tt=tt, t=t, f=f: e.matmul(
                                    psF[:, t * 512:(t + 1) * 512],
                                    ter2P[e_i // 3][(e_i % 3) * 32:(e_i % 3) * 32 + 16, tt * 128:(tt + 1) * 128],
                                    dBTP[e_i // 3][(e_i % 3) * 32:(e_i % 3) * 32 + 16, f * 512:(f + 1) * 512],
                                    start=False, stop=(e_i == E - 1)))
                                m_pk = pe_s.n
                                if e_i == E - 1:
                                    xp = t % 2
                                    r.wait("gp", ve_s, free_xres[xp])
                                    m_xr = r.dma("gp", d_xr[xp], xres[:, xp * 512:(xp + 1) * 512],
                                                 dr_x[tt * 128:(tt + 1) * 128, f * 512:(f + 1) * 512])
                                    r.wait("ve", pe_s, m_pk)
                                    r.wait("ve", d_xr[xp], m_xr)
                                    r.wait("ve", d_ot[xp], free_osb[xp])
                                    m_ov = r.op("ve", ve_s, lambda e, t=t, xp=xp: e.tensor_tensor(
                                        osb[:, xp * 512:(xp + 1) * 512], psF[:, t * 512:(t + 1) * 512],
                                        xres[:, xp * 512:(xp + 1) * 512], OP.add))
                                    free_bank[t] = m_ov
                                    free_xres[xp] = m_ov
                                    r.wait("gp", ve_s, m_ov)
                                    m_os = r.dma("gp", d_ot[xp],
                                                 dr_out[tt * 128:(tt + 1) * 128, f * 512:(f + 1) * 512],
                                                 osb[:, xp * 512:(xp + 1) * 512])
                                    free_osb[xp] = m_os
                            free_dn[ep] = pe_s.n
                            free_hbb[ep] = pe_s.n
                        free_wo_par[fpar] = pe_s.n
            r.wait("gp", d_ot[0], d_ot[0].n)
            r.wait("gp", d_ot[1], d_ot[1].n)
        else:
            r.wait("gp", d_sp, d_sp.n)

        r.emit(block)
    return nc



def _host_consts(core):
    cl = core % 4
    idbf = np.eye(128, dtype=ml_dtypes.bfloat16)
    idf = np.eye(128, dtype=np.float32)
    mask = np.zeros((128, 4 * 512), np.float32)
    for mi in range(4):
        o = mi * 128
        p = np.arange(128)[:, None]
        q = np.arange(512)[None, :]
        mask[:, mi * 512:(mi + 1) * 512] = np.where(q >= p + o, 0.0, NEG)
    cfg = np.zeros((1, 64), np.int32)
    chA, chB = cl, 7 - cl
    units = [(0, chA), (1, chB)]
    for j in range(chA):
        units.append((0, j))
    for j in range(chB):
        units.append((1, j))
    assert len(units) == 9
    for u, (qc, j) in enumerate(units):
        rk = j if j < 4 else 7 - j
        lci = 0 if j < 4 else 1
        cfg[0, u * 4 + 0] = qc
        cfg[0, u * 4 + 1] = j
    cfg[0, 40] = cl
    return {"c_idbf": idbf, "c_idf": idf, "c_mask": mask, "c_cfg": cfg}


def _shard_x(x, core):
    cl = core % 4
    b = core // 4
    chA, chB = cl, 7 - cl
    return np.ascontiguousarray(np.concatenate(
        [x[b, chA * 1024:(chA + 1) * 1024], x[b, chB * 1024:(chB + 1) * 1024]], axis=0))


def kernel(L0_attn_x, L0_attn_w_qkv, L0_attn_w_out, L0_ffn_gate_weight,
           L0_ffn_gate_lora_A, L0_ffn_gate_lora_B, L0_ffn_gate_up_proj,
           L0_ffn_down_proj, L0_ffn_gup_lora_A, L0_ffn_gup_lora_B,
           L0_ffn_down_lora_A, L0_ffn_down_lora_B):
    x = np.asarray(L0_attn_x, np.float32)
    weights = {
        "wqkv": np.ascontiguousarray(np.asarray(L0_attn_w_qkv, np.float32)),
        "wout": np.ascontiguousarray(np.asarray(L0_attn_w_out, np.float32)),
        "gatew": np.ascontiguousarray(np.asarray(L0_ffn_gate_weight, np.float32)),
        "gateA": np.ascontiguousarray(np.asarray(L0_ffn_gate_lora_A, np.float32)),
        "gateB": np.ascontiguousarray(np.asarray(L0_ffn_gate_lora_B, np.float32)),
        "gup": np.ascontiguousarray(np.asarray(L0_ffn_gate_up_proj, np.float32)),
        "down": np.ascontiguousarray(np.asarray(L0_ffn_down_proj, np.float32)),
        "gupA": np.ascontiguousarray(np.asarray(L0_ffn_gup_lora_A, np.float32)),
        "gupB": np.ascontiguousarray(np.asarray(L0_ffn_gup_lora_B, np.float32)),
        "downA": np.ascontiguousarray(np.asarray(L0_ffn_down_lora_A, np.float32)),
        "downB": np.ascontiguousarray(np.asarray(L0_ffn_down_lora_B, np.float32)),
    }
    nc = build(taps=TAPS, phases=PHASES)
    in_maps = []
    for c in range(8):
        m = dict(weights)
        m["x"] = _shard_x(x, c)
        m.update(_host_consts(c))
        in_maps.append(m)
    res = run_bass_kernel_spmd(nc, in_maps, core_ids=list(range(8)))
    out = np.zeros((2, 8192, H), np.float32)
    for c in range(8):
        cl, b = c % 4, c // 4
        chA, chB = cl, 7 - cl
        o = res.results[c]["out"]
        out[b, chA * 1024:(chA + 1) * 1024] = o[0:1024]
        out[b, chB * 1024:(chB + 1) * 1024] = o[1024:2048]
    return out


# version 07 (final)
# speedup vs baseline: 1.1817x; 1.0092x over previous
import numpy as np
import ml_dtypes
from contextlib import ExitStack

import concourse.bass as bass
import concourse.mybir as mybir
from concourse.bass_utils import run_bass_kernel_spmd

F32 = mybir.dt.float32
BF16 = mybir.dt.bfloat16
I32 = mybir.dt.int32
AF = mybir.ActivationFunctionType
OP = mybir.AluOpType
AX = mybir.AxisListType.X
ds = bass.ds

H = 2048
NHEAD = 16
DH = 128
T = 2048
E = 8
II = 1024
R = 16
SCALING = 2.0
NEG = -1e9
EPS = 1e-5
ISQD = float(1.0 / np.sqrt(DH))
GROUPS = [[0, 1, 2, 3], [4, 5, 6, 7]]

PHASES = 7
TAPS = ()


def _diag_status(s, qh):
    if s * 128 + 127 <= qh * 512:
        return "o"
    if s * 128 > qh * 512 + 511:
        return "c"
    return "p"


class Sem:
    def __init__(self, nc, es, name):
        self.h = es.enter_context(nc.semaphore(name))
        self.n = 0


class Rec:

    def __init__(self):
        self.ops = {k: [] for k in ("sy", "pe", "ve", "sc", "gp")}

    def raw(self, eng, fn):
        self.ops[eng].append(fn)

    def wait(self, eng, sem, val):
        if val > 0:
            self.ops[eng].append(lambda e, s=sem, v=val: e.wait_ge(s.h, v))

    def op(self, eng, sem, fn):
        def g(e, fn=fn, sem=sem):
            fn(e).then_inc(sem.h, 1)
        self.ops[eng].append(g)
        sem.n += 1
        return sem.n

    def opn(self, eng, fn):
        self.ops[eng].append(fn)

    def barrier(self, bar, drains):
        for eng, sems in drains.items():
            for sm in sems:
                self.wait(eng, sm, sm.n)
        for eng in ("sy", "pe", "ve", "sc", "gp"):
            def g(e, bar=bar):
                e.nop().then_inc(bar.h, 1)
            self.ops[eng].append(g)
            bar.n += 1
        tgt = bar.n
        for eng in ("sy", "pe", "ve", "sc", "gp"):
            self.wait(eng, bar, tgt)

    def vsync(self, eng, sem):
        def g(e, sem=sem):
            e.nop().then_inc(sem.h, 1)
        self.ops[eng].append(g)
        sem.n += 1
        self.wait(eng, sem, sem.n)

    def dma(self, eng, sem, out, in_, **kw):
        def g(e, out=out, in_=in_, kw=kw, sem=sem):
            e.dma_start(out=out, in_=in_, **kw).then_inc(sem.h, 16)
        self.ops[eng].append(g)
        sem.n += 16
        return sem.n

    def emit(self, block):
        ops = self.ops

        @block.sync
        def _(e):
            for fn in ops["sy"]:
                fn(e)

        @block.tensor
        def _(e):
            for fn in ops["pe"]:
                fn(e)

        @block.vector
        def _(e):
            for fn in ops["ve"]:
                fn(e)

        @block.scalar
        def _(e):
            for fn in ops["sc"]:
                fn(e)

        @block.gpsimd
        def _(e):
            for fn in ops["gp"]:
                fn(e)


def build(taps=(), phases=7):
    nc = bass.Bass("TRN2", target_bir_lowering=False, debug=False, detect_race_conditions=False)
    es = ExitStack()

    dr_x = nc.dram_tensor("x", [T, H], F32, kind="ExternalInput").ap()
    dr_wqkv = nc.dram_tensor("wqkv", [H, 3 * H], F32, kind="ExternalInput").ap()
    dr_wout = nc.dram_tensor("wout", [H, H], F32, kind="ExternalInput").ap()
    dr_gw = nc.dram_tensor("gatew", [E, H], F32, kind="ExternalInput").ap()
    dr_gA = nc.dram_tensor("gateA", [R, H], F32, kind="ExternalInput").ap()
    dr_gB = nc.dram_tensor("gateB", [E, R], F32, kind="ExternalInput").ap()
    dr_gup = nc.dram_tensor("gup", [E, H, 2 * II], F32, kind="ExternalInput").ap()
    dr_down = nc.dram_tensor("down", [E, II, H], F32, kind="ExternalInput").ap()
    dr_gupA = nc.dram_tensor("gupA", [E, R, H], F32, kind="ExternalInput").ap()
    dr_gupB = nc.dram_tensor("gupB", [E, 2 * II, R], F32, kind="ExternalInput").ap()
    dr_downA = nc.dram_tensor("downA", [E, R, II], F32, kind="ExternalInput").ap()
    dr_downB = nc.dram_tensor("downB", [E, H, R], F32, kind="ExternalInput").ap()
    dr_idbf = nc.dram_tensor("c_idbf", [128, 128], BF16, kind="ExternalInput").ap()
    dr_idf = nc.dram_tensor("c_idf", [128, 128], F32, kind="ExternalInput").ap()
    dr_mask = nc.dram_tensor("c_mask", [128, 4 * 512], F32, kind="ExternalInput").ap()
    dr_cfg = nc.dram_tensor("c_cfg", [1, 64], I32, kind="ExternalInput").ap()
    dr_out = nc.dram_tensor("out", [T, H], F32, kind="ExternalOutput").ap()

    dr_ck = nc.dram_tensor("contribK", [H, T], BF16, kind="Internal").ap()
    dr_cv = nc.dram_tensor("contribV", [T, H], BF16, kind="Internal").ap()
    dr_gk = nc.dram_tensor("gatheredK", [4 * H, T], BF16, kind="Internal").ap()
    dr_gv = nc.dram_tensor("gatheredV", [4 * T, H], BF16, kind="Internal").ap()
    dr_h = nc.dram_tensor("hspill", [E, II, T], BF16, kind="Internal").ap()
    dr_wr = nc.dram_tensor("wrout", [E, T], F32, kind="Internal").ap()
    dr_kni = nc.dram_tensor("kn_in", [1, NHEAD], F32, kind="Internal").ap()
    dr_kno = nc.dram_tensor("kn_out", [1, NHEAD], F32, kind="Internal").ap()

    tap_aps = {}
    if taps:
        tshapes = {"xn": ([128, 16 * T], BF16), "qo": ([128, NHEAD * T], BF16),
                   "gk": ([4 * H, T], BF16), "gv": ([4 * T, H], BF16),
                   "h": ([E, II, T], BF16), "w": ([E, T], F32),
                   "negc": ([128, 32], F32), "lnst": ([128, 16], F32)}
        for t in taps:
            sh, dt = tshapes[t]
            tap_aps[t] = nc.dram_tensor("tap_" + t, sh, dt, kind="ExternalOutput").ap()

    m_xn_done = 0
    m_qkv_pe = 0
    m_qkv_ve = 0
    m_wsp = 0
    m_gup_done = (0, 0)
    m_attn_ve = 0

    with es:
        sbt = lambda name, shape, dtype: es.enter_context(nc.sbuf_tensor(name, shape, dtype))
        xn_es = ExitStack()
        xn = xn_es.enter_context(nc.sbuf_tensor("xn", [128, 16 * T], BF16, side="right"))
        QO = sbt("QO", [128, NHEAD * T], BF16)
        idbf = sbt("idbf", [128, 128], BF16)
        idf = sbt("idf", [128, 128], F32)
        cfgs = sbt("cfgs", [1, 64], I32)
        ones_col = sbt("ones_col", [128, 1], F32)
        ones_col_bf = sbt("ones_col_bf", [128, 1], BF16)
        ones_row = sbt("ones_row", [1, 128], F32)
        zero_bf = sbt("zero_bf", [128, 512], BF16)
        eps_t = sbt("eps_t", [128, 1], F32)
        qn2 = sbt("qn2", [1, 32], F32)
        kn2 = sbt("kn2", [1, NHEAD], F32)
        negc_rep = sbt("negc_rep", [128, 32], F32)
        w45_es = ExitStack()
        wT8 = w45_es.enter_context(nc.sbuf_tensor("wT8", [E, T], F32, side="right"))

        S = lambda name: Sem(nc, es, name)
        d_in = S("d_in")
        d_x = [S("d_x0"), S("d_x1")]
        d_w2 = [S("d_w20"), S("d_w21")]
        d_kst = [S("d_kst0"), S("d_kst1")]
        d_vtt = [S("d_vtt0"), S("d_vtt1")]
        d_g4 = S("d_g4")
        d_a5 = S("d_a5")
        d_b5 = S("d_b5")
        d_w1 = S("d_w1")
        d_w5 = [S("d_w50"), S("d_w51")]
        d_h = [S("d_h0"), S("d_h1")]
        d_kv = [S("d_kv0"), S("d_kv1")]
        d_kn6 = S("d_kn6")
        d_b7 = S("d_b7")
        d_a7 = S("d_a7")
        d_hbs = [S("d_hbs0"), S("d_hbs1")]
        d_wo = [S("d_wo0"), S("d_wo1")]
        d_dn = [S("d_dn0"), S("d_dn1")]
        d_hb = [S("d_hb0"), S("d_hb1")]
        d_xr = [S("d_xr0"), S("d_xr1")]
        d_ot = [S("d_ot0"), S("d_ot1")]
        d_sp = S("d_sp")
        c_k = S("c_k")
        c_v = S("c_v")
        c_nn = S("c_nn")
        pe_s = S("pe_s")
        ve_s = S("ve_s")
        sc_s = S("sc_s")
        gp_s = S("gp_s")
        bar = S("bar")

        def phase_barrier():
            r.barrier(bar, {
                "sy": [d_w2[0], d_w2[1], d_kv[0], d_kv[1], d_g4, d_a5, d_b5,
                       d_w1, d_w5[0], d_w5[1], d_hbs[0], d_hbs[1], d_wo[0],
                       d_wo[1], d_dn[0], d_dn[1], d_kn6, d_b7, d_a7],
                "gp": [d_in, d_x[0], d_x[1], d_kst[0], d_kst[1], d_vtt[0],
                       d_vtt[1], d_sp, d_h[0], d_h[1], d_hb[0], d_hb[1],
                       d_xr[0], d_xr[1], d_ot[0], d_ot[1]],
            })

        rg_sy = [es.enter_context(nc.sync.register(f"rs{i}")) for i in range(3)]
        rg_pe = es.enter_context(nc.tensor.register("rt0"))
        rg_ve = es.enter_context(nc.vector.register("rv0"))
        rg_sc = es.enter_context(nc.scalar.register("rc0"))
        rg_gp = es.enter_context(nc.gpsimd.register("rg0"))

        block = es.enter_context(nc.Block())
        r = Rec()

        r.dma("gp", d_in, idbf[:, :], dr_idbf)
        r.dma("gp", d_in, idf[:, :], dr_idf)
        m_consts = r.dma("gp", d_in, cfgs[:, :], dr_cfg)
        r.opn("ve", lambda e: e.memset(ones_col[:, :], 1.0))
        r.opn("ve", lambda e: e.memset(ones_col_bf[:, :], 1.0))
        r.opn("ve", lambda e: e.memset(ones_row[:, :], 1.0))
        r.opn("ve", lambda e: e.memset(zero_bf[:, :], 0.0))
        r.opn("ve", lambda e: e.memset(eps_t[:, :], EPS))
        r.opn("ve", lambda e: e.memset(qn2[:, :], 0.0))
        m_vec_consts = r.op("ve", ve_s, lambda e: e.memset(kn2[:, :], 0.0))

        with ExitStack() as _ph:
            xbuf = _ph.enter_context(nc.sbuf_tensor("xbuf", [128, 2 * H], F32))
            junk = _ph.enter_context(nc.sbuf_tensor("junk", [128, H], BF16))
            lnst = _ph.enter_context(nc.sbuf_tensor("lnst", [128, 16], F32))
            xntm = _ph.enter_context(nc.sbuf_tensor("xntm", [128, 2 * H], BF16))
            lntmp = _ph.enter_context(nc.sbuf_tensor("lntmp", [128, H], F32))
            lnac = _ph.enter_context(nc.sbuf_tensor("lnac", [128, 1], F32))
            psLN = _ph.enter_context(nc.psum_tensor("psLN", [128, 1024], F32))
            r.opn("ve", lambda e: e.memset(lnst[:, :], 0.0))
            free_xb = [0, 0]
            free_half = [0, 0]
            prev_var = 0
            xn3 = xn.ap().rearrange("p (k t) -> p k t", t=T)
            ps3 = psLN.ap().rearrange("p (g c) -> p g c", c=128)
            for t in range(16):
                par = t % 2
                xb = xbuf[:, par * H:(par + 1) * H]
                xt = xntm[:, par * H:(par + 1) * H]
                r.wait("gp", ve_s, free_xb[par])
                m_x = r.dma("gp", d_x[par], xb, dr_x[t * 128:(t + 1) * 128, :])
                r.wait("ve", d_x[par], m_x)
                m_s1 = r.op("ve", ve_s, lambda e, xb=xb: e.reduce_sum(lnst[:, 0:1], xb, axis=AX))
                r.wait("sc", d_x[par], m_x)
                r.wait("sc", ve_s, m_s1)
                r.opn("sc", lambda e: e.activation(lnst[:, 1:2], lnst[:, 0:1], AF.Copy, scale=1.0 / H))
                m_sq = r.op("sc", sc_s, lambda e, xb=xb: e.activation(
                    junk[:, :], xb, AF.Square, accum_out=lnac[:, 0:1]))
                r.wait("ve", sc_s, m_sq)
                r.vsync("ve", ve_s)
                r.opn("ve", lambda e: e.tensor_scalar(lnst[:, 9:10], lnst[:, 1:2], lnst[:, 0:1], None, OP.mult))
                r.vsync("ve", ve_s)
                m_var = r.op("ve", ve_s, lambda e: e.tensor_scalar(
                    lnst[:, 5:6], lnac[:, 0:1], lnst[:, 9:10], None, OP.subtract))
                prev_var = m_var
                r.wait("sc", ve_s, m_var)
                m_sd = r.op("sc", sc_s, lambda e: e.activation(
                    lnst[:, 6:7], lnst[:, 5:6], AF.Sqrt, scale=1.0 / H, bias=eps_t[:, 0:1]))
                r.wait("ve", sc_s, m_sd)
                r.opn("ve", lambda e: e.reciprocal(lnst[:, 7:8], lnst[:, 6:7]))
                r.opn("ve", lambda e: e.tensor_scalar(lnst[:, 8:9], lnst[:, 1:2], -1.0, None, OP.mult))
                r.vsync("ve", ve_s)
                r.opn("ve", lambda e, xb=xb: e.tensor_scalar(
                    lntmp[:, :], xb, lnst[:, 8:9], None, OP.add))
                m_xnt = r.op("ve", ve_s, lambda e, xt=xt: e.tensor_scalar(
                    xt, lntmp[:, :], lnst[:, 7:8], None, OP.mult))
                free_xb[par] = m_xnt
                for g in range(4):
                    half = g % 2
                    r.wait("pe", ve_s, max(m_xnt, free_half[half], m_vec_consts))
                    r.wait("pe", d_in, m_consts)
                    for kk in range(4):
                        k = g * 4 + kk
                        r.op("pe", pe_s, lambda e, xt=xt, k=k, half=half, kk=kk: e.matmul(
                            psLN[:, half * 512 + kk * 128: half * 512 + (kk + 1) * 128],
                            xt[:, k * 128:(k + 1) * 128], idbf[:, :],
                            start=True, stop=True))
                    m_tr = pe_s.n
                    r.wait("ve", pe_s, m_tr)
                    m_ev = r.op("ve", ve_s, lambda e, g=g, t=t, half=half: e.tensor_copy(
                        xn3[:, g * 4:(g + 1) * 4, t * 128:(t + 1) * 128],
                        ps3[:, half * 4:(half + 1) * 4, :]))
                    free_half[half] = m_ev
            m_xn_done = ve_s.n
            if "lnst" in tap_aps:
                r.wait("gp", ve_s, m_xn_done)
                r.dma("gp", d_sp, tap_aps["lnst"], lnst[:, :])

        if "xn" in tap_aps:
            r.wait("gp", ve_s, m_xn_done)
            r.dma("gp", d_sp, tap_aps["xn"], xn[:, :])

        phase_barrier()

        if phases >= 2:
            with ExitStack() as _ph:
                wst = _ph.enter_context(nc.sbuf_tensor("wst", [128, 2 * H], F32))
                wbf = _ph.enter_context(nc.sbuf_tensor("wbf", [128, 2 * H], BF16))
                kst = _ph.enter_context(nc.sbuf_tensor("kst", [128, 2 * 512], BF16))
                vst = _ph.enter_context(nc.sbuf_tensor("vst", [128, 2 * 512], BF16))
                vtt = _ph.enter_context(nc.sbuf_tensor("vtt", [128, 2 * 512], BF16))
                sqb = _ph.enter_context(nc.sbuf_tensor("sqb", [128, 4 * 512], BF16))
                nrm = _ph.enter_context(nc.sbuf_tensor("nrm", [1, 8], F32))
                nsb = _ph.enter_context(nc.sbuf_tensor("nsb", [1, 2048], F32))
                psQ = _ph.enter_context(nc.psum_tensor("psQ", [128, 4 * 512], F32))
                psN = _ph.enter_context(nc.psum_tensor("psN", [128, 1024], F32))
                psVT = _ph.enter_context(nc.psum_tensor("psVT", [128, 512], F32))
                wq3 = dr_wqkv.rearrange("(kc p) m -> p kc m", p=128)
                free_w = [0, 0]
                free_kst = [0, 0]
                free_vst = [0, 0]
                free_vtt = [0, 0]
                free_psq = [0, 0, 0, 0]
                free_psn = [0, 0]
                free_sqb = [0, 0, 0, 0]
                free_psvt = 0
                norm_q = []
                for m in range(48):
                    par = m % 2
                    r.wait("sy", pe_s, free_w[par])
                    m_w = r.dma("sy", d_w2[par],
                                wst[:, par * H:(par + 1) * H].rearrange("p (kc m) -> p kc m", m=128),
                                wq3[:, :, m * 128:(m + 1) * 128])
                    r.wait("ve", d_w2[par], m_w)
                    m_cv = r.op("ve", ve_s, lambda e, par=par: e.tensor_copy(
                        wbf[:, par * H:(par + 1) * H], wst[:, par * H:(par + 1) * H]))
                    last_mm = 0
                    for t in range(4):
                        r.wait("pe", ve_s, max(m_cv, m_xn_done))
                        r.wait("pe", sc_s, free_psq[t])
                        for k in range(16):
                            r.op("pe", pe_s, lambda e, par=par, k=k, t=t: e.matmul(
                                psQ[:, t * 512:(t + 1) * 512],
                                wbf[:, par * H + k * 128: par * H + (k + 1) * 128],
                                xn[:, k * T + t * 512: k * T + (t + 1) * 512],
                                start=(k == 0), stop=(k == 15)))
                        m_mm = pe_s.n
                        last_mm = m_mm
                        if norm_q:
                            norm_q.pop(0)()
                        if m < 16:
                            hq = m
                            r.wait("sc", pe_s, m_mm)
                            m_ev = r.op("sc", sc_s, lambda e, t=t, hq=hq: e.activation(
                                QO[:, hq * T + t * 512: hq * T + (t + 1) * 512],
                                psQ[:, t * 512:(t + 1) * 512], AF.Copy, scale=ISQD))
                            free_psq[t] = m_ev
                            r.wait("sc", pe_s, free_sqb[t])
                            m_sq = r.op("sc", sc_s, lambda e, t=t, hq=hq: e.activation(
                                sqb[:, t * 512:(t + 1) * 512],
                                QO[:, hq * T + t * 512: hq * T + (t + 1) * 512], AF.Square))

                            def norm_fn(m_sq=m_sq, t=t, hq=hq):
                                nh = t % 2
                                r.wait("pe", sc_s, m_sq)
                                r.wait("pe", ve_s, free_psn[nh])
                                m_on = r.op("pe", pe_s, lambda e, nh=nh, t=t: e.matmul(
                                    psN[0:1, nh * 512:(nh + 1) * 512], ones_col_bf[:, :],
                                    sqb[:, t * 512:(t + 1) * 512], start=True, stop=True))
                                free_sqb[t] = m_on
                                r.wait("ve", pe_s, m_on)
                                m_nm = r.op("ve", ve_s, lambda e, nh=nh, t=t: e.tensor_copy(
                                    nsb[0:1, t * 512:(t + 1) * 512], psN[0:1, nh * 512:(nh + 1) * 512]))
                                free_psn[nh] = m_nm
                                if t == 3:
                                    r.opn("ve", lambda e: e.reduce_max(nrm[0:1, 0:1], nsb[0:1, 0:1024], axis=AX))
                                    m_nm2 = r.op("ve", ve_s, lambda e: e.reduce_max(nrm[0:1, 1:2], nsb[0:1, 1024:2048], axis=AX))
                                    r.wait("sc", ve_s, m_nm2)
                                    r.opn("sc", lambda e, hq=hq: e.activation(qn2[0:1, hq * 2: hq * 2 + 1], nrm[0:1, 0:1], AF.Copy))
                                    m_qn = r.op("sc", sc_s, lambda e, hq=hq: e.activation(qn2[0:1, hq * 2 + 1: hq * 2 + 2], nrm[0:1, 1:2], AF.Copy))
                                    r.wait("ve", sc_s, m_qn)
                                    r.op("ve", ve_s, lambda e: e.nop())
                            norm_q.append(norm_fn)
                        elif m < 32:
                            hk = m - 16
                            kp = t % 2
                            r.wait("sc", pe_s, m_mm)
                            r.wait("sc", d_kst[kp], free_kst[kp])
                            m_ev = r.op("sc", sc_s, lambda e, t=t, kp=kp: e.activation(
                                kst[:, kp * 512:(kp + 1) * 512], psQ[:, t * 512:(t + 1) * 512], AF.Copy))
                            free_psq[t] = m_ev
                            r.wait("gp", sc_s, m_ev)
                            m_kd = r.dma("gp", d_kst[kp],
                                         dr_ck[hk * 128:(hk + 1) * 128, t * 512:(t + 1) * 512],
                                         kst[:, kp * 512:(kp + 1) * 512])
                            free_kst[kp] = m_kd
                            r.wait("sc", pe_s, free_sqb[t])
                            m_sq = r.op("sc", sc_s, lambda e, t=t, kp=kp: e.activation(
                                sqb[:, t * 512:(t + 1) * 512], kst[:, kp * 512:(kp + 1) * 512], AF.Square))

                            def norm_fn(m_sq=m_sq, t=t, hk=hk):
                                nh = t % 2
                                r.wait("pe", sc_s, m_sq)
                                r.wait("pe", ve_s, free_psn[nh])
                                m_on = r.op("pe", pe_s, lambda e, nh=nh, t=t: e.matmul(
                                    psN[0:1, nh * 512:(nh + 1) * 512], ones_col_bf[:, :],
                                    sqb[:, t * 512:(t + 1) * 512], start=True, stop=True))
                                free_sqb[t] = m_on
                                r.wait("ve", pe_s, m_on)
                                m_nm = r.op("ve", ve_s, lambda e, nh=nh, t=t: e.tensor_copy(
                                    nsb[0:1, t * 512:(t + 1) * 512], psN[0:1, nh * 512:(nh + 1) * 512]))
                                free_psn[nh] = m_nm
                                if t == 3:
                                    m_nm2 = r.op("ve", ve_s, lambda e: e.reduce_max(nrm[0:1, 2:3], nsb[0:1, 0:2048], axis=AX))
                                    r.wait("sc", ve_s, m_nm2)
                                    m_kn = r.op("sc", sc_s, lambda e, hk=hk: e.activation(kn2[0:1, hk:hk + 1], nrm[0:1, 2:3], AF.Copy))
                                    r.wait("ve", sc_s, m_kn)
                                    r.op("ve", ve_s, lambda e: e.nop())
                            norm_q.append(norm_fn)
                        else:
                            hv = m - 32
                            vp = t % 2
                            r.wait("sc", pe_s, m_mm)
                            r.wait("sc", pe_s, free_vst[vp])
                            m_ev = r.op("sc", sc_s, lambda e, t=t, vp=vp: e.activation(
                                vst[:, vp * 512:(vp + 1) * 512], psQ[:, t * 512:(t + 1) * 512], AF.Copy))
                            free_psq[t] = m_ev
                            r.wait("pe", sc_s, m_ev)
                            r.wait("pe", ve_s, free_psvt)
                            for i in range(4):
                                r.op("pe", pe_s, lambda e, vp=vp, i=i: e.matmul(
                                    psVT[:, i * 128:(i + 1) * 128],
                                    vst[:, vp * 512 + i * 128: vp * 512 + (i + 1) * 128],
                                    idbf[:, :], start=True, stop=True))
                            m_tr = pe_s.n
                            free_vst[vp] = m_tr
                            r.wait("ve", pe_s, m_tr)
                            r.wait("ve", d_vtt[vp], free_vtt[vp])
                            m_vv = r.op("ve", ve_s, lambda e, vp=vp: e.tensor_copy(
                                vtt[:, vp * 512:(vp + 1) * 512], psVT[:, :]))
                            free_psvt = m_vv
                            r.wait("gp", ve_s, m_vv)
                            m_vd = r.dma("gp", d_vtt[vp],
                                         dr_cv[t * 512:(t + 1) * 512, hv * 128:(hv + 1) * 128]
                                         .rearrange("(i p) d -> p i d", p=128),
                                         vtt[:, vp * 512:(vp + 1) * 512].rearrange("p (i d) -> p i d", d=128))
                            free_vtt[vp] = m_vd
                    free_w[par] = last_mm
                while norm_q:
                    norm_q.pop(0)()
                m_qkv_pe = pe_s.n
                m_qkv_ve = ve_s.n
            r.wait("gp", ve_s, m_qkv_ve)
            r.dma("gp", d_sp, dr_kni, kn2[:, :])

        if phases >= 3:
            r.wait("gp", d_sp, d_sp.n)
            r.wait("gp", d_kst[0], d_kst[0].n)
            r.wait("gp", d_kst[1], d_kst[1].n)
            r.wait("gp", d_vtt[0], d_vtt[0].n)
            r.wait("gp", d_vtt[1], d_vtt[1].n)
            r.op("gp", c_k, lambda e: e.collective_compute(
                "AllGather", OP.bypass, replica_groups=GROUPS,
                ins=[dr_ck], outs=[dr_gk]))
            r.op("gp", c_v, lambda e: e.collective_compute(
                "AllGather", OP.bypass, replica_groups=GROUPS,
                ins=[dr_cv], outs=[dr_gv]))
            r.op("gp", c_nn, lambda e: e.collective_compute(
                "AllReduce", OP.max, replica_groups=GROUPS,
                ins=[dr_kni], outs=[dr_kno]))

        if phases >= 4:
            phase_barrier()

        if phases >= 4:
            with ExitStack() as _ph:
                gst = _ph.enter_context(nc.sbuf_tensor("gst", [16, 2 * H], F32))
                gbst = _ph.enter_context(nc.sbuf_tensor("gbst", [8, 16], F32))
                gbf = _ph.enter_context(nc.sbuf_tensor("gbf", [16, 2 * H], BF16))
                gbbf = _ph.enter_context(nc.sbuf_tensor("gbbf", [8, 16], BF16))
                wgT = _ph.enter_context(nc.sbuf_tensor("wgT", [128, 16 * 8], BF16))
                agT = _ph.enter_context(nc.sbuf_tensor("agT", [128, 16 * 16], BF16))
                bgT = _ph.enter_context(nc.sbuf_tensor("bgT", [16, 8], BF16))
                rterg = _ph.enter_context(nc.sbuf_tensor("rterg", [16, 512], BF16))
                lgsb = _ph.enter_context(nc.sbuf_tensor("lgsb", [8, 512], F32))
                rst = _ph.enter_context(nc.sbuf_tensor("rst", [128, 96], F32))
                wtm = _ph.enter_context(nc.sbuf_tensor("wtm", [128, 8], F32))
                psR = _ph.enter_context(nc.psum_tensor("psR", [128, 1024], F32))
                psR2 = _ph.enter_context(nc.psum_tensor("psR2", [128, 512], F32))
                r.dma("sy", d_g4, gst[0:8, 0:H], dr_gw)
                r.dma("sy", d_g4, gst[0:16, H:2 * H], dr_gA)
                m_gl = r.dma("sy", d_g4, gbst[:, :], dr_gB)
                r.wait("ve", d_g4, m_gl)
                r.opn("ve", lambda e: e.tensor_copy(gbf[0:8, 0:H], gst[0:8, 0:H]))
                r.opn("ve", lambda e: e.tensor_copy(gbf[0:16, H:2 * H], gst[0:16, H:2 * H]))
                m_gc = r.op("ve", ve_s, lambda e: e.tensor_copy(gbbf[:, :], gbst[:, :]))
                last_ev = [0, 0]
                r.wait("pe", ve_s, m_gc)
                for k in range(16):
                    pp = k % 2
                    r.wait("pe", ve_s, last_ev[pp])
                    r.op("pe", pe_s, lambda e, k=k, pp=pp: e.matmul(
                        psR[:, pp * 512: pp * 512 + 8],
                        gbf[0:8, k * 128:(k + 1) * 128], idbf[0:8, 0:8],
                        start=True, stop=True))
                    m_t = pe_s.n
                    r.wait("ve", pe_s, m_t)
                    last_ev[pp] = r.op("ve", ve_s, lambda e, k=k, pp=pp: e.tensor_copy(
                        wgT[:, k * 8:(k + 1) * 8], psR[:, pp * 512: pp * 512 + 8]))
                for k in range(16):
                    pp = k % 2
                    r.wait("pe", ve_s, last_ev[pp])
                    r.op("pe", pe_s, lambda e, k=k, pp=pp: e.matmul(
                        psR[:, pp * 512: pp * 512 + 16],
                        gbf[0:16, H + k * 128: H + (k + 1) * 128], idbf[0:16, 0:16],
                        start=True, stop=True))
                    m_t = pe_s.n
                    r.wait("ve", pe_s, m_t)
                    last_ev[pp] = r.op("ve", ve_s, lambda e, k=k, pp=pp: e.tensor_copy(
                        agT[:, k * 16:(k + 1) * 16], psR[:, pp * 512: pp * 512 + 16]))
                r.wait("pe", ve_s, last_ev[0])
                r.op("pe", pe_s, lambda e: e.matmul(psR[0:16, 0:8], gbbf[:, :], idbf[0:8, 0:8],
                                               start=True, stop=True))
                m_t3 = pe_s.n
                r.wait("ve", pe_s, m_t3)
                m_bgT = r.op("ve", ve_s, lambda e: e.tensor_copy(bgT[:, :], psR[0:16, 0:8]))

                prev_psR_sc = 0
                prev_psR_ve = m_bgT
                for t in range(4):
                    r.wait("pe", ve_s, prev_psR_ve)
                    r.wait("pe", sc_s, prev_psR_sc)
                    for k in range(16):
                        r.op("pe", pe_s, lambda e, k=k, t=t: e.matmul(
                            psR[0:16, 0:512], agT[:, k * 16:(k + 1) * 16],
                            xn[:, k * T + t * 512: k * T + (t + 1) * 512],
                            start=(k == 0), stop=(k == 15)))
                    m_ter = pe_s.n
                    r.wait("sc", pe_s, m_ter)
                    m_terg = r.op("sc", sc_s, lambda e: e.activation(
                        rterg[:, :], psR[0:16, 0:512], AF.Copy, scale=SCALING))
                    r.wait("pe", sc_s, m_terg)
                    for k in range(16):
                        r.opn("pe", lambda e, k=k, t=t: e.matmul(
                            psR2[0:8, 0:512], wgT[:, k * 8:(k + 1) * 8],
                            xn[:, k * T + t * 512: k * T + (t + 1) * 512],
                            start=(k == 0), stop=False))
                    r.op("pe", pe_s, lambda e: e.matmul(psR2[0:8, 0:512], bgT[:, :], rterg[:, :],
                                                   start=False, stop=True))
                    m_lg = pe_s.n
                    r.wait("ve", pe_s, m_lg)
                    m_lgs = r.op("ve", ve_s, lambda e: e.tensor_copy(lgsb[:, :], psR2[0:8, 0:512]))
                    r.wait("pe", ve_s, m_lgs)
                    for i in range(4):
                        r.op("pe", pe_s, lambda e, i=i: e.matmul(
                            psR[:, 512 + i * 128: 512 + i * 128 + 8],
                            lgsb[0:8, i * 128:(i + 1) * 128],
                            idf[0:8, 0:8], start=True, stop=True))
                    m_lt = pe_s.n
                    for i in range(4):
                        sl = psR[:, 512 + i * 128: 512 + i * 128 + 8]
                        r.wait("ve", pe_s, m_lt)
                        m_nm = r.op("ve", ve_s, lambda e, sl=sl: e.reduce_max(
                            rst[:, 0:1], sl, axis=AX, negate=True))
                        r.wait("sc", ve_s, m_nm)
                        m_px = r.op("sc", sc_s, lambda e, sl=sl: e.activation(
                            rst[:, 8:16], sl, AF.Exp, bias=rst[:, 0:1], accum_out=rst[:, 1:2]))
                        prev_psR_sc = m_px
                        r.wait("ve", sc_s, m_px)
                        r.opn("ve", lambda e: e.reciprocal(rst[:, 2:3], rst[:, 1:2]))
                        r.vsync("ve", ve_s)
                        m_p = r.op("ve", ve_s, lambda e: e.tensor_scalar(rst[:, 16:24], rst[:, 8:16], rst[:, 2:3], None, OP.mult))
                        r.wait("sc", ve_s, m_p)
                        m_pc = r.op("sc", sc_s, lambda e: e.activation(rst[:, 48:56], rst[:, 16:24], AF.Copy))
                        r.wait("ve", sc_s, m_pc)
                        r.opn("ve", lambda e: e.reduce_max(rst[:, 24:25], rst[:, 48:56], axis=AX))
                        r.vsync("ve", ve_s)
                        m_pm = r.op("ve", ve_s, lambda e: e.scalar_tensor_tensor(
                            rst[:, 32:40], rst[:, 48:56], rst[:, 24:25], rst[:, 48:56], OP.is_lt, OP.mult))
                        r.wait("sc", ve_s, m_pm)
                        r.opn("sc", lambda e: e.activation(rst[:, 56:64], rst[:, 32:40], AF.Copy))
                        m_m1c = r.op("sc", sc_s, lambda e: e.activation(rst[:, 28:29], rst[:, 24:25], AF.Copy))
                        r.wait("ve", sc_s, m_m1c)
                        r.opn("ve", lambda e: e.reduce_max(rst[:, 25:26], rst[:, 56:64], axis=AX))
                        r.vsync("ve", ve_s)
                        m_den = r.op("ve", ve_s, lambda e: e.tensor_scalar(rst[:, 26:27], rst[:, 28:29], rst[:, 25:26], None, OP.add))
                        r.wait("sc", ve_s, m_den)
                        m_dc = r.op("sc", sc_s, lambda e: e.activation(rst[:, 29:30], rst[:, 26:27], AF.Copy))
                        r.wait("ve", sc_s, m_dc)
                        r.opn("ve", lambda e: e.reciprocal(rst[:, 27:28], rst[:, 29:30]))
                        r.vsync("ve", ve_s)
                        m_sel = r.op("ve", ve_s, lambda e: e.scalar_tensor_tensor(
                            rst[:, 40:48], rst[:, 48:56], rst[:, 25:26], rst[:, 48:56], OP.is_ge, OP.mult))
                        r.wait("sc", ve_s, m_sel)
                        m_slc = r.op("sc", sc_s, lambda e: e.activation(rst[:, 64:72], rst[:, 40:48], AF.Copy))
                        r.wait("ve", sc_s, m_slc)
                        m_wtm = r.op("ve", ve_s, lambda e: e.tensor_scalar(
                            wtm[:, 0:8], rst[:, 64:72], rst[:, 27:28], None, OP.mult))
                        prev_psR_ve = m_wtm
                        r.wait("pe", ve_s, m_wtm)
                        r.op("pe", pe_s, lambda e: e.matmul(psR2[0:8, 0:128], wtm[:, 0:8], idf[:, :],
                                                       start=True, stop=True))
                        m_wt = pe_s.n
                        r.wait("ve", pe_s, m_wt)
                        m_w8 = r.op("ve", ve_s, lambda e, t=t, i=i: e.tensor_copy(
                            wT8[:, t * 512 + i * 128: t * 512 + (i + 1) * 128], psR2[0:8, 0:128]))
                        r.wait("pe", ve_s, m_w8)
                m_rout_ve = ve_s.n
            r.wait("gp", ve_s, m_rout_ve)
            m_wsp = r.dma("gp", d_sp, dr_wr, wT8[:, :])
            if "w" in tap_aps:
                r.wait("gp", d_sp, m_wsp)
                r.dma("gp", d_sp, tap_aps["w"], dr_wr)

        if phases >= 5:
            phase_barrier()

        if phases >= 5:
            with ExitStack() as _ph:
                agupT = _ph.enter_context(nc.sbuf_tensor("agupT", [128, 256], BF16))
                gast = _ph.enter_context(nc.sbuf_tensor("gast", [16, H // 2], F32))
                gabf = _ph.enter_context(nc.sbuf_tensor("gabf", [16, H], BF16))
                gbst2 = _ph.enter_context(nc.sbuf_tensor("gbst2", [128, 256], F32))
                gbbf2 = _ph.enter_context(nc.sbuf_tensor("gbbf2", [128, 256], BF16))
                bgupT = _ph.enter_context(nc.sbuf_tensor("bgupT", [16, H], BF16))
                gter = _ph.enter_context(nc.sbuf_tensor("gter", [16, T], BF16))
                w1r = _ph.enter_context(nc.sbuf_tensor("w1r", [1, 512], F32))
                wrep = _ph.enter_context(nc.sbuf_tensor("wrep", [128, 4 * 512], BF16))
                gwst = _ph.enter_context(nc.sbuf_tensor("gwst", [128, 4096], F32))
                gwbf = _ph.enter_context(nc.sbuf_tensor("gwbf", [128, 2 * 4096], BF16))
                sgx = _ph.enter_context(nc.sbuf_tensor("sgx", [128, 2 * 512], BF16))
                upx = _ph.enter_context(nc.sbuf_tensor("upx", [128, 2 * 512], BF16))
                hst = _ph.enter_context(nc.sbuf_tensor("hst", [128, 2 * 512], BF16))
                psT5 = _ph.enter_context(nc.psum_tensor("psT5", [128, 512], F32))
                psW5 = _ph.enter_context(nc.psum_tensor("psW5", [128, 512], F32))
                psG = _ph.enter_context(nc.psum_tensor("psG", [128, 2 * 512], F32))
                psU5 = _ph.enter_context(nc.psum_tensor("psU5", [128, 2 * 512], F32))
                prev_t5_sc = 0
                m_prev_gabf_rd = 0
                m_prev_gbbf_rd = 0
                gw3 = dr_gup.rearrange("e (kc p) m -> e p kc m", p=128)
                free_gw = [0, 0]
                m_gw_conv = 0
                free_psg = [0, 0]
                free_psu = [0, 0]
                free_sgx = [0, 0]
                free_upx = [0, 0]
                free_hst = [0, 0]
                m_bc_prev = 0
                for e_i in range(E):
                    m_ac = 0
                    for hf in range(2):
                        r.wait("sy", ve_s, m_ac)
                        r.wait("sy", pe_s, m_prev_gabf_rd)
                        m_a = r.dma("sy", d_a5, gast[:, :], dr_gupA[e_i, :, hf * (H // 2):(hf + 1) * (H // 2)])
                        r.wait("ve", d_a5, m_a)
                        m_ac = r.op("ve", ve_s, lambda e, hf=hf: e.tensor_copy(
                            gabf[:, hf * (H // 2):(hf + 1) * (H // 2)], gast[:, :]))
                    m_agupT = 0
                    for k in range(16):
                        r.wait("pe", ve_s, max(m_ac, m_agupT))
                        r.wait("pe", sc_s, prev_t5_sc)
                        r.op("pe", pe_s, lambda e, k=k: e.matmul(
                            psT5[:, 0:16], gabf[0:16, k * 128:(k + 1) * 128],
                            idbf[0:16, 0:16], start=True, stop=True))
                        m_t = pe_s.n
                        r.wait("ve", pe_s, m_t)
                        m_agupT = r.op("ve", ve_s, lambda e, k=k: e.tensor_copy(
                            agupT[:, k * 16:(k + 1) * 16], psT5[:, 0:16]))
                    m_prev_gabf_rd = pe_s.n
                    r.wait("sy", ve_s, m_bc_prev)
                    r.wait("sy", pe_s, m_prev_gbbf_rd)
                    m_b = r.dma("sy", d_b5, gbst2.ap().rearrange("p (c r) -> p c r", r=16),
                                dr_gupB[e_i, :, :].rearrange("(c p) r -> p c r", p=128))
                    r.wait("ve", d_b5, m_b)
                    r.wait("ve", pe_s, m_prev_gbbf_rd)
                    m_bc = r.op("ve", ve_s, lambda e: e.tensor_copy(gbbf2[:, :], gbst2[:, :]))
                    m_bc_prev = m_bc
                    last_ev = 0
                    for k in range(16):
                        r.wait("pe", ve_s, max(m_bc, last_ev))
                        r.op("pe", pe_s, lambda e, k=k: e.matmul(
                            psT5[0:16, 0:128], gbbf2[:, k * 16:(k + 1) * 16],
                            idbf[:, :], start=True, stop=True))
                        m_t = pe_s.n
                        r.wait("ve", pe_s, m_t)
                        last_ev = r.op("ve", ve_s, lambda e, k=k: e.tensor_copy(
                            bgupT[0:16, k * 128:(k + 1) * 128], psT5[0:16, 0:128]))
                    m_prev_gbbf_rd = pe_s.n
                    r.wait("sy", d_sp, m_wsp)
                    last_wr = 0
                    last_w1_rd = 0
                    for t in range(4):
                        r.wait("sy", pe_s, last_w1_rd)
                        m_w1 = r.dma("sy", d_w1, w1r[:, :], dr_wr[e_i:e_i + 1, t * 512:(t + 1) * 512])
                        r.wait("pe", d_w1, m_w1)
                        r.wait("pe", ve_s, max(last_wr, last_ev))
                        r.op("pe", pe_s, lambda e: e.matmul(
                            psW5[:, 0:512], ones_row[:, :], w1r[0:1, 0:512],
                            start=True, stop=True))
                        m_t = pe_s.n
                        last_w1_rd = m_t
                        r.wait("ve", pe_s, m_t)
                        last_wr = r.op("ve", ve_s, lambda e, t=t: e.tensor_copy(
                            wrep[:, t * 512:(t + 1) * 512], psW5[:, 0:512]))
                    m_wrep = last_wr
                    last_terg = prev_t5_sc
                    for t in range(4):
                        r.wait("pe", sc_s, last_terg)
                        for k in range(16):
                            r.op("pe", pe_s, lambda e, k=k, t=t: e.matmul(
                                psT5[0:16, 0:512], agupT[:, k * 16:(k + 1) * 16],
                                xn[:, k * T + t * 512: k * T + (t + 1) * 512],
                                start=(k == 0), stop=(k == 15)))
                        m_tm = pe_s.n
                        r.wait("sc", pe_s, m_tm)
                        last_terg = r.op("sc", sc_s, lambda e, t=t: e.activation(
                            gter[:, t * 512:(t + 1) * 512], psT5[0:16, 0:512], AF.Copy, scale=SCALING))
                    prev_t5_sc = last_terg
                    m_gter = last_terg
                    for mp in range(8):
                        par = mp % 2
                        gpar = (e_i * 8 + mp) % 2
                        r.wait("sy", pe_s, free_gw[par])
                        r.wait("sy", ve_s, m_gw_conv)
                        r.dma("sy", d_w5[gpar],
                              gwst[:, 0:2048].rearrange("p (kc m) -> p kc m", m=128),
                              gw3[e_i, :, :, mp * 128:(mp + 1) * 128])
                        m_w2 = r.dma("sy", d_w5[gpar],
                                     gwst[:, 2048:4096].rearrange("p (kc m) -> p kc m", m=128),
                                     gw3[e_i, :, :, (mp + 8) * 128:(mp + 9) * 128])
                        r.wait("ve", d_w5[gpar], m_w2)
                        m_wc = r.op("ve", ve_s, lambda e, par=par: e.tensor_copy(
                            gwbf[:, par * 4096:(par + 1) * 4096], gwst[:, :]))
                        m_gw_conv = m_wc
                        for t in range(4):
                            tp = t % 2
                            r.wait("pe", ve_s, max(m_wc, m_wrep))
                            r.wait("pe", sc_s, max(m_gter, free_psg[tp]))
                            for k in range(16):
                                r.opn("pe", lambda e, par=par, k=k, t=t, tp=tp: e.matmul(
                                    psG[:, tp * 512:(tp + 1) * 512],
                                    gwbf[:, par * 4096 + k * 128: par * 4096 + (k + 1) * 128],
                                    xn[:, k * T + t * 512: k * T + (t + 1) * 512],
                                    start=(k == 0), stop=False))
                            r.op("pe", pe_s, lambda e, mp=mp, t=t, tp=tp: e.matmul(
                                psG[:, tp * 512:(tp + 1) * 512],
                                bgupT[0:16, mp * 128:(mp + 1) * 128],
                                gter[:, t * 512:(t + 1) * 512], start=False, stop=True))
                            m_gmm = pe_s.n
                            r.wait("pe", ve_s, free_psu[tp])
                            for k in range(16):
                                r.opn("pe", lambda e, par=par, k=k, t=t, tp=tp: e.matmul(
                                    psU5[:, tp * 512:(tp + 1) * 512],
                                    gwbf[:, par * 4096 + 2048 + k * 128: par * 4096 + 2048 + (k + 1) * 128],
                                    xn[:, k * T + t * 512: k * T + (t + 1) * 512],
                                    start=(k == 0), stop=False))
                            r.op("pe", pe_s, lambda e, mp=mp, t=t, tp=tp: e.matmul(
                                psU5[:, tp * 512:(tp + 1) * 512],
                                bgupT[0:16, (mp + 8) * 128:(mp + 9) * 128],
                                gter[:, t * 512:(t + 1) * 512], start=False, stop=True))
                            m_umm = pe_s.n
                            r.wait("sc", pe_s, m_gmm)
                            r.wait("sc", ve_s, free_sgx[tp])
                            m_sg = r.op("sc", sc_s, lambda e, tp=tp: e.activation(
                                sgx[:, tp * 512:(tp + 1) * 512], psG[:, tp * 512:(tp + 1) * 512], AF.Silu))
                            free_psg[tp] = m_sg
                            r.wait("ve", pe_s, m_umm)
                            r.wait("ve", ve_s, free_upx[tp])
                            m_ux = r.op("ve", ve_s, lambda e, t=t, tp=tp: e.tensor_tensor(
                                upx[:, tp * 512:(tp + 1) * 512], psU5[:, tp * 512:(tp + 1) * 512],
                                wrep[:, t * 512:(t + 1) * 512], OP.mult))
                            free_psu[tp] = m_ux
                            r.wait("ve", sc_s, m_sg)
                            r.wait("ve", d_h[tp], free_hst[tp])
                            m_h = r.op("ve", ve_s, lambda e, tp=tp: e.tensor_tensor(
                                hst[:, tp * 512:(tp + 1) * 512], sgx[:, tp * 512:(tp + 1) * 512],
                                upx[:, tp * 512:(tp + 1) * 512], OP.mult))
                            free_sgx[tp] = m_h
                            free_upx[tp] = m_h
                            r.wait("gp", ve_s, m_h)
                            m_hd = r.dma("gp", d_h[tp],
                                         dr_h[e_i, mp * 128:(mp + 1) * 128, t * 512:(t + 1) * 512],
                                         hst[:, tp * 512:(tp + 1) * 512])
                            free_hst[tp] = m_hd
                        free_gw[par] = pe_s.n
            m_gup_done = (d_h[0].n, d_h[1].n)
            if "h" in tap_aps:
                r.wait("gp", d_h[0], m_gup_done[0])
                r.wait("gp", d_h[1], m_gup_done[1])
                r.dma("gp", d_sp, tap_aps["h"], dr_h)

        if phases >= 6:
            phase_barrier()
        w45_es.close()
        xn_es.close()

        if phases >= 6:
            with ExitStack() as _ph:
                kbuf = _ph.enter_context(nc.sbuf_tensor("kbuf", [128, 2 * 1024], BF16))
                vbuf = _ph.enter_context(nc.sbuf_tensor("vbuf", [128, 2 * 1024], BF16))
                ptb = _ph.enter_context(nc.sbuf_tensor("ptb", [128, 4 * 1024], BF16))
                smask = _ph.enter_context(nc.sbuf_tensor("smask", [128, 2 * 512], F32))
                masks = _ph.enter_context(nc.sbuf_tensor("masks", [128, 4 * 512], F32))
                ssum = _ph.enter_context(nc.sbuf_tensor("ssum", [128, 2 * 2048], F32))
                ssumC = _ph.enter_context(nc.sbuf_tensor("ssumC", [128, 2 * 2048], F32))
                knrow = _ph.enter_context(nc.sbuf_tensor("knrow", [1, NHEAD], F32))
                kdup = _ph.enter_context(nc.sbuf_tensor("kdup", [1, 32], F32))
                crow = _ph.enter_context(nc.sbuf_tensor("crow", [1, 32], F32))
                csq = _ph.enter_context(nc.sbuf_tensor("csq", [1, 32], F32))
                rrow = _ph.enter_context(nc.sbuf_tensor("rrow", [1, 1024], F32))
                rrep = _ph.enter_context(nc.sbuf_tensor("rrep", [128, 1024], F32))
                psS0 = _ph.enter_context(nc.psum_tensor("psS0", [128, 1024], F32))
                psS1 = _ph.enter_context(nc.psum_tensor("psS1", [128, 1024], F32))
                psUa = _ph.enter_context(nc.psum_tensor("psUa", [128, 2048], F32))
                psS = [psS0, psS1]
                m_masks = r.dma("gp", d_in, masks[:, :], dr_mask)
                r.wait("sy", c_nn, 1)
                m_kn = r.dma("sy", d_kn6, knrow[:, :], dr_kno)
                kd3 = kdup.ap().rearrange("o (hh two) -> o hh two", two=2)
                kn3 = knrow.ap().rearrange("o (h u) -> o h u", u=1)
                r.wait("sc", d_kn6, m_kn)
                r.opn("sc", lambda e: e.activation(kd3[:, :, 0:1], kn3, AF.Copy))
                m_kd = r.op("sc", sc_s, lambda e: e.activation(kd3[:, :, 1:2], kn3, AF.Copy))
                r.wait("ve", sc_s, m_kd)
                r.wait("ve", ve_s, m_qkv_ve)
                m_cp = r.op("ve", ve_s, lambda e: e.tensor_tensor(csq[:, :], qn2[:, :], kdup[:, :], OP.mult))
                r.wait("sc", ve_s, m_cp)
                m_cr = r.op("sc", sc_s, lambda e: e.activation(crow[:, :], csq[:, :], AF.Sqrt))
                r.wait("ve", sc_s, m_cr)
                m_ng = r.op("ve", ve_s, lambda e: e.tensor_scalar(csq[:, :], crow[:, :], -1.0, None, OP.mult))
                r.wait("pe", ve_s, m_ng)
                r.wait("pe", pe_s, m_qkv_pe)
                r.op("pe", pe_s, lambda e: e.matmul(psS0[:, 0:32], ones_row[:, :], csq[0:1, :],
                                               start=True, stop=True))
                m_nr = pe_s.n
                r.wait("ve", pe_s, m_nr)
                m_negc = r.op("ve", ve_s, lambda e: e.tensor_copy(negc_rep[:, :], psS0[:, 0:32]))
                if "negc" in tap_aps:
                    r.wait("gp", ve_s, m_negc)
                    r.dma("gp", d_sp, tap_aps["negc"], negc_rep[:, :])
                r.wait("pe", ve_s, m_negc)
                r.wait("sy", c_k, 1)
                r.wait("sy", c_v, 1)
                r.wait("ve", d_in, m_masks)
                r.wait("sc", ve_s, m_negc)

                gv3 = dr_gv.rearrange("(s p) d -> p s d", p=128)

                def emit_attn_variant(rv, cl):
                    units = [(0, cl), (1, 7 - cl)]
                    for jj in range(cl):
                        units.append((0, jj))
                    for jj in range(7 - cl):
                        units.append((1, jj))
                    kv_pe_done = {}
                    g_seq = 0
                    for h_i in range(NHEAD):
                        rv.wait("pe", ve_s, ve_s.n)
                        for j4 in range(4):
                            rv.op("pe", pe_s, lambda e, j4=j4: e.matmul(
                                psUa[:, j4 * 512:(j4 + 1) * 512], zero_bf[:, 0:128], zero_bf[:, :],
                                start=True, stop=False))
                        pe_s.n
                        m_ssz = rv.op("ve", ve_s, lambda e: e.memset(ssum[:, 0:2048], 0.0))
                        rv.wait("gp", pe_s, pe_s.n)
                        rv.op("gp", gp_s, lambda e: e.memset(ssumC[:, 0:2048], 0.0))
                        m_exp_prev = [0, 0]
                        m_psS_ve = [0, 0]
                        m_pt_pe = [0, 0, 0, 0]
                        m_pt_ve = [0, 0, 0, 0]
                        m_pt_gp = [0, 0, 0, 0]
                        m_smask_rd = [0, 0]
                        cnt = {("A", 0): 0, ("A", 1): 0, ("C", 0): 0, ("C", 1): 0}
                        pending = []

                        def emit_av(rv=rv):
                            g, sss, p4, segs, m_exp, up, qc = pending.pop(0)
                            if segs[0] != "c" or segs[1] != "c":
                                rv.wait("pe", sc_s, m_exp if m_exp else sc_s.n)
                            for qh in range(2):
                                if segs[qh] == "c":
                                    continue
                                rv.op("pe", pe_s, lambda e, sss=sss, qh=qh, p4=p4, up=up, qc=qc: e.matmul(
                                    psUa[:, qc * 1024 + qh * 512: qc * 1024 + (qh + 1) * 512],
                                    vbuf[:, up * 1024 + sss * 128: up * 1024 + (sss + 1) * 128],
                                    ptb[:, p4 * 1024 + qh * 512: p4 * 1024 + (qh + 1) * 512],
                                    start=False, stop=False))
                            m_pt_pe[p4] = pe_s.n
                            if sss == 7:
                                kv_pe_done[g] = pe_s.n
                        for u in range(9):
                            qc, jch = units[u]
                            up = g_seq % 2
                            rk = jch if jch < 4 else 7 - jch
                            lci = 0 if jch < 4 else 1
                            rv.wait("sy", pe_s, kv_pe_done.get(g_seq - 2, 0))
                            rv.dma("sy", d_kv[up],
                                   kbuf[:, up * 1024:(up + 1) * 1024],
                                   dr_gk[rk * H + h_i * 128: rk * H + (h_i + 1) * 128,
                                         lci * 1024:(lci + 1) * 1024])
                            vs = (rk * T + lci * 1024) // 128
                            m_kvd = rv.dma("sy", d_kv[up],
                                           vbuf[:, up * 1024:(up + 1) * 1024].rearrange("p (s d) -> p s d", d=128),
                                           gv3[:, vs:vs + 8, h_i * 128:(h_i + 1) * 128])
                            nbias = negc_rep[:, h_i * 2 + qc: h_i * 2 + qc + 1]
                            diag = u < 2
                            for sss in range(8):
                                p2 = sss % 2
                                p4 = sss % 4
                                segs = [_diag_status(sss, qh) for qh in range(2)] if diag else ["f", "f"]
                                rv.wait("pe", d_kv[up], m_kvd)
                                rv.wait("pe", sc_s, m_exp_prev[p2])
                                rv.wait("pe", ve_s, m_psS_ve[p2])
                                for qh in range(2):
                                    if segs[qh] == "c":
                                        continue
                                    rv.op("pe", pe_s, lambda e, sss=sss, qh=qh, up=up, p2=p2, h_i=h_i, qc=qc: e.matmul(
                                        psS[p2][:, qh * 512:(qh + 1) * 512],
                                        kbuf[:, up * 1024 + sss * 128: up * 1024 + (sss + 1) * 128],
                                        QO[:, h_i * T + qc * 1024 + qh * 512: h_i * T + qc * 1024 + (qh + 1) * 512],
                                        start=True, stop=True))
                                m_smm = pe_s.n
                                if pending:
                                    emit_av()
                                m_msk = 0
                                if diag:
                                    for qh in range(2):
                                        if segs[qh] != "p":
                                            continue
                                        mi = (sss * 128 - qh * 512) // 128
                                        rv.wait("ve", pe_s, m_smm)
                                        rv.wait("ve", sc_s, m_smask_rd[qh])
                                        m_msk = rv.op("ve", ve_s, lambda e, p2=p2, qh=qh, mi=mi: e.tensor_tensor(
                                            smask[:, qh * 512:(qh + 1) * 512],
                                            psS[p2][:, qh * 512:(qh + 1) * 512],
                                            masks[:, mi * 512:(mi + 1) * 512], OP.add))
                                        m_psS_ve[p2] = m_msk
                                    for qh in range(2):
                                        if segs[qh] == "c":
                                            rv.wait("ve", pe_s, m_pt_pe[p4])
                                            if m_pt_gp[p4]:
                                                rv.wait("ve", gp_s, m_pt_gp[p4])
                                            m_msk = rv.op("ve", ve_s, lambda e, p4=p4, qh=qh: e.memset(
                                                ptb[:, p4 * 1024 + qh * 512: p4 * 1024 + (qh + 1) * 512], 0.0))
                                rv.wait("sc", pe_s, max(m_smm, m_pt_pe[p4]))
                                rv.wait("sc", ve_s, max(m_msk, m_pt_ve[p4]))
                                if m_pt_gp[p4]:
                                    rv.wait("sc", gp_s, m_pt_gp[p4])
                                if not diag:
                                    m_exp = rv.op("sc", sc_s, lambda e, p2=p2, p4=p4, nbias=nbias: e.activation(
                                        ptb[:, p4 * 1024:(p4 + 1) * 1024], psS[p2][:, :],
                                        AF.Exp, bias=nbias))
                                    m_exp_prev[p2] = m_exp
                                else:
                                    m_exp = 0
                                    for qh in range(2):
                                        if segs[qh] == "c":
                                            continue
                                        if segs[qh] == "p":
                                            m_exp = rv.op("sc", sc_s, lambda e, p4=p4, qh=qh, nbias=nbias: e.activation(
                                                ptb[:, p4 * 1024 + qh * 512: p4 * 1024 + (qh + 1) * 512],
                                                smask[:, qh * 512:(qh + 1) * 512], AF.Exp, bias=nbias))
                                            m_smask_rd[qh] = m_exp
                                        else:
                                            m_exp = rv.op("sc", sc_s, lambda e, p2=p2, p4=p4, qh=qh, nbias=nbias: e.activation(
                                                ptb[:, p4 * 1024 + qh * 512: p4 * 1024 + (qh + 1) * 512],
                                                psS[p2][:, qh * 512:(qh + 1) * 512], AF.Exp, bias=nbias))
                                            m_exp_prev[p2] = m_exp
                                ch = "A" if sss in (0, 2, 4, 6, 7) else "C"
                                aeng = "ve" if ch == "A" else "gp"
                                asem = ve_s if ch == "A" else gp_s
                                abuf = ssum if ch == "A" else ssumC
                                c0 = cnt[(ch, qc)]
                                spp, dpp = c0 % 2, (c0 + 1) % 2
                                cnt[(ch, qc)] = c0 + 1
                                rv.wait(aeng, sc_s, m_exp if m_exp else sc_s.n)
                                if ch == "C" and diag and m_msk:
                                    rv.wait("gp", ve_s, m_msk)
                                if u == 0 and sss == 0:
                                    rv.wait("ve", ve_s, m_ssz)
                                rv.opn(aeng, lambda e, spp=spp, dpp=dpp, p4=p4, qc=qc, abuf=abuf: e.tensor_tensor(
                                    abuf[:, dpp * 2048 + qc * 1024: dpp * 2048 + (qc + 1) * 1024],
                                    abuf[:, spp * 2048 + qc * 1024: spp * 2048 + (qc + 1) * 1024],
                                    ptb[:, p4 * 1024:(p4 + 1) * 1024], OP.add))
                                m_ssa = rv.op(aeng, asem, lambda e: e.nop())
                                if ch == "A":
                                    m_pt_ve[p4] = m_ssa
                                else:
                                    m_pt_gp[p4] = m_ssa
                                pending.append((g_seq, sss, p4, segs, m_exp, up, qc))
                            g_seq += 1
                        while pending:
                            emit_av()
                        for j4 in range(4):
                            if j4 == 3:
                                rv.op("pe", pe_s, lambda e, j4=j4: e.matmul(
                                    psUa[:, j4 * 512:(j4 + 1) * 512], zero_bf[:, 0:128], zero_bf[:, :],
                                    start=False, stop=True))
                            else:
                                rv.opn("pe", lambda e, j4=j4: e.matmul(
                                    psUa[:, j4 * 512:(j4 + 1) * 512], zero_bf[:, 0:128], zero_bf[:, :],
                                    start=False, stop=True))
                        m_last_exp = sc_s.n
                        m_last_ssa = ve_s.n
                        m_last_av = pe_s.n
                        m_last_gadd = gp_s.n
                        for qc in range(2):
                            ppf = qc
                            pA = (cnt[("A", qc)] % 2) * 2048
                            pC = (cnt[("C", qc)] % 2) * 2048
                            rv.wait("pe", sc_s, m_last_exp)
                            rv.wait("pe", ve_s, max(m_last_ssa, m_psS_ve[ppf]))
                            rv.wait("pe", gp_s, m_last_gadd)
                            for qh in range(2):
                                rv.opn("pe", lambda e, qc=qc, qh=qh, ppf=ppf, pA=pA: e.matmul(
                                    psS[ppf][0:1, qh * 512:(qh + 1) * 512], ones_col[:, :],
                                    ssum[:, pA + qc * 1024 + qh * 512: pA + qc * 1024 + (qh + 1) * 512],
                                    start=True, stop=False))
                                rv.op("pe", pe_s, lambda e, qc=qc, qh=qh, ppf=ppf, pC=pC: e.matmul(
                                    psS[ppf][0:1, qh * 512:(qh + 1) * 512], ones_col[:, :],
                                    ssumC[:, pC + qc * 1024 + qh * 512: pC + qc * 1024 + (qh + 1) * 512],
                                    start=False, stop=True))
                            m_sr = pe_s.n
                            rv.wait("ve", pe_s, m_sr)
                            m_rr = rv.op("ve", ve_s, lambda e, ppf=ppf: e.reciprocal(
                                rrow[0:1, :], psS[ppf][0:1, :]))
                            rv.wait("pe", ve_s, m_rr)
                            for qh in range(2):
                                rv.op("pe", pe_s, lambda e, qh=qh, ppf=ppf: e.matmul(
                                    psS[ppf][:, qh * 512:(qh + 1) * 512], ones_row[:, :],
                                    rrow[0:1, qh * 512:(qh + 1) * 512], start=True, stop=True))
                            m_rp = pe_s.n
                            rv.wait("ve", pe_s, m_rp)
                            m_rrep = rv.op("ve", ve_s, lambda e, ppf=ppf: e.tensor_copy(
                                rrep[:, :], psS[ppf][:, :]))
                            rv.wait("ve", pe_s, m_last_av)
                            rv.op("ve", ve_s, lambda e, h_i=h_i, qc=qc: e.tensor_tensor(
                                QO[:, h_i * T + qc * 1024: h_i * T + (qc + 1) * 1024],
                                psUa[:, qc * 1024:(qc + 1) * 1024], rrep[:, :], OP.mult))

                all_sems = [d_in, d_kv[0], d_kv[1], d_sp, c_k, c_v, c_nn, pe_s, ve_s, sc_s, gp_s]
                snap0 = {id(sm): sm.n for sm in all_sems}
                variants = []
                final0 = None
                for v in range(4):
                    for sm in all_sems:
                        sm.n = snap0[id(sm)]
                    rvv = Rec()
                    emit_attn_variant(rvv, v)
                    fin = {id(sm): sm.n for sm in all_sems}
                    if final0 is None:
                        final0 = fin
                    else:
                        assert fin == final0, "attention variants diverge in sem counts"
                    variants.append(rvv)
                for sm in all_sems:
                    sm.n = final0[id(sm)]
                eng_regs = {"sy": rg_sy[0], "pe": rg_pe, "ve": rg_ve, "sc": rg_sc, "gp": rg_gp}
                for eng in ("sy", "pe", "ve", "sc", "gp"):
                    bodies = [variants[v].ops[eng] for v in range(4)]
                    r.wait(eng, d_in, m_consts)
                    def sw(e, reg=eng_regs[eng], bodies=bodies):
                        e.reg_load(reg, cfgs[0:1, 40:41])
                        idx = e.snap(reg, min_val=0, max_val=3)
                        for jv in nc.Switch(engines=[e], index=[idx], n=4):
                            for fn in bodies[jv]:
                                fn(e)
                    r.raw(eng, sw)
            m_attn_ve = ve_s.n
            if "qo" in tap_aps:
                r.wait("gp", ve_s, m_attn_ve)
                r.dma("gp", d_sp, tap_aps["qo"], QO[:, :])
        if "qo" in tap_aps and phases < 6:
            r.wait("gp", sc_s, sc_s.n)
            r.dma("gp", d_sp, tap_aps["qo"], QO[:, :])
        if "gk" in tap_aps and phases >= 3:
            r.wait("gp", c_k, 1)
            r.dma("gp", d_sp, tap_aps["gk"], dr_gk)
        if "gv" in tap_aps and phases >= 3:
            r.wait("gp", c_v, 1)
            r.dma("gp", d_sp, tap_aps["gv"], dr_gv)

        if phases >= 7:
            phase_barrier()

        if phases >= 7:
            with ExitStack() as _ph:
                dATst = _ph.enter_context(nc.sbuf_tensor("dATst", [16, II], F32))
                ter2P = [_ph.enter_context(nc.sbuf_tensor(f"ter2_{j}", [128, T], BF16)) for j in range(3)]
                dBTP = [_ph.enter_context(nc.sbuf_tensor(f"dBT_{j}", [128, H], BF16)) for j in range(3)]
                dATbf = _ph.enter_context(nc.sbuf_tensor("dATbf", [16, II], BF16))
                dAT = _ph.enter_context(nc.sbuf_tensor("dAT", [128, 128], BF16))
                dBst = _ph.enter_context(nc.sbuf_tensor("dBst", [128, 256], F32))
                dBbf = _ph.enter_context(nc.sbuf_tensor("dBbf", [128, 256], BF16))
                wost = _ph.enter_context(nc.sbuf_tensor("wost", [128, 512], F32))
                wobf = _ph.enter_context(nc.sbuf_tensor("wobf", [128, 2 * 16 * 512], BF16))
                dnst = _ph.enter_context(nc.sbuf_tensor("dnst", [128, 4096], F32))
                dnbf = _ph.enter_context(nc.sbuf_tensor("dnbf", [128, 2 * 4096], BF16))
                hbb = _ph.enter_context(nc.sbuf_tensor("hbb", [128, 2 * 8192], BF16))
                xres = _ph.enter_context(nc.sbuf_tensor("xres", [128, 2 * 512], F32))
                osb = _ph.enter_context(nc.sbuf_tensor("osb", [128, 2 * 512], F32))
                ps7_es = ExitStack()
                psT7 = ps7_es.enter_context(nc.psum_tensor("psT7", [128, 512], F32))
                m_dbt = 0
                m_bc7 = 0
                m_dbbf_rd = 0
                for e_i in range(E):
                    r.wait("sy", ve_s, m_bc7)
                    r.wait("sy", pe_s, m_dbbf_rd)
                    m_b = r.dma("sy", d_b7, dBst.ap().rearrange("p (c r) -> p c r", r=16),
                                dr_downB[e_i, :, :].rearrange("(c p) r -> p c r", p=128))
                    r.wait("ve", d_b7, m_b)
                    r.wait("ve", pe_s, m_dbbf_rd)
                    m_bc = r.op("ve", ve_s, lambda e: e.tensor_copy(dBbf[:, :], dBst[:, :]))
                    m_bc7 = m_bc
                    last_ev = 0
                    for k in range(16):
                        r.wait("pe", ve_s, max(m_bc, last_ev))
                        r.op("pe", pe_s, lambda e, k=k: e.matmul(
                            psT7[0:16, 0:128], dBbf[:, k * 16:(k + 1) * 16],
                            idbf[:, :], start=True, stop=True))
                        m_t = pe_s.n
                        r.wait("ve", pe_s, m_t)
                        last_ev = r.op("ve", ve_s, lambda e, e_i=e_i, k=k: e.tensor_copy(
                            dBTP[e_i // 3][(e_i % 3) * 32:(e_i % 3) * 32 + 16, k * 128:(k + 1) * 128],
                            psT7[0:16, 0:128]))
                    m_dbt = last_ev
                    m_dbbf_rd = pe_s.n
                free_hb = [0, 0]
                prev_t7_sc = 0
                m_ac7 = 0
                m_datbf_rd = 0
                for e_i in range(E):
                    r.wait("sy", ve_s, m_ac7)
                    r.wait("sy", pe_s, m_datbf_rd)
                    m_a = r.dma("sy", d_a7, dATst[:, :], dr_downA[e_i, :, :])
                    r.wait("ve", d_a7, m_a)
                    r.wait("ve", pe_s, m_datbf_rd)
                    m_ac = r.op("ve", ve_s, lambda e: e.tensor_copy(dATbf[:, :], dATst[:, :]))
                    m_ac7 = m_ac
                    last_ev = 0
                    for k in range(8):
                        r.wait("pe", ve_s, max(m_ac, last_ev, m_dbt))
                        r.wait("pe", sc_s, prev_t7_sc)
                        r.op("pe", pe_s, lambda e, k=k: e.matmul(
                            psT7[:, 0:16], dATbf[0:16, k * 128:(k + 1) * 128],
                            idbf[0:16, 0:16], start=True, stop=True))
                        m_t = pe_s.n
                        r.wait("ve", pe_s, m_t)
                        last_ev = r.op("ve", ve_s, lambda e, k=k: e.tensor_copy(
                            dAT[:, k * 16:(k + 1) * 16], psT7[:, 0:16]))
                    m_dat = last_ev
                    m_datbf_rd = pe_s.n
                    for t in range(4):
                        hp = t % 2
                        r.wait("sy", pe_s, free_hb[hp])
                        r.wait("sy", d_h[0], m_gup_done[0])
                        r.wait("sy", d_h[1], m_gup_done[1])
                        m_hb = r.dma("sy", d_hbs[hp],
                                     hbb[:, hp * 8192: hp * 8192 + 4096].rearrange("p (ic t) -> p ic t", t=512),
                                     dr_h[e_i, :, t * 512:(t + 1) * 512].rearrange("(ic p) t -> p ic t", p=128))
                        r.wait("pe", d_hbs[hp], m_hb)
                        r.wait("pe", ve_s, m_dat)
                        r.wait("pe", sc_s, prev_t7_sc)
                        for k in range(8):
                            r.op("pe", pe_s, lambda e, k=k, hp=hp: e.matmul(
                                psT7[0:16, 0:512], dAT[:, k * 16:(k + 1) * 16],
                                hbb[:, hp * 8192 + k * 512: hp * 8192 + (k + 1) * 512],
                                start=(k == 0), stop=(k == 7)))
                        m_tm = pe_s.n
                        free_hb[hp] = m_tm
                        r.wait("sc", pe_s, m_tm)
                        prev_t7_sc = r.op("sc", sc_s, lambda e, e_i=e_i, t=t: e.activation(
                            ter2P[e_i // 3][(e_i % 3) * 32:(e_i % 3) * 32 + 16, t * 512:(t + 1) * 512],
                            psT7[0:16, 0:512], AF.Copy, scale=SCALING))
                m_ter2_sc = prev_t7_sc
                ps7_es.close()
                psF = _ph.enter_context(nc.psum_tensor("psF", [128, 8 * 512], F32))
                free_hbb_init = pe_s.n
                wo3 = dr_wout.rearrange("(kc p) f -> p kc f", p=128)
                dn3 = dr_down.rearrange("e (ic p) f -> e p ic f", p=128)
                free_bank = [0] * 8
                free_wo = 0
                free_dn = [0, 0]
                free_hbb = [free_hbb_init, free_hbb_init]
                free_xres = [0, 0]
                free_osb = [0, 0]
                free_wo_par = [0, 0]
                m_dnc_last = 0
                m_woc = 0
                for f in range(4):
                    fpar = f % 2
                    r.wait("sy", pe_s, free_wo_par[fpar])
                    for k in range(16):
                        wp = k % 2
                        r.wait("sy", ve_s, m_woc)
                        m_wd = r.dma("sy", d_wo[wp], wost[:, :],
                                     wo3[:, k, f * 512:(f + 1) * 512])
                        r.wait("ve", d_wo[wp], m_wd)
                        m_woc = r.op("ve", ve_s, lambda e, k=k, fpar=fpar: e.tensor_copy(
                            wobf[:, fpar * 8192 + k * 512: fpar * 8192 + (k + 1) * 512],
                            wost[:, :]))
                    m_wo_ready = m_woc
                    for th in range(2):
                        for e_i in range(E):
                            ep = e_i % 2
                            r.wait("sy", pe_s, free_dn[ep])
                            r.wait("sy", ve_s, m_dnc_last)
                            m_dd = r.dma("sy", d_dn[ep],
                                         dnst.ap().rearrange("p (k c) -> p k c", c=512),
                                         dn3[e_i, :, :, f * 512:(f + 1) * 512])
                            r.wait("ve", d_dn[ep], m_dd)
                            m_dnc = r.op("ve", ve_s, lambda e, ep=ep: e.tensor_copy(
                                dnbf[:, ep * 4096:(ep + 1) * 4096], dnst[:, :]))
                            m_dnc_last = m_dnc
                            r.wait("gp", pe_s, free_hbb[ep])
                            r.wait("gp", d_h[0], m_gup_done[0])
                            r.wait("gp", d_h[1], m_gup_done[1])
                            m_hh = r.dma("gp", d_hb[ep],
                                         hbb[:, ep * 8192:(ep + 1) * 8192].rearrange("p (ic t) -> p ic t", t=1024),
                                         dr_h[e_i, :, th * 1024:(th + 1) * 1024].rearrange("(ic p) t -> p ic t", p=128))
                            for t in range(8):
                                tt = th * 8 + t
                                if e_i == 0:
                                    r.wait("pe", ve_s, max(m_wo_ready, free_bank[t], m_attn_ve))
                                    r.wait("pe", sc_s, m_ter2_sc)
                                    for k in range(16):
                                        r.opn("pe", lambda e, k=k, tt=tt, t=t, fpar=fpar: e.matmul(
                                            psF[:, t * 512:(t + 1) * 512],
                                            QO[:, k * T + tt * 128: k * T + (tt + 1) * 128],
                                            wobf[:, fpar * 8192 + k * 512: fpar * 8192 + (k + 1) * 512],
                                            start=(k == 0), stop=False))
                                r.wait("pe", ve_s, m_dnc)
                                r.wait("pe", d_hb[ep], m_hh)
                                r.wait("pe", sc_s, m_ter2_sc)
                                for k in range(8):
                                    r.opn("pe", lambda e, k=k, t=t, ep=ep: e.matmul(
                                        psF[:, t * 512:(t + 1) * 512],
                                        hbb[:, ep * 8192 + k * 1024 + t * 128: ep * 8192 + k * 1024 + (t + 1) * 128],
                                        dnbf[:, ep * 4096 + k * 512: ep * 4096 + (k + 1) * 512],
                                        start=False, stop=False))
                                r.op("pe", pe_s, lambda e, e_i=e_i, tt=tt, t=t, f=f: e.matmul(
                                    psF[:, t * 512:(t + 1) * 512],
                                    ter2P[e_i // 3][(e_i % 3) * 32:(e_i % 3) * 32 + 16, tt * 128:(tt + 1) * 128],
                                    dBTP[e_i // 3][(e_i % 3) * 32:(e_i % 3) * 32 + 16, f * 512:(f + 1) * 512],
                                    start=False, stop=(e_i == E - 1)))
                                m_pk = pe_s.n
                                if e_i == E - 1:
                                    xp = t % 2
                                    r.wait("gp", ve_s, free_xres[xp])
                                    m_xr = r.dma("gp", d_xr[xp], xres[:, xp * 512:(xp + 1) * 512],
                                                 dr_x[tt * 128:(tt + 1) * 128, f * 512:(f + 1) * 512])
                                    r.wait("ve", pe_s, m_pk)
                                    r.wait("ve", d_xr[xp], m_xr)
                                    r.wait("ve", d_ot[xp], free_osb[xp])
                                    m_ov = r.op("ve", ve_s, lambda e, t=t, xp=xp: e.tensor_tensor(
                                        osb[:, xp * 512:(xp + 1) * 512], psF[:, t * 512:(t + 1) * 512],
                                        xres[:, xp * 512:(xp + 1) * 512], OP.add))
                                    free_bank[t] = m_ov
                                    free_xres[xp] = m_ov
                                    r.wait("gp", ve_s, m_ov)
                                    m_os = r.dma("gp", d_ot[xp],
                                                 dr_out[tt * 128:(tt + 1) * 128, f * 512:(f + 1) * 512],
                                                 osb[:, xp * 512:(xp + 1) * 512])
                                    free_osb[xp] = m_os
                            free_dn[ep] = pe_s.n
                            free_hbb[ep] = pe_s.n
                        free_wo_par[fpar] = pe_s.n
            r.wait("gp", d_ot[0], d_ot[0].n)
            r.wait("gp", d_ot[1], d_ot[1].n)
        else:
            r.wait("gp", d_sp, d_sp.n)

        r.emit(block)
    return nc



def _host_consts(core):
    cl = core % 4
    idbf = np.eye(128, dtype=ml_dtypes.bfloat16)
    idf = np.eye(128, dtype=np.float32)
    mask = np.zeros((128, 4 * 512), np.float32)
    for mi in range(4):
        o = mi * 128
        p = np.arange(128)[:, None]
        q = np.arange(512)[None, :]
        mask[:, mi * 512:(mi + 1) * 512] = np.where(q >= p + o, 0.0, NEG)
    cfg = np.zeros((1, 64), np.int32)
    chA, chB = cl, 7 - cl
    units = [(0, chA), (1, chB)]
    for j in range(chA):
        units.append((0, j))
    for j in range(chB):
        units.append((1, j))
    assert len(units) == 9
    for u, (qc, j) in enumerate(units):
        rk = j if j < 4 else 7 - j
        lci = 0 if j < 4 else 1
        cfg[0, u * 4 + 0] = qc
        cfg[0, u * 4 + 1] = j
    cfg[0, 40] = cl
    return {"c_idbf": idbf, "c_idf": idf, "c_mask": mask, "c_cfg": cfg}


def _shard_x(x, core):
    cl = core % 4
    b = core // 4
    chA, chB = cl, 7 - cl
    return np.ascontiguousarray(np.concatenate(
        [x[b, chA * 1024:(chA + 1) * 1024], x[b, chB * 1024:(chB + 1) * 1024]], axis=0))


def kernel(L0_attn_x, L0_attn_w_qkv, L0_attn_w_out, L0_ffn_gate_weight,
           L0_ffn_gate_lora_A, L0_ffn_gate_lora_B, L0_ffn_gate_up_proj,
           L0_ffn_down_proj, L0_ffn_gup_lora_A, L0_ffn_gup_lora_B,
           L0_ffn_down_lora_A, L0_ffn_down_lora_B):
    x = np.asarray(L0_attn_x, np.float32)
    weights = {
        "wqkv": np.ascontiguousarray(np.asarray(L0_attn_w_qkv, np.float32)),
        "wout": np.ascontiguousarray(np.asarray(L0_attn_w_out, np.float32)),
        "gatew": np.ascontiguousarray(np.asarray(L0_ffn_gate_weight, np.float32)),
        "gateA": np.ascontiguousarray(np.asarray(L0_ffn_gate_lora_A, np.float32)),
        "gateB": np.ascontiguousarray(np.asarray(L0_ffn_gate_lora_B, np.float32)),
        "gup": np.ascontiguousarray(np.asarray(L0_ffn_gate_up_proj, np.float32)),
        "down": np.ascontiguousarray(np.asarray(L0_ffn_down_proj, np.float32)),
        "gupA": np.ascontiguousarray(np.asarray(L0_ffn_gup_lora_A, np.float32)),
        "gupB": np.ascontiguousarray(np.asarray(L0_ffn_gup_lora_B, np.float32)),
        "downA": np.ascontiguousarray(np.asarray(L0_ffn_down_lora_A, np.float32)),
        "downB": np.ascontiguousarray(np.asarray(L0_ffn_down_lora_B, np.float32)),
    }
    nc = build(taps=TAPS, phases=PHASES)
    in_maps = []
    for c in range(8):
        m = dict(weights)
        m["x"] = _shard_x(x, c)
        m.update(_host_consts(c))
        in_maps.append(m)
    res = run_bass_kernel_spmd(nc, in_maps, core_ids=list(range(8)))
    out = np.zeros((2, 8192, H), np.float32)
    for c in range(8):
        cl, b = c % 4, c // 4
        chA, chB = cl, 7 - cl
        o = res.results[c]["out"]
        out[b, chA * 1024:(chA + 1) * 1024] = o[0:1024]
        out[b, chB * 1024:(chB + 1) * 1024] = o[1024:2048]
    return out
